# version 10 (final)
# baseline (speedup 1.0000x reference)
import contextlib
import numpy as np

import concourse.bass as bass
import concourse.mybir as mybir
from concourse.bass_utils import run_bass_kernel_spmd

F32 = mybir.dt.float32
BF16 = mybir.dt.bfloat16
U8 = mybir.dt.uint8
AF = mybir.ActivationFunctionType
ALU = mybir.AluOpType
AX = mybir.AxisListType

NCORES = 8
T = 8192
D = 1024
MT = 1024
TS = 512
C0 = 128
C1 = 64
EPS = 1e-5

QUEUES = ("tensor", "scalar", "vector", "gpsimd", "sync")
STREAMS = {
    "pe": ("tensor", 1),
    "act": ("scalar", 1),
    "dve": ("vector", 1),
    "pool": ("gpsimd", 1),
    "gdma": ("gpsimd", 16),
    "sdma": ("sync", 16),
}
DMA_STREAMS = ("gdma", "sdma")


class Buf:
    __slots__ = ("name", "writers", "readers", "hz")

    def __init__(self, name):
        self.name = name
        self.writers = {}
        self.readers = {}
        self.hz = None


class MK:
    def __init__(self, nc, sems):
        self.nc = nc
        self.sems = sems
        self.rec = {q: [] for q in QUEUES}
        self.cnt = {s: 0 for s in STREAMS}
        self.waited = {s: {t: 0 for t in STREAMS} for s in STREAMS}
        self.nops = 0

    SER_DMA = False

    def op(self, stream, build, reads=(), writes=(), serialize=False,
           hazard=False):
        q, inc = STREAMS[stream]
        if self.SER_DMA and stream in DMA_STREAMS:
            serialize = True
        deps = {}
        for b in reads:
            if b.hz and b.hz[0] == stream and b.hz[1] > self.waited[stream][stream]:
                serialize = True
        if serialize and self.cnt[stream] > self.waited[stream][stream]:
            c0 = self.cnt[stream]
            sem0 = self.sems[stream]
            self.rec[q].append(lambda e, sem=sem0, c=c0: e.wait_ge(sem, c))
            self.waited[stream][stream] = c0
        for b in reads:
            for s, c in b.writers.items():
                deps[s] = max(deps.get(s, 0), c)
        for b in writes:
            for s, c in b.writers.items():
                deps[s] = max(deps.get(s, 0), c)
            for s, c in b.readers.items():
                deps[s] = max(deps.get(s, 0), c)
        for s in sorted(deps):
            c = deps[s]
            if s == stream and s not in DMA_STREAMS:
                continue
            if self.waited[stream][s] >= c:
                continue
            sem = self.sems[s]
            self.rec[q].append(lambda e, sem=sem, c=c: e.wait_ge(sem, c))
            self.waited[stream][s] = c
        self.cnt[stream] += inc
        cval = self.cnt[stream]
        sem = self.sems[stream]

        def doit(e, build=build, sem=sem, inc=inc):
            build(e).then_inc(sem, inc)

        self.rec[q].append(doit)
        self.nops += 1
        for b in reads:
            b.readers[stream] = max(b.readers.get(stream, 0), cval)
        for b in writes:
            b.writers[stream] = max(b.writers.get(stream, 0), cval)
            if hazard:
                b.hz = (stream, cval)

    def barrier(self):
        snap = dict(self.cnt)
        for q in QUEUES:
            q_streams = [s for s in STREAMS if STREAMS[s][0] == q]
            for s2, c in snap.items():
                if c == 0:
                    continue
                if all(self.waited[qs][s2] >= c for qs in q_streams):
                    continue
                sem = self.sems[s2]
                self.rec[q].append(lambda e, sem=sem, c=c: e.wait_ge(sem, c))
                for qs in q_streams:
                    self.waited[qs][s2] = max(self.waited[qs][s2], c)

    def emit(self, block):
        for q in QUEUES:
            closures = self.rec[q]
            if not closures:
                continue

            def body(eng, closures=closures):
                for f in closures:
                    f(eng)

            getattr(block, q)(body)


class SbufMap:
    def __init__(self, nc, base=16384, limit=228 * 1024):
        self.nc = nc
        self.cur = base
        self.limit = limit

    def alloc(self, name, shape, dtype, align=64):
        nbytes = int(np.prod(shape[1:])) * np.dtype(mybir.dt.np(dtype)).itemsize
        self.cur = (self.cur + align - 1) // align * align
        t = self.nc.alloc_sbuf_tensor_at(name, list(shape), dtype, offset=self.cur)
        self.cur += nbytes
        assert self.cur <= self.limit, f"SBUF overflow at {name}: {self.cur}"
        return t.ap()


class Arena:
    def __init__(self, nc, base, size):
        self.nc = nc
        self.base = base
        self.size = size
        self.cur = 0
        self.cache = {}

    def alloc(self, name, shape, dtype, align=64):
        if name in self.cache:
            return self.cache[name]
        nbytes = int(np.prod(shape[1:])) * np.dtype(mybir.dt.np(dtype)).itemsize
        self.cur = (self.cur + align - 1) // align * align
        assert self.cur + nbytes <= self.size, \
            f"arena overflow at {name}: {self.cur + nbytes}"
        t = self.nc.alloc_sbuf_tensor_at(name, list(shape), dtype,
                                         offset=self.base + self.cur)
        entry = (t.ap(), Buf(name))
        self.cache[name] = entry
        self.cur += nbytes
        return entry

    def reset(self):
        self.cur = 0


PHASES_ALL = ("ffn0a", "attn0", "ffn0b", "ffn1a", "attn1", "ffn1b")

WEIGHT_NAMES = [
    ("L0_attn_Wq", [D, 512], F32), ("L0_attn_Wk", [D, 512], F32),
    ("L0_attn_Wv", [D, D], F32), ("L0_attn_Wg", [D, D], F32),
    ("L0_attn_Wgk1", [D, 16], F32), ("L0_attn_Wgk2", [16, 512], F32),
    ("L0_attn_bgk2", [1, 512], F32), ("L0_attn_norm_w", [1, 128], F32),
    ("L0_attn_Wo", [D, D], F32),
    ("L0_ffn_W1", [4096, D], F32), ("L0_ffn_b1", [1, 4096], F32),
    ("L0_ffn_W2", [D, 4096], F32), ("L0_ffn_b2", [1, D], F32),
    ("L1_attn_Wq", [D, D], F32), ("L1_attn_Wk", [D, D], F32),
    ("L1_attn_Wv", [D, D], F32), ("L1_attn_Wo", [D, D], F32),
    ("L1_attn_Wf1", [D, 64], F32), ("L1_attn_Wf2", [64, D], F32),
    ("L1_attn_Wg1", [D, 64], F32), ("L1_attn_Wg2", [64, D], F32),
    ("L1_attn_norm_weight", [1, D], F32),
    ("L1_ffn_k0", [2048, 4096], U8), ("L1_ffn_k1", [4096, 4096], U8),
    ("L1_ffn_k2", [4096, D], U8),
    ("L1_ffn_s0", [1, 1], F32), ("L1_ffn_s1", [1, 1], F32),
    ("L1_ffn_s2", [1, 1], F32),
]


def build_program(phases=PHASES_ALL, nt=T // MT):
    nc = bass.Bass("TRN2", target_bir_lowering=False)

    x_dev = nc.dram_tensor("x_dev", [D, T], F32, kind="ExternalInput").ap()
    y_dev = nc.dram_tensor("y_dev", [D, T], F32, kind="ExternalOutput").ap()
    wIn = {}
    for name, shape, dt in WEIGHT_NAMES:
        wIn[name] = nc.dram_tensor(name, shape, dt, kind="ExternalInput").ap()

    need = set()
    if "attn0" in phases:
        need |= {"wq0", "wk0", "wv0", "wg0", "wo0"}
    if "ffn0a" in phases or "ffn0b" in phases:
        need |= {"w1t", "w2timg"}
    if "ffn1a" in phases or "ffn1b" in phases:
        need |= {"w0f", "w1bimg", "w2bimg"}
    if "attn1" in phases:
        need |= {"wq1", "wk1", "wv1", "wo1"}
    shapes = {
        "wq0": [128, 4096], "wk0": [128, 4096], "wv0": [128, 8192],
        "wg0": [128, 8192], "wo0": [128, 8192],
        "w1t": [128, 32768], "w2timg": [8, 128, 32, 128],
        "w0f": [128, 32768], "w1bimg": [32, 128, 32, 128],
        "w2bimg": [8, 128, 32, 128],
        "wq1": [128, 8192], "wk1": [128, 8192], "wv1": [128, 8192],
        "wo1": [128, 8192],
    }
    scr = {k: nc.dram_tensor("scr_" + k, shapes[k], BF16, kind="Internal").ap()
           for k in need}
    scr["dbgprep"] = nc.dram_tensor("scr_dbgprep", [128, 16], F32,
                                    kind="Internal").ap()
    shapes["dbgprep"] = [128, 16]
    scr["a1dbg"] = nc.dram_tensor("scr_a1dbg", [128, 1024], F32,
                                  kind="Internal").ap()
    shapes["a1dbg"] = [128, 1024]

    sm = SbufMap(nc)
    x_fm = sm.alloc("x_fm", [128, 8 * MT], F32)
    xln = sm.alloc("xln", [128, 8 * MT], BF16)
    NSLOT = 6
    slots = [sm.alloc(f"slot{i}", [128, 4096], BF16) for i in range(NSLOT)]
    ident_bf = sm.alloc("ident_bf", [128, 128], BF16)
    utri_bf = sm.alloc("utri_bf", [128, 128], BF16)
    utri64x2_bf = sm.alloc("utri64x2_bf", [128, 128], BF16)
    mask8_f = sm.alloc("mask8_f", [128, 1024], BF16)
    mask64_f = sm.alloc("mask64_f", [64, 1024], BF16)
    nw0 = sm.alloc("nw0", [128, 1024], F32)
    nw1 = sm.alloc("nw1", [128, 1024], F32)
    bgk2b = sm.alloc("bgk2b", [128, 512], F32)
    b1c = sm.alloc("b1c", [128, 32], F32)
    b2h = sm.alloc("b2h", [128, 8], F32)
    wgk1_sb = sm.alloc("wgk1_sb", [128, 128], BF16)
    wgk2_sb = sm.alloc("wgk2_sb", [16, 512], BF16)
    wf1_sb = sm.alloc("wf1_sb", [128, 512], BF16)
    wf2_sb = sm.alloc("wf2_sb", [64, 1024], BF16)
    wg1_sb = sm.alloc("wg1_sb", [128, 512], BF16)
    wg2_sb = sm.alloc("wg2_sb", [64, 1024], BF16)
    onescol_bf = sm.alloc("onescol_bf", [128, 1], BF16)
    onescol_f = sm.alloc("onescol_f", [128, 1], F32)
    onesrow_f = sm.alloc("onesrow_f", [1, 128], F32)
    s_cols = sm.alloc("s_cols", [128, 4], F32)
    rowbuf = sm.alloc("rowbuf", [1, 512], F32)
    rowbuf2 = sm.alloc("rowbuf2", [1, 512], F32)
    lnxsq = sm.alloc("lnxsq", [128, 4 * TS], BF16)
    S0 = sm.alloc("S0", [64, 1024], F32)
    S0bf = sm.alloc("S0bf", [64, 1024], BF16)
    S1 = sm.alloc("S1", [64, 1024], F32)
    S1bf = sm.alloc("S1bf", [64, 1024], BF16)
    identf32 = sm.alloc("identf32", [32, 32], F32)
    identf8 = sm.alloc("identf8", [8, 8], F32)
    cvals = sm.alloc("cvals", [128, 6], F32)
    nc.const_aps.register(F32, EPS, cvals[:, 0:1])
    nc.const_aps.register(F32, 1.0, cvals[:, 1:2])
    nc.const_aps.register(F32, 0.5, cvals[:, 2:3])
    nc.const_aps.register(F32, -0.5, cvals[:, 3:4])
    nc.const_aps.register(F32, 0.0, cvals[:, 4:5])

    arena_base = (sm.cur + 63) // 64 * 64
    ARENA_SZ = (229312 - arena_base) // 64 * 64
    assert ARENA_SZ >= 61 * 1024, f"arena too small: {ARENA_SZ}"
    arena = Arena(nc, arena_base, ARENA_SZ)

    pb = [nc.alloc_psum_tensor(f"pb{i}", [128, 512], F32).ap() for i in range(5)]
    pw0 = nc.alloc_psum_tensor("pw0", [128, 1024], F32).ap()
    pt = nc.alloc_psum_tensor("pt", [128, 1024], BF16).ap()
    assert nc.psum_banks_remaining == 0

    bxh = [Buf("x0"), Buf("x1")]
    blnx4 = [Buf(f"lnxsq{a}") for a in range(4)]
    bxlnh = [Buf("xln0"), Buf("xln1")]
    brow = Buf("rows")
    bconst = Buf("const")
    bslot = [Buf(f"slot{i}") for i in range(NSLOT)]
    bpb = [Buf(f"pb{i}") for i in range(5)]
    bpw0, bpt = Buf("pw0"), Buf("pt")
    bS0, bS0bf, bS1, bS1bf = Buf("S0"), Buf("S0bf"), Buf("S1"), Buf("S1bf")
    bz1, bzf1, bzg1 = Buf("z1"), Buf("zf1"), Buf("zg1")
    bscr = {k: Buf("scr_" + k) for k in scr}
    by = Buf("y")

    ctx = contextlib.ExitStack()
    sems = {s: ctx.enter_context(nc.semaphore("sem_" + s)) for s in STREAMS}
    mk = MK(nc, sems)

    slot_rr = {"i": 0}

    def next_slot():
        i = slot_rr["i"] % NSLOT
        slot_rr["i"] += 1
        return slots[i], bslot[i]

    def emit_consts():
        mk.op("pool", lambda e: e.memset(ident_bf[:, :], 0.0), writes=[bconst])
        mk.op("pool", lambda e: e.affine_select(
            out=ident_bf[:, :], in_=ident_bf[:, :], compare_op=ALU.not_equal,
            fill=1.0, base=0, pattern=[[-1, 128]], channel_multiplier=1),
            reads=[bconst], writes=[bconst])
        mk.op("pool", lambda e: e.memset(utri_bf[:, :], 0.0), writes=[bconst])
        mk.op("pool", lambda e: e.affine_select(
            out=utri_bf[:, :], in_=utri_bf[:, :], compare_op=ALU.is_gt,
            fill=1.0, base=0, pattern=[[-1, 128]], channel_multiplier=1),
            reads=[bconst], writes=[bconst])
        mk.op("pool", lambda e: e.memset(utri64x2_bf[:, :], 0.0), writes=[bconst])
        for h in (0, 64):
            mk.op("pool", lambda e, h=h: e.affine_select(
                out=utri64x2_bf[h:h + 64, h:h + 64],
                in_=utri64x2_bf[h:h + 64, h:h + 64], compare_op=ALU.is_gt,
                fill=1.0, base=0, pattern=[[-1, 64]], channel_multiplier=1),
                reads=[bconst], writes=[bconst])
        mk.op("pool", lambda e: e.memset(mask8_f[:, :], 0.0), writes=[bconst])
        for i in range(8):
            mk.op("pool", lambda e, i=i: e.affine_select(
                out=mask8_f[:, 128 * i:128 * i + 128],
                in_=mask8_f[:, 128 * i:128 * i + 128], compare_op=ALU.is_gt,
                fill=1.0, base=0, pattern=[[-1, 128]], channel_multiplier=1),
                reads=[bconst], writes=[bconst])
        mk.op("pool", lambda e: e.memset(mask64_f[:, :], 0.0), writes=[bconst])
        for m in range(16):
            mk.op("pool", lambda e, m=m: e.affine_select(
                out=mask64_f[:, 64 * m:64 * m + 64],
                in_=mask64_f[:, 64 * m:64 * m + 64],
                compare_op=ALU.is_gt, fill=1.0, base=0,
                pattern=[[-1, 64]], channel_multiplier=1),
                reads=[bconst], writes=[bconst])
        mk.op("pool", lambda e: e.memset(onescol_bf[:, :], 1.0), writes=[bconst])
        mk.op("pool", lambda e: e.memset(onescol_f[:, :], 1.0), writes=[bconst])
        mk.op("pool", lambda e: e.memset(onesrow_f[:, :], 1.0), writes=[bconst])

        mk.op("sdma", lambda e: e.dma_start(out=rowbuf[0:1, 0:512],
                                            in_=wIn["L0_attn_bgk2"]),
              writes=[bconst])
        mk.op("pe", lambda e: e.matmul(pb[0][:, :], onesrow_f[:, :],
                                       rowbuf[0:1, 0:512], start=True, stop=True),
              reads=[bconst], writes=[bpb[0]])
        mk.op("dve", lambda e: e.tensor_copy(bgk2b[:, :], pb[0][:, :]),
              reads=[bpb[0]], writes=[bconst])

        mk.op("sdma", lambda e: e.dma_start(out=rowbuf2[0:1, 0:128],
                                            in_=wIn["L0_attn_norm_w"]),
              writes=[bconst])
        mk.op("pe", lambda e: e.matmul(pb[1][:, 0:128], onesrow_f[:, :],
                                       rowbuf2[0:1, 0:128], start=True, stop=True),
              reads=[bconst], writes=[bpb[1]])
        for i in range(8):
            mk.op("dve", lambda e, i=i: e.tensor_copy(
                nw0[:, 128 * i:128 * i + 128], pb[1][:, 0:128]),
                reads=[bpb[1]], writes=[bconst])

        for h in range(2):
            mk.op("sdma", lambda e, h=h: e.dma_start(
                out=rowbuf[0:1, :],
                in_=wIn["L1_attn_norm_weight"][:, 512 * h:512 * h + 512]),
                writes=[bconst])
            mk.op("pe", lambda e, h=h: e.matmul(
                pb[2][:, :], onesrow_f[:, :], rowbuf[0:1, :],
                start=True, stop=True), reads=[bconst], writes=[bpb[2]])
            mk.op("dve", lambda e, h=h: e.tensor_copy(
                nw1[:, 512 * h:512 * h + 512], pb[2][:, :]),
                reads=[bpb[2]], writes=[bconst])

        for i, k in enumerate(["L1_ffn_s0", "L1_ffn_s1", "L1_ffn_s2"]):
            mk.op("sdma", lambda e, i=i, k=k: e.dma_start(
                out=rowbuf[0:1, i:i + 1], in_=wIn[k]), writes=[bconst])
        mk.op("dve", lambda e: e.tensor_scalar(rowbuf[0:1, 0:2], rowbuf[0:1, 0:2],
                                               2.0, None, ALU.mult),
              reads=[bconst], writes=[bconst])
        mk.op("pe", lambda e: e.matmul(pb[3][:, 0:4], onesrow_f[:, :],
                                       rowbuf[0:1, 0:4], start=True, stop=True),
              reads=[bconst], writes=[bpb[3]])
        mk.op("dve", lambda e: e.tensor_copy(s_cols[:, :], pb[3][:, 0:4]),
              reads=[bpb[3]], writes=[bconst])

        b1t, bb1t = arena.alloc("pp_b1t", [32, 128], F32)
        mk.op("sdma", lambda e: e.dma_start(
            out=b1t[:, :],
            in_=wIn["L0_ffn_b1"].rearrange("a (j c) -> (a j) c", c=128)),
            writes=[bb1t])
        mk.op("pe", lambda e: e.transpose(pb[4][:, 0:32], b1t[:, :],
                                          identf32[:, :]),
              reads=[bb1t, bconst], writes=[bpb[4]])
        mk.op("dve", lambda e: e.tensor_copy(b1c[:, :], pb[4][:, 0:32]),
              reads=[bpb[4]], writes=[bconst])
        b2t, bb2t = arena.alloc("pp_b2t", [8, 128], F32)
        mk.op("sdma", lambda e: e.dma_start(
            out=b2t[:, :],
            in_=wIn["L0_ffn_b2"].rearrange("a (j c) -> (a j) c", c=128)),
            writes=[bb2t])
        mk.op("pe", lambda e: e.transpose(pb[4][:, 32:40], b2t[:, :],
                                          identf8[:, :]),
              reads=[bb2t, bconst], writes=[bpb[4]])
        mk.op("dve", lambda e: e.tensor_scalar(b2h[:, :], pb[4][:, 32:40], 0.5,
                                               None, ALU.mult),
              reads=[bpb[4]], writes=[bconst])

        mk.op("gdma", lambda e: e.dma_start(
            out=wgk1_sb.rearrange("p (j c) -> p j c", c=16),
            in_=wIn["L0_attn_Wgk1"].rearrange("(j p) c -> p j c", p=128)),
            writes=[bconst])
        mk.op("gdma", lambda e: e.dma_start(out=wgk2_sb[:, :],
                                            in_=wIn["L0_attn_Wgk2"]),
              writes=[bconst])
        mk.op("gdma", lambda e: e.dma_start(
            out=wf1_sb.rearrange("p (j c) -> p j c", c=64),
            in_=wIn["L1_attn_Wf1"].rearrange("(j p) c -> p j c", p=128)),
            writes=[bconst])
        mk.op("gdma", lambda e: e.dma_start(out=wf2_sb[:, :],
                                            in_=wIn["L1_attn_Wf2"]),
              writes=[bconst])
        mk.op("gdma", lambda e: e.dma_start(
            out=wg1_sb.rearrange("p (j c) -> p j c", c=64),
            in_=wIn["L1_attn_Wg1"].rearrange("(j p) c -> p j c", p=128)),
            writes=[bconst])
        mk.op("gdma", lambda e: e.dma_start(out=wg2_sb[:, :],
                                            in_=wIn["L1_attn_Wg2"]),
              writes=[bconst])

    def emit_identf():
        mk.op("pool", lambda e: e.memset(cvals[:, 0:1], EPS), writes=[bconst])
        mk.op("pool", lambda e: e.memset(cvals[:, 1:2], 1.0), writes=[bconst])
        mk.op("pool", lambda e: e.memset(cvals[:, 2:3], 0.5), writes=[bconst])
        mk.op("pool", lambda e: e.memset(cvals[:, 3:4], -0.5), writes=[bconst])
        mk.op("pool", lambda e: e.memset(cvals[:, 4:5], 0.0), writes=[bconst])
        mk.op("pool", lambda e: e.memset(identf32[:, :], 0.0), writes=[bconst])
        mk.op("pool", lambda e: e.affine_select(
            out=identf32[:, :], in_=identf32[:, :], compare_op=ALU.not_equal,
            fill=1.0, base=0, pattern=[[-1, 32]], channel_multiplier=1),
            reads=[bconst], writes=[bconst])
        mk.op("pool", lambda e: e.memset(identf8[:, :], 0.0), writes=[bconst])
        mk.op("pool", lambda e: e.affine_select(
            out=identf8[:, :], in_=identf8[:, :], compare_op=ALU.not_equal,
            fill=1.0, base=0, pattern=[[-1, 8]], channel_multiplier=1),
            reads=[bconst], writes=[bconst])

    def emit_prep():
        for src, dst, cc in [("L0_attn_Wq", "wq0", 512),
                             ("L0_attn_Wk", "wk0", 512),
                             ("L0_attn_Wv", "wv0", 1024),
                             ("L0_attn_Wg", "wg0", 1024),
                             ("L0_attn_Wo", "wo0", 1024),
                             ("L1_attn_Wq", "wq1", 1024),
                             ("L1_attn_Wk", "wk1", 1024),
                             ("L1_attn_Wv", "wv1", 1024),
                             ("L1_attn_Wo", "wo1", 1024)]:
            if dst in scr:
                mk.op("gdma", lambda e, src=src, dst=dst, cc=cc: e.dma_start(
                    out=scr[dst].rearrange("p (j c) -> p j c", c=cc),
                    in_=wIn[src].rearrange("(j p) c -> p j c", p=128)),
                    writes=[bscr[dst]])
        if "w1t" in scr:
            _prep_quant_ffn0()
            mk.barrier()
            arena.reset()
        if "w0f" in scr:
            _prep_ffn1()

    def _prep_quant_ffn0():
        wt = [arena.alloc("pp_wtile0", [128, 4096], F32),
              arena.alloc("pp_wtile1", [128, 1024], F32)]
        m01, bm01 = arena.alloc("pp_m01", [128, 4096], F32)
        qtile, bqt = arena.alloc("pp_qtile", [128, 4096], BF16)
        stage, bst = arena.alloc("pp_stage", [128, 4096], BF16)
        acc, bacc = arena.alloc("pp_acc", [128, 1], F32)
        tmp1, btmp1 = arena.alloc("pp_tmp1", [128, 1], F32)
        ginv, bgv = arena.alloc("pp_ginv", [128, 1], F32)

        for wname, dstk, ntile, width, nelem in [
                ("L0_ffn_W1", "w1t", 32, 1024, 4096 * 1024),
                ("L0_ffn_W2", "w2timg", 8, 4096, 4096 * 1024)]:
            mk.op("pool", lambda e: e.memset(acc[:, :], 0.0), writes=[bacc])
            for i in range(ntile):
                wtile, bwt = wt[i % 2] if width == 1024 else wt[0]
                mk.op("sdma", lambda e, i=i, wname=wname, width=width,
                      wtile=wtile: e.dma_start(
                    out=wtile[:, 0:width],
                    in_=wIn[wname][128 * i:128 * (i + 1), :]),
                    reads=[bwt], writes=[bwt])
                mk.op("dve", lambda e, width=width, wtile=wtile: e.tensor_reduce(
                    tmp1[:, :], wtile[:, 0:width], axis=AX.X, op=ALU.add,
                    apply_absolute_value=True), reads=[bwt], writes=[btmp1],
                    hazard=True)
                mk.op("dve", lambda e: e.tensor_add(acc[:, :], acc[:, :],
                                                    tmp1[:, :]),
                      reads=[btmp1, bacc], writes=[bacc])
            mk.op("pe", lambda e: e.matmul(pb[4][0:1, 0:1], acc[:, :],
                                           onescol_f[:, :], start=True, stop=True),
                  reads=[bacc, bconst], writes=[bpb[4]])
            mk.op("dve", lambda e, nelem=nelem: e.tensor_scalar(
                rowbuf2[0:1, 8:9], pb[4][0:1, 0:1], 1.0 / nelem, EPS,
                ALU.mult, ALU.add), reads=[bpb[4]], writes=[bconst])
            mk.op("dve", lambda e: e.reciprocal(rowbuf2[0:1, 9:10],
                                                rowbuf2[0:1, 8:9]),
                  reads=[bconst], writes=[bconst], serialize=True, hazard=True)
            mk.op("pe", lambda e: e.matmul(pb[4][:, 0:1], onesrow_f[:, :],
                                           rowbuf2[0:1, 9:10], start=True,
                                           stop=True),
                  reads=[bconst], writes=[bpb[4]])
            mk.op("dve", lambda e: e.tensor_copy(ginv[:, :], pb[4][:, 0:1]),
                  reads=[bpb[4]], writes=[bgv])
            for i in range(ntile):
                wtile, bwt = wt[i % 2] if width == 1024 else wt[0]
                sgn, bsg = wtile, bwt
                mk.op("sdma", lambda e, i=i, wname=wname, width=width,
                      wtile=wtile: e.dma_start(
                    out=wtile[:, 0:width],
                    in_=wIn[wname][128 * i:128 * (i + 1), :]),
                    reads=[bwt], writes=[bwt])
                mk.op("dve", lambda e, width=width, wtile=wtile: e.tensor_scalar(
                    m01[:, 0:width], wtile[:, 0:width], ginv[:, 0:1], None,
                    ALU.mult), reads=[bwt, bgv], writes=[bm01])
                mk.op("act", lambda e, width=width, sgn=sgn: e.sign(
                    sgn[:, 0:width], m01[:, 0:width], bias=0.5),
                    reads=[bm01], writes=[bsg])
                mk.op("act", lambda e, width=width: e.sign(
                    m01[:, 0:width], m01[:, 0:width], bias=-0.5),
                    reads=[bm01], writes=[bm01])
                mk.op("dve", lambda e, width=width, sgn=sgn: e.tensor_add(
                    sgn[:, 0:width], sgn[:, 0:width], m01[:, 0:width]),
                    reads=[bm01, bsg], writes=[bsg])
                mk.op("dve", lambda e, width=width, sgn=sgn: e.tensor_single_scalar(
                    qtile[:, 0:width], sgn[:, 0:width], 0.5, ALU.mult),
                    reads=[bsg], writes=[bqt])
                nblk = width // 128
                for bat in range(0, nblk, 8):
                    nb = min(8, nblk - bat)

                    def trs(e, bat=bat, nb=nb):
                        last = None
                        for j in range(nb):
                            last = e.transpose(
                                pt[:, 128 * j:128 * j + 128],
                                qtile[:, 128 * (bat + j):128 * (bat + j) + 128],
                                ident_bf[:, :])
                        return last
                    mk.op("pe", trs, reads=[bqt, bconst], writes=[bpt])
                    mk.op("act", lambda e, bat=bat, nb=nb: e.copy(
                        stage[:, 128 * bat:128 * (bat + nb)], pt[:, 0:128 * nb]),
                        reads=[bpt], writes=[bst])
                if dstk == "w1t":
                    mk.op("sdma", lambda e, i=i: e.dma_start(
                        out=scr["w1t"].rearrange(
                            "p (fb j c) -> p fb j c", fb=8, j=8)
                        [:, i // 4, :, 128 * (i % 4):128 * (i % 4) + 128],
                        in_=stage[:, 0:1024].rearrange("p (j c) -> p j c", c=128)),
                        reads=[bst], writes=[bscr["w1t"]])
                else:
                    mk.op("sdma", lambda e, i=i: e.dma_start(
                        out=scr["w2timg"][i],
                        in_=stage.rearrange("p (j c) -> p j c", c=128)),
                        reads=[bst], writes=[bscr["w2timg"]])

    def _prep_ffn1():
        ta, bta = arena.alloc("pp_f1a", [128, 4096], BF16)
        tb, btb = arena.alloc("pp_f1b", [128, 4096], BF16)
        for j in range(8):
            mk.op("gdma", lambda e, j=j: e.dma_start(
                out=ta[:, :], in_=wIn["L1_ffn_k0"][128 * j:128 * (j + 1), :]),
                reads=[bta], writes=[bta])
            mk.op("gdma", lambda e, j=j: e.dma_start(
                out=tb[:, :],
                in_=wIn["L1_ffn_k0"][1024 + 128 * j:1024 + 128 * (j + 1), :]),
                reads=[btb], writes=[btb])
            mk.op("dve", lambda e: e.tensor_sub(ta[:, :], ta[:, :], tb[:, :]),
                  reads=[bta, btb], writes=[bta])
            mk.op("sdma", lambda e, j=j: e.dma_start(
                out=scr["w0f"].rearrange("p (fb jj c) -> p fb jj c",
                                         fb=8, jj=8)[:, :, j, :],
                in_=ta.rearrange("p (fb c) -> p fb c", c=512)),
                reads=[bta], writes=[bscr["w0f"]])
        for j in range(32):
            mk.op("gdma", lambda e, j=j: e.dma_start(
                out=ta[:, :], in_=wIn["L1_ffn_k1"][128 * j:128 * (j + 1), :]),
                reads=[bta], writes=[bta])
            mk.op("dve", lambda e: e.tensor_scalar(ta[:, :], ta[:, :], 2.0, -1.0,
                                                   ALU.mult, ALU.add),
                  reads=[bta], writes=[bta])
            mk.op("sdma", lambda e, j=j: e.dma_start(
                out=scr["w1bimg"][:, :, j, :].rearrange("fb p c -> p fb c"),
                in_=ta.rearrange("p (fb c) -> p fb c", c=128)),
                reads=[bta], writes=[bscr["w1bimg"]])
        for j in range(32):
            mk.op("gdma", lambda e, j=j: e.dma_start(
                out=tb[:, 0:1024], in_=wIn["L1_ffn_k2"][128 * j:128 * (j + 1), :]),
                reads=[btb], writes=[btb])
            mk.op("dve", lambda e: e.tensor_scalar(tb[:, 0:1024], tb[:, 0:1024],
                                                   2.0, -1.0, ALU.mult, ALU.add),
                  reads=[btb], writes=[btb])
            mk.op("sdma", lambda e, j=j: e.dma_start(
                out=scr["w2bimg"][:, :, j, :].rearrange("fb p c -> p fb c"),
                in_=tb[:, 0:1024].rearrange("p (fb c) -> p fb c", c=128)),
                reads=[btb], writes=[bscr["w2bimg"]])

    def emit_load_x(ti):
        mk.op("sdma", lambda e, ti=ti: e.dma_start(
            out=x_fm.rearrange("p (j t) -> p j t", j=8),
            in_=x_dev.rearrange("(j p) t -> p j t", p=128)
            [:, :, MT * ti:MT * (ti + 1)]),
            reads=[bxh[0], bxh[1]], writes=[bxh[0], bxh[1]])

    def emit_store_x(ti):
        mk.op("sdma", lambda e, ti=ti: e.dma_start(
            out=y_dev.rearrange("(j p) t -> p j t", p=128)
            [:, :, MT * ti:MT * (ti + 1)],
            in_=x_fm.rearrange("p (j t) -> p j t", j=8)),
            reads=[bxh[0], bxh[1]], writes=[by])

    def emit_ln(tsI, fset=False):
        if True:
            t0 = TS * tsI
            bxT, bxlnT = bxh[tsI], bxlnh[tsI]
            if fset:
                sA, bsA = pw0[:, 0:512], bpw0
                sB, bsB = pw0[:, 512:1024], bpw0
                cA, bcA = pw0[:, 0:512], bpw0
                cB, bcB = pw0[:, 512:1024], bpw0
            else:
                sA, bsA = pb[0][:, :], bpb[0]
                sB, bsB = pb[1][:, :], bpb[1]
                cA, bcA = pb[2][:, :], bpb[2]
                cB, bcB = pb[3][:, :], bpb[3]
            for j in range(8):
                sl = j % 4
                mk.op("act", lambda e, j=j, sl=sl, t0=t0: e.activation(
                    lnxsq[:, TS * sl:TS * sl + TS],
                    x_fm[:, 1024 * j + t0:1024 * j + t0 + TS], AF.Square),
                    reads=[bxT], writes=[blnx4[sl]])
                if j == 3:
                    for jj in range(8):
                        mk.op("pe", lambda e, jj=jj, t0=t0, sA=sA: e.matmul(
                            sA[0:1, :], onescol_f[:, :],
                            x_fm[:, 1024 * jj + t0:1024 * jj + t0 + TS],
                            start=(jj == 0), stop=(jj == 7)),
                            reads=[bxT, bconst], writes=[bsA])
                mk.op("pe", lambda e, j=j, sl=sl, sB=sB: e.matmul(
                    sB[0:1, :], onescol_bf[:, :],
                    lnxsq[:, TS * sl:TS * sl + TS],
                    start=(j == 0), stop=(j == 7)),
                    reads=[blnx4[sl], bconst], writes=[bsB])
            mk.op("dve", lambda e, sA=sA: e.tensor_scalar(
                rowbuf[0:1, 0:TS], sA[0:1, :], 1.0 / D, None, ALU.mult),
                reads=[bsA], writes=[brow])
            mk.op("dve", lambda e: e.tensor_mul(
                rowbuf2[0:1, 0:TS], rowbuf[0:1, 0:TS], rowbuf[0:1, 0:TS]),
                reads=[brow], writes=[brow])
            mk.op("dve", lambda e, sB=sB: e.scalar_tensor_tensor(
                rowbuf2[0:1, 0:TS], sB[0:1, :], 1.0 / D, rowbuf2[0:1, 0:TS],
                ALU.mult, ALU.subtract), reads=[bsB, brow], writes=[brow])
            mk.op("act", lambda e: e.activation(
                rowbuf2[0:1, 0:TS], rowbuf2[0:1, 0:TS], AF.Ln,
                bias=cvals[0:1, 0:1]),
                reads=[brow], writes=[brow])
            mk.op("act", lambda e: e.activation(
                rowbuf2[0:1, 0:TS], rowbuf2[0:1, 0:TS], AF.Exp, scale=-0.5,
                bias=cvals[0:1, 4:5]),
                reads=[brow], writes=[brow])
            mk.op("pe", lambda e, cA=cA: e.matmul(cA[:, :], onesrow_f[:, :],
                                                  rowbuf2[0:1, 0:TS],
                                                  start=True, stop=True),
                  reads=[brow, bconst], writes=[bcA])
            mk.op("pe", lambda e, cB=cB: e.matmul(cB[:, :], onesrow_f[:, :],
                                                  rowbuf[0:1, 0:TS],
                                                  start=True, stop=True),
                  reads=[brow, bconst], writes=[bcB])
            for j in range(8):
                mk.op("dve", lambda e, j=j, t0=t0, cB=cB: e.tensor_tensor(
                    xln[:, 1024 * j + t0:1024 * j + t0 + TS],
                    x_fm[:, 1024 * j + t0:1024 * j + t0 + TS], cB[:, :],
                    ALU.subtract), reads=[bxT, bcB], writes=[bxlnT])
                mk.op("dve", lambda e, j=j, t0=t0, cA=cA: e.tensor_tensor(
                    xln[:, 1024 * j + t0:1024 * j + t0 + TS],
                    xln[:, 1024 * j + t0:1024 * j + t0 + TS], cA[:, :],
                    ALU.mult), reads=[bxlnT, bcA], writes=[bxlnT])

    def emit_ffn0(lnmid=None, lntail=None):
        h, bh = arena.alloc("f0_h", [128, 32 * TS], BF16)
        tmp, btmp = arena.alloc("f0_tmp", [128, TS], F32)
        for tsI in range(MT // TS):
            t0 = TS * tsI
            bxT, bxlnT = bxh[tsI], bxlnh[tsI]
            for fb in range(8):
                sl, bsl = next_slot()
                mk.op("sdma", lambda e, sl=sl, fb=fb: e.dma_start(
                    out=sl[:, :],
                    in_=scr["w1t"][:, 4096 * fb:4096 * (fb + 1)]),
                    reads=[bscr["w1t"], bsl], writes=[bsl])
                for fc in range(4):
                    fg = 4 * fb + fc
                    bp = bpb[fg % 4]
                    pa = pb[fg % 4]

                    def m1(e, sl=sl, fc=fc, pa=pa, t0=t0):
                        last = None
                        for j in range(8):
                            last = e.matmul(
                                pa[:, :],
                                sl[:, 512 * j + 128 * fc:512 * j + 128 * fc + 128],
                                xln[:, 1024 * j + t0:1024 * j + t0 + TS],
                                start=(j == 0), stop=(j == 7))
                        return last
                    mk.op("pe", m1, reads=[bsl, bxlnT], writes=[bp])
                    mk.op("act", lambda e, fg=fg, pa=pa: e.activation(
                        h[:, TS * fg:TS * (fg + 1)], pa[:, :], AF.Relu,
                        bias=b1c[:, fg:fg + 1]),
                        reads=[bp, bconst], writes=[bh])
            if tsI == 0 and lnmid:
                lnmid()
            if tsI == 1 and lntail:
                lntail()
            for db in range(8):
                sl, bsl = next_slot()
                mk.op("sdma", lambda e, sl=sl, db=db: e.dma_start(
                    out=sl[:, :],
                    in_=scr["w2timg"][db].rearrange("p a b -> p (a b)")),
                    reads=[bscr["w2timg"], bsl], writes=[bsl])
                bp = bpb[db % 4]
                pa = pb[db % 4]

                def m2(e, sl=sl, pa=pa):
                    last = None
                    for jj in range(32):
                        last = e.matmul(pa[:, :], sl[:, 128 * jj:128 * jj + 128],
                                        h[:, TS * jj:TS * (jj + 1)],
                                        start=(jj == 0), stop=(jj == 31))
                    return last
                mk.op("pe", m2, reads=[bsl, bh], writes=[bp])
                mk.op("act", lambda e, db=db, pa=pa: e.activation(
                    tmp[:, :], pa[:, :], AF.Identity, scale=0.5,
                    bias=b2h[:, db:db + 1]),
                    reads=[bp, bconst], writes=[btmp])
                mk.op("dve", lambda e, db=db, t0=t0: e.tensor_add(
                    x_fm[:, 1024 * db + t0:1024 * db + t0 + TS],
                    x_fm[:, 1024 * db + t0:1024 * db + t0 + TS], tmp[:, :]),
                    reads=[btmp, bxT], writes=[bxT])

    def emit_ffn1(lnmid=None, lntail=None):
        h1, bh1 = arena.alloc("f1_h1", [128, 32 * TS], BF16)
        h2, bh2 = arena.alloc("f1_h2", [128, 32 * TS], BF16)
        for tsI in range(MT // TS):
            t0 = TS * tsI
            bxT, bxlnT = bxh[tsI], bxlnh[tsI]
            for fb in range(8):
                sl, bsl = next_slot()
                mk.op("sdma", lambda e, sl=sl, fb=fb: e.dma_start(
                    out=sl[:, :],
                    in_=scr["w0f"][:, 4096 * fb:4096 * (fb + 1)]),
                    reads=[bscr["w0f"], bsl], writes=[bsl])
                for fc in range(4):
                    fg = 4 * fb + fc
                    bp = bpb[fg % 4]
                    pa = pb[fg % 4]

                    def m0(e, sl=sl, fc=fc, pa=pa, t0=t0):
                        last = None
                        for j in range(8):
                            last = e.matmul(
                                pa[:, :],
                                sl[:, 512 * j + 128 * fc:512 * j + 128 * fc + 128],
                                xln[:, 1024 * j + t0:1024 * j + t0 + TS],
                                start=(j == 0), stop=(j == 7))
                        return last
                    mk.op("pe", m0, reads=[bsl, bxlnT], writes=[bp])
                    mk.op("dve", lambda e, fg=fg, pa=pa: e.tensor_scalar(
                        h1[:, TS * fg:TS * (fg + 1)], pa[:, :],
                        s_cols[:, 0:1], 0.0, ALU.mult, ALU.max),
                        reads=[bp, bconst], writes=[bh1])
            if tsI == 0 and lnmid:
                lnmid()
            if tsI == 1 and lntail:
                lntail()
            for fb2 in range(32):
                sl, bsl = next_slot()
                mk.op("sdma", lambda e, sl=sl, fb2=fb2: e.dma_start(
                    out=sl[:, :],
                    in_=scr["w1bimg"][fb2].rearrange("p a b -> p (a b)")),
                    reads=[bscr["w1bimg"], bsl], writes=[bsl])
                bp = bpb[fb2 % 4]
                pa = pb[fb2 % 4]

                def m1(e, sl=sl, pa=pa):
                    last = None
                    for jj in range(32):
                        last = e.matmul(pa[:, :], sl[:, 128 * jj:128 * jj + 128],
                                        h1[:, TS * jj:TS * (jj + 1)],
                                        start=(jj == 0), stop=(jj == 31))
                    return last
                mk.op("pe", m1, reads=[bsl, bh1], writes=[bp])
                mk.op("dve", lambda e, fb2=fb2, pa=pa: e.tensor_scalar(
                    h2[:, TS * fb2:TS * (fb2 + 1)], pa[:, :],
                    s_cols[:, 1:2], 0.0, ALU.mult, ALU.max),
                    reads=[bp, bconst], writes=[bh2])
            for db in range(8):
                sl, bsl = next_slot()
                mk.op("sdma", lambda e, sl=sl, db=db: e.dma_start(
                    out=sl[:, :],
                    in_=scr["w2bimg"][db].rearrange("p a b -> p (a b)")),
                    reads=[bscr["w2bimg"], bsl], writes=[bsl])
                bp = bpb[db % 4]
                pa = pb[db % 4]

                def m2(e, sl=sl, pa=pa):
                    last = None
                    for jj in range(32):
                        last = e.matmul(pa[:, :], sl[:, 128 * jj:128 * jj + 128],
                                        h2[:, TS * jj:TS * (jj + 1)],
                                        start=(jj == 0), stop=(jj == 31))
                    return last
                mk.op("pe", m2, reads=[bsl, bh2], writes=[bp])
                mk.op("dve", lambda e, db=db, t0=t0, pa=pa: e.scalar_tensor_tensor(
                    x_fm[:, 1024 * db + t0:1024 * db + t0 + TS], pa[:, :],
                    s_cols[:, 2:3],
                    x_fm[:, 1024 * db + t0:1024 * db + t0 + TS],
                    ALU.mult, ALU.add),
                    reads=[bp, bconst, bxT], writes=[bxT])

    def emit_attn0(ti, lnmid=None, lntail=None):
        z1gk, _bz1 = arena.alloc("a0_z1gk", [16, 1024], BF16)
        u, bu = arena.alloc("a0_u", [128, 512], F32)
        mn, bmn = arena.alloc("a0_mn", [128, 512], F32)
        gkb, bgkb = arena.alloc("a0_gk", [128, 512], BF16)
        eb, beb = arena.alloc("a0_eb", [128, 512], F32)
        enb, benb = arena.alloc("a0_enb", [128, 512], F32)
        qe, bqe = arena.alloc("a0_qe", [128, 512], BF16)
        ke2 = [arena.alloc(f"a0_ke{a}", [128, 512], BF16) for a in range(2)]
        qefm2 = [arena.alloc(f"a0_qefm{a}", [64, 1024], BF16) for a in range(2)]
        kefm2 = [arena.alloc(f"a0_kefm{a}", [64, 1024], BF16) for a in range(2)]
        vbf2 = [arena.alloc(f"a0_vbf{a}", [128, 1024], BF16) for a in range(2)]
        am, bam = arena.alloc("a0_am", [128, 1024], BF16)
        osq, bosq = arena.alloc("a0_osq", [128, 1024], F32)
        oep, boep = arena.alloc("a0_oep", [128, 1024], BF16)
        rsq, brsq = arena.alloc("a0_rsq", [128, 8], F32)
        scol2 = [arena.alloc(f"a0_scol{a}", [64, 8], F32) for a in range(2)]
        oefm, boefm = arena.alloc("a0_oefm", [128, 8192], BF16)

        if ti == 0:
            mk.op("pool", lambda e: e.memset(S0[:, :], 0.0), writes=[bS0])
            mk.op("pool", lambda e: e.memset(S0bf[:, :], 0.0), writes=[bS0bf])

        slq, bslq = next_slot()
        mk.op("sdma", lambda e, sl=slq: e.dma_start(
            out=sl[:, :], in_=scr["wq0"]),
            reads=[bscr["wq0"], bslq], writes=[bslq])
        slk, bslk = next_slot()
        mk.op("sdma", lambda e, sl=slk: e.dma_start(
            out=sl[:, :], in_=scr["wk0"]),
            reads=[bscr["wk0"], bslk], writes=[bslk])
        slv = []
        for hh in range(2):
            sl, bsl = next_slot()
            mk.op("sdma", lambda e, sl=sl, hh=hh: e.dma_start(
                out=sl[:, :],
                in_=scr["wv0"][:, 4096 * hh:4096 * (hh + 1)]),
                reads=[bscr["wv0"], bsl], writes=[bsl])
            slv.append((sl, bsl))
        slg = []
        for hh in range(2):
            sl, bsl = next_slot()
            mk.op("sdma", lambda e, sl=sl, hh=hh: e.dma_start(
                out=sl[:, :],
                in_=scr["wg0"][:, 4096 * hh:4096 * (hh + 1)]),
                reads=[bscr["wg0"], bsl], writes=[bsl])
            slg.append((sl, bsl))

        def emit_z1(half):
            def z1mm(e, half=half):
                last = None
                for j in range(8):
                    last = e.matmul(
                        pb[4][0:16, :], wgk1_sb[:, 16 * j:16 * j + 16],
                        xln[:, 1024 * j + 512 * half:1024 * j + 512 * half + 512],
                        start=(j == 0), stop=(j == 7))
                return last
            mk.op("pe", z1mm, reads=[bxlnh[half], bconst], writes=[bpb[4]])
            mk.op("dve", lambda e, half=half: e.tensor_copy(
                z1gk[:, 512 * half:512 * half + 512], pb[4][0:16, :]),
                reads=[bpb[4]], writes=[bz1])
        emit_z1(0)

        ocp2 = [arena.alloc(f"a0_ocp{a}", [128, 1024], F32) for a in range(2)]
        sgg3 = [arena.alloc(f"a0_sg{a}", [128, 1024], BF16) for a in range(3)]

        def emit_epi(c):
            par = c % 2
            tc = C0 * c
            ocp, bocp = ocp2[par]
            sgg, bsgg = sgg3[c % 3]
            mk.op("act", lambda e, ocp=ocp: e.activation(
                osq[:, :], ocp[:, :], AF.Square),
                reads=[bocp], writes=[bosq])
            mk.op("dve", lambda e: e.tensor_reduce(
                rsq[:, 0:8], osq.rearrange("p (h c) -> p h c", c=128),
                axis=AX.X, op=ALU.add), reads=[bosq], writes=[brsq],
                hazard=True)
            mk.op("act", lambda e: e.activation(rsq[:, 0:8], rsq[:, 0:8],
                                                AF.Ln, scale=1.0 / 128.0,
                                                bias=cvals[:, 0:1]),
                  reads=[brsq], writes=[brsq])
            mk.op("act", lambda e: e.activation(rsq[:, 0:8], rsq[:, 0:8],
                                                AF.Exp, scale=-0.5,
                                                bias=cvals[:, 4:5]),
                  reads=[brsq], writes=[brsq])
            mk.op("dve", lambda e, sgg=sgg: e.tensor_mul(
                sgg[:, :], sgg[:, :], nw0[:, :]),
                reads=[bsgg, bconst], writes=[bsgg])
            for h in range(8):
                mk.op("dve", lambda e, h=h, ocp=ocp, sgg=sgg:
                      e.scalar_tensor_tensor(
                          oep[:, 128 * h:128 * h + 128],
                          ocp[:, 128 * h:128 * h + 128],
                          rsq[:, h:h + 1], sgg[:, 128 * h:128 * h + 128],
                          ALU.mult, ALU.mult),
                      reads=[bocp, brsq, bsgg], writes=[boep])

            def tr_o(e):
                last = None
                for j in range(8):
                    last = e.transpose(pt[:, 128 * j:128 * j + 128],
                                       oep[:, 128 * j:128 * j + 128],
                                       ident_bf[:, :])
                return last
            mk.op("pe", tr_o, reads=[boep, bconst], writes=[bpt])
            mk.op("act", lambda e, tc=tc: e.copy(
                oefm.rearrange("p (j t) -> p j t", j=8)[:, :, tc:tc + 128],
                pt.rearrange("p (j t) -> p j t", j=8)),
                reads=[bpt], writes=[boefm])

        def emit_A(c):
            tc = C0 * c
            par = c % 2
            bxlnT = bxlnh[tc // TS]
            sgg, bsgg = sgg3[c % 3]
            ke, bke = ke2[par]
            qefm, bqefm = qefm2[par]
            kefm, bkefm = kefm2[par]
            vbf, bvbf = vbf2[par]
            scol, bscol = scol2[par]

            mk.op("pe", lambda e, tc=tc: e.matmul(
                pb[4][:, :], z1gk[:, tc:tc + 128], wgk2_sb[:, :],
                start=True, stop=True),
                reads=[bz1, bconst], writes=[bpb[4]])
            mk.op("dve", lambda e: e.tensor_add(u[:, :], pb[4][:, :],
                                                bgk2b[:, :]),
                  reads=[bpb[4], bconst], writes=[bu])
            mk.op("act", lambda e: e.activation(mn[:, :], u[:, :], AF.Sigmoid),
                  reads=[bu], writes=[bmn])
            mk.op("act", lambda e: e.activation(gkb[:, :], mn[:, :], AF.Ln),
                  reads=[bmn], writes=[bgkb])

            def q_mm(e, tc=tc):
                last = None
                for j in range(8):
                    last = e.matmul(pb[0][:, :],
                                    xln[:, 1024 * j + tc:1024 * j + tc + 128],
                                    slq[:, 512 * j:512 * (j + 1)],
                                    start=(j == 0), stop=(j == 7))
                return last
            mk.op("pe", q_mm, reads=[bxlnT, bslq], writes=[bpb[0]])

            def k_mm(e, tc=tc):
                last = None
                for j in range(8):
                    last = e.matmul(pb[1][:, :],
                                    xln[:, 1024 * j + tc:1024 * j + tc + 128],
                                    slk[:, 512 * j:512 * (j + 1)],
                                    start=(j == 0), stop=(j == 7))
                return last
            mk.op("pe", k_mm, reads=[bxlnT, bslk], writes=[bpb[1]])

            def v_mm(e, tc=tc):
                last = None
                for hh in range(2):
                    for j in range(8):
                        sl = slv[j // 4][0]
                        last = e.matmul(
                            pw0[:, 512 * hh:512 * hh + 512],
                            xln[:, 1024 * j + tc:1024 * j + tc + 128],
                            sl[:, 1024 * (j % 4) + 512 * hh:
                               1024 * (j % 4) + 512 * hh + 512],
                            start=(j == 0), stop=(j == 7))
                return last
            mk.op("pe", v_mm, reads=[bxlnT, slv[0][1], slv[1][1]], writes=[bpw0])
            mk.op("act", lambda e: e.copy(vbf[:, :], pw0[:, :]),
                  reads=[bpw0], writes=[bvbf])

            def g_mm(e, hh, tc=tc):
                last = None
                for j in range(8):
                    sl = slg[j // 4][0]
                    last = e.matmul(
                        pb[2 + hh][:, :],
                        xln[:, 1024 * j + tc:1024 * j + tc + 128],
                        sl[:, 1024 * (j % 4) + 512 * hh:
                           1024 * (j % 4) + 512 * hh + 512],
                        start=(j == 0), stop=(j == 7))
                return last
            for hh in range(2):
                mk.op("pe", lambda e, hh=hh, tc=tc: g_mm(e, hh, tc),
                      reads=[bxlnT, slg[0][1], slg[1][1]], writes=[bpb[2 + hh]])
            for hh in range(2):
                mk.op("act", lambda e, hh=hh, sgg=sgg: e.activation(
                    sgg[:, 512 * hh:512 * hh + 512], pb[2 + hh][:, :],
                    AF.Silu), reads=[bpb[2 + hh]], writes=[bsgg])

            mk.op("pe", lambda e: e.matmul(pb[4][:, :], utri_bf[:, :],
                                           gkb[:, :], start=True, stop=True),
                  reads=[bgkb, bconst], writes=[bpb[4]])
            mk.op("act", lambda e: e.activation(eb[:, :], pb[4][:, :], AF.Exp,
                                                scale=1.0 / 16.0,
                                                bias=cvals[:, 4:5]),
                  reads=[bpb[4]], writes=[beb])
            mk.op("act", lambda e: e.activation(enb[:, :], pb[4][:, :], AF.Exp,
                                                scale=-1.0 / 16.0),
                  reads=[bpb[4]], writes=[benb])

            mk.op("dve", lambda e: e.scalar_tensor_tensor(
                qe[:, :], pb[0][:, :], 0.125, eb[:, :], ALU.mult, ALU.mult),
                reads=[bpb[0], beb], writes=[bqe])
            mk.op("dve", lambda e: e.tensor_tensor(
                ke[:, :], pb[1][:, :], enb[:, :], ALU.mult),
                reads=[bpb[1], benb], writes=[bke])

            def blc(e):
                last = None
                for h in range(8):
                    last = e.matmul(pb[1][0:64, h:h + 1],
                                    gkb[:, 64 * h:64 * h + 64],
                                    onescol_bf[:, :], start=True, stop=True)
                return last
            mk.op("pe", blc, reads=[bgkb, bconst], writes=[bpb[1]])
            mk.op("act", lambda e: e.activation(scol[:, :], pb[1][0:64, 0:8],
                                                AF.Exp, scale=1.0 / 16.0,
                                                bias=cvals[0:64, 4:5]),
                  reads=[bpb[1]], writes=[bscol])

            def tr_q(e):
                last = None
                for h in range(8):
                    last = e.transpose(pt[0:64, 128 * h:128 * h + 128],
                                       qe[:, 64 * h:64 * h + 64],
                                       ident_bf[:, :])
                return last
            mk.op("pe", tr_q, reads=[bqe, bconst], writes=[bpt])
            mk.op("dve", lambda e: e.tensor_copy(qefm[:, :], pt[0:64, :]),
                  reads=[bpt], writes=[bqefm])

            def tr_k(e):
                last = None
                for h in range(8):
                    last = e.transpose(pt[0:64, 128 * h:128 * h + 128],
                                       ke[:, 64 * h:64 * h + 64],
                                       ident_bf[:, :])
                return last
            mk.op("pe", tr_k, reads=[bke, bconst], writes=[bpt])
            mk.op("dve", lambda e: e.tensor_copy(kefm[:, :], pt[0:64, :]),
                  reads=[bpt], writes=[bkefm])

        def emit_B(c):
            tc = C0 * c
            par = c % 2
            ocp, bocp = ocp2[par]
            ke, bke = ke2[par]
            qefm, bqefm = qefm2[par]
            kefm, bkefm = kefm2[par]
            vbf, bvbf = vbf2[par]
            scol, bscol = scol2[par]

            def a_mm(e):
                last = None
                for h in range(8):
                    last = e.matmul(pw0[:, 128 * h:128 * h + 128],
                                    kefm[:, 128 * h:128 * h + 128],
                                    qefm[:, 128 * h:128 * h + 128],
                                    start=True, stop=True)
                return last
            mk.op("pe", a_mm, reads=[bqefm, bkefm], writes=[bpw0])
            mk.op("dve", lambda e: e.tensor_tensor(am[:, :], pw0[:, :],
                                                   mask8_f[:, :], ALU.mult),
                  reads=[bpw0, bconst], writes=[bam])

            def o_mm(e):
                last = None
                for h in range(8):
                    last = e.matmul(pw0[:, 128 * h:128 * h + 128],
                                    am[:, 128 * h:128 * h + 128],
                                    vbf[:, 128 * h:128 * h + 128],
                                    start=True, stop=False)
                    last = e.matmul(pw0[:, 128 * h:128 * h + 128],
                                    qefm[:, 128 * h:128 * h + 128],
                                    S0bf[:, 128 * h:128 * h + 128],
                                    start=False, stop=True)
                return last
            mk.op("pe", o_mm, reads=[bam, bvbf, bqefm, bS0bf], writes=[bpw0])

            mk.op("act", lambda e, ocp=ocp: e.copy(ocp[:, :], pw0[:, :]),
                  reads=[bpw0], writes=[bocp])

            def s_mm(e):
                last = None
                for h in range(8):
                    last = e.matmul(pw0[0:64, 128 * h:128 * h + 128],
                                    ke[:, 64 * h:64 * h + 64],
                                    vbf[:, 128 * h:128 * h + 128],
                                    start=True, stop=True)
                return last
            mk.op("pe", s_mm, reads=[bke, bvbf, bocp], writes=[bpw0])
            mk.op("dve", lambda e: e.tensor_add(
                S0[:, :], S0[:, :], pw0[0:64, :]),
                reads=[bpw0, bS0], writes=[bS0])
            for h in range(8):
                mk.op("dve", lambda e, h=h: e.tensor_scalar(
                    S0[:, 128 * h:128 * h + 128], S0[:, 128 * h:128 * h + 128],
                    scol[:, h:h + 1], None, ALU.mult),
                    reads=[bscol, bS0], writes=[bS0])
            mk.op("act", lambda e: e.copy(S0bf[:, :], S0[:, :]),
                  reads=[bS0], writes=[bS0bf])

        NC = MT // C0
        for c in range(NC + 1):
            if c < NC:
                emit_A(c)
            if c == 2:
                if lnmid:
                    lnmid()
                emit_z1(1)
            if c >= 1:
                emit_B(c - 1)
            if c >= 2:
                emit_epi(c - 2)
        emit_epi(NC - 1)

        slo = []
        for hh in range(2):
            sl, bsl = next_slot()
            mk.op("sdma", lambda e, sl=sl, hh=hh: e.dma_start(
                out=sl[:, :],
                in_=scr["wo0"][:, 4096 * hh:4096 * (hh + 1)]),
                reads=[bscr["wo0"], bsl], writes=[bsl])
            slo.append((sl, bsl))
        for tsI in range(2):
            t0 = TS * tsI
            if tsI == 1 and lntail:
                lntail()
            for db in range(8):
                bp = bpb[db % 2]
                pa = pb[db % 2]

                def op_mm(e, db=db, t0=t0, pa=pa):
                    last = None
                    for j in range(8):
                        sl = slo[j // 4][0]
                        last = e.matmul(
                            pa[:, :],
                            sl[:, 1024 * (j % 4) + 128 * db:
                               1024 * (j % 4) + 128 * db + 128],
                            oefm[:, 1024 * j + t0:1024 * j + t0 + TS],
                            start=(j == 0), stop=(j == 7))
                    return last
                mk.op("pe", op_mm, reads=[boefm, slo[0][1], slo[1][1]],
                      writes=[bp])
                mk.op("dve", lambda e, db=db, t0=t0, pa=pa: e.tensor_add(
                    x_fm[:, 1024 * db + t0:1024 * db + t0 + TS],
                    x_fm[:, 1024 * db + t0:1024 * db + t0 + TS], pa[:, :]),
                    reads=[bp, bxh[tsI]], writes=[bxh[tsI]])

    def emit_attn1(ti, lnmid=None, lntail=None):
        zf1, _bzf = arena.alloc("a1_zf1", [64, 1024], BF16)
        zg1, _bzg = arena.alloc("a1_zg1", [64, 1024], BF16)
        sg, bsg = arena.alloc("a1_sgf", [128, 1024], F32)
        sgg, bsgg = arena.alloc("a1_sg", [128, 1024], BF16)
        psil, bpsil = arena.alloc("a1_psil", [128, 1024], F32)
        psq, bpsq = arena.alloc("a1_psq", [128, 1024], BF16)
        gkb, bgkb = arena.alloc("a1_gk", [128, 1024], BF16)
        gk0, bgk0 = arena.alloc("a1_gk0", [64, 1024], BF16)
        gk1, bgk1 = arena.alloc("a1_gk1", [64, 1024], BF16)
        eb, beb = arena.alloc("a1_eb", [128, 1024], F32)
        enb, benb = arena.alloc("a1_enb", [128, 1024], F32)
        qe0, bqe0 = arena.alloc("a1_qe0", [64, 1024], BF16)
        qe1, bqe1 = arena.alloc("a1_qe1", [64, 1024], BF16)
        ke0, bke0 = arena.alloc("a1_ke0", [64, 1024], BF16)
        ke1, bke1 = arena.alloc("a1_ke1", [64, 1024], BF16)
        vbf0, bvbf0 = arena.alloc("a1_vbf0", [64, 1024], BF16)
        vbf1, bvbf1 = arena.alloc("a1_vbf1", [64, 1024], BF16)
        qefm, bqefm = arena.alloc("a1_qefm", [64, 1024], BF16)
        kefm, bkefm = arena.alloc("a1_kefm", [64, 1024], BF16)
        am, bam = gkb[0:64, :], bgkb
        ogs = [(sg[0:64, :], bsg), (enb[0:64, :], benb)]
        osq, bosq = psil[0:64, :], bpsil
        oeps = [arena.alloc(f"a1_oep{a}", [64, 1024], BF16) for a in range(2)]
        rsq, brsq = arena.alloc("a1_rsq", [64, 32], F32)
        scol, bscol = arena.alloc("a1_scol", [64, 32], F32)
        oefm, boefm = arena.alloc("a1_oefm", [128, 8192], BF16)

        if ti == 0:
            mk.op("pool", lambda e: e.memset(S1[:, :], 0.0), writes=[bS1])
            mk.op("pool", lambda e: e.memset(S1bf[:, :], 0.0), writes=[bS1bf])

        wsl = {}
        for nm, key in [("q", "wq1"), ("k", "wk1"), ("v", "wv1")]:
            pair = []
            for hh in range(2):
                sl, bsl = next_slot()
                mk.op("sdma", lambda e, sl=sl, hh=hh, key=key: e.dma_start(
                    out=sl[:, :],
                    in_=scr[key][:, 4096 * hh:4096 * (hh + 1)]),
                    reads=[bscr[key], bsl], writes=[bsl])
                pair.append((sl, bsl))
            wsl[nm] = pair

        def emit_zfg(half):
            def zf_mm(e, half=half):
                last = None
                for j in range(8):
                    last = e.matmul(
                        pb[4][0:64, :], wf1_sb[:, 64 * j:64 * j + 64],
                        xln[:, 1024 * j + 512 * half:1024 * j + 512 * half + 512],
                        start=(j == 0), stop=(j == 7))
                return last
            mk.op("pe", zf_mm, reads=[bxlnh[half], bconst], writes=[bpb[4]])
            mk.op("dve", lambda e, half=half: e.tensor_copy(
                zf1[:, 512 * half:512 * half + 512], pb[4][0:64, :]),
                reads=[bpb[4]], writes=[bzf1])

            def zg_mm(e, half=half):
                last = None
                for j in range(8):
                    last = e.matmul(
                        pb[4][0:64, :], wg1_sb[:, 64 * j:64 * j + 64],
                        xln[:, 1024 * j + 512 * half:1024 * j + 512 * half + 512],
                        start=(j == 0), stop=(j == 7))
                return last
            mk.op("pe", zg_mm, reads=[bxlnh[half], bconst], writes=[bpb[4]])
            mk.op("dve", lambda e, half=half: e.tensor_copy(
                zg1[:, 512 * half:512 * half + 512], pb[4][0:64, :]),
                reads=[bpb[4]], writes=[bzg1])
        emit_zfg(0)

        def emit_tro(tc0):
            for s in range(2):
                oepS, boepS = oeps[s]

                def tr_o(e, oepS=oepS):
                    last = None
                    for j in range(8):
                        last = e.transpose(pt[:, 64 * j:64 * j + 64],
                                           oepS[:, 128 * j:128 * j + 128],
                                           ident_bf[0:64, 0:64])
                    return last
                mk.op("pe", tr_o, reads=[boepS, bconst], writes=[bpt])
                toff = tc0 + 64 * s
                mk.op("act", lambda e, toff=toff: e.copy(
                    oefm.rearrange("p (j t) -> p j t", j=8)
                    [:, :, toff:toff + 64],
                    pt.rearrange("p (j t) -> p j t", j=16)[:, 0:8, :]),
                    reads=[bpt], writes=[boefm])

        pend_tro = []
        stg1 = getattr(build_program, "a1_stage", 99)
        for pr in range(MT // 128):
            tc = 128 * pr
            bxlnT = bxlnh[tc // TS]
            if pr == 2:
                if lnmid:
                    lnmid()
                emit_zfg(1)

            def zf2_mm(e, tc=tc):
                last = None
                for hh in range(2):
                    last = e.matmul(pb[2 + hh][:, :], zf1[:, tc:tc + 128],
                                    wf2_sb[:, 512 * hh:512 * hh + 512],
                                    start=True, stop=True)
                return last
            mk.op("pe", zf2_mm, reads=[bzf1, bconst],
                  writes=[bpb[2], bpb[3]])
            for hh in range(2):
                mk.op("act", lambda e, hh=hh: e.activation(
                    sg[:, 512 * hh:512 * hh + 512], pb[2 + hh][:, :],
                    AF.Sigmoid), reads=[bpb[2 + hh]], writes=[bsg])
            mk.op("act", lambda e: e.activation(gkb[:, :], sg[:, :], AF.Ln),
                  reads=[bsg], writes=[bgkb])
            mk.op("dve", lambda e: e.tensor_copy(gk0[:, :], gkb[0:64, :]),
                  reads=[bgkb], writes=[bgk0])
            mk.op("dve", lambda e: e.tensor_copy(gk1[:, :], gkb[64:128, :]),
                  reads=[bgkb], writes=[bgk1])

            def zg2_mm(e, tc=tc):
                last = None
                for hh in range(2):
                    last = e.matmul(pb[2 + hh][:, :], zg1[:, tc:tc + 128],
                                    wg2_sb[:, 512 * hh:512 * hh + 512],
                                    start=True, stop=True)
                return last
            mk.op("pe", zg2_mm, reads=[bzg1, bconst],
                  writes=[bpb[2], bpb[3]])
            for hh in range(2):
                mk.op("act", lambda e, hh=hh: e.activation(
                    sgg[:, 512 * hh:512 * hh + 512], pb[2 + hh][:, :],
                    AF.Sigmoid), reads=[bpb[2 + hh]], writes=[bsgg])

            def qproj(e, tc=tc, sla=wsl["q"][0][0], slb=wsl["q"][1][0]):
                last = None
                for hh in range(2):
                    for j in range(8):
                        sl = sla if j < 4 else slb
                        last = e.matmul(
                            pw0[:, 512 * hh:512 * hh + 512],
                            xln[:, 1024 * j + tc:1024 * j + tc + 128],
                            sl[:, 1024 * (j % 4) + 512 * hh:
                               1024 * (j % 4) + 512 * hh + 512],
                            start=(j == 0), stop=(j == 7))
                return last
            mk.op("pe", qproj, reads=[bxlnT, wsl["q"][0][1], wsl["q"][1][1]],
                  writes=[bpw0])
            mk.op("act", lambda e: e.activation(psq[:, :], pw0[:, :],
                                                AF.Silu),
                  reads=[bpw0], writes=[bpsq])
            def kproj(e, tc=tc, sla=wsl["k"][0][0], slb=wsl["k"][1][0]):
                last = None
                for hh in range(2):
                    for j in range(8):
                        sl = sla if j < 4 else slb
                        last = e.matmul(
                            pw0[:, 512 * hh:512 * hh + 512],
                            xln[:, 1024 * j + tc:1024 * j + tc + 128],
                            sl[:, 1024 * (j % 4) + 512 * hh:
                               1024 * (j % 4) + 512 * hh + 512],
                            start=(j == 0), stop=(j == 7))
                return last
            mk.op("pe", kproj, reads=[bxlnT, wsl["k"][0][1], wsl["k"][1][1]],
                  writes=[bpw0])
            mk.op("act", lambda e: e.activation(psil[:, :], pw0[:, :],
                                                AF.Silu),
                  reads=[bpw0], writes=[bpsil])
            if pend_tro:
                emit_tro(pend_tro.pop())
            if stg1 < 2:
                continue
            def cs_mm(e):
                last = None
                for hh in range(2):
                    last = e.matmul(pb[2 + hh][:, :], utri64x2_bf[:, :],
                                    gkb[:, 512 * hh:512 * hh + 512],
                                    start=True, stop=True)
                return last
            mk.op("pe", cs_mm, reads=[bgkb, bconst], writes=[bpb[2], bpb[3]])
            for hh in range(2):
                mk.op("act", lambda e, hh=hh: e.activation(
                    eb[:, 512 * hh:512 * hh + 512], pb[2 + hh][:, :], AF.Exp,
                    bias=cvals[:, 4:5]),
                    reads=[bpb[2 + hh]], writes=[beb])
                mk.op("act", lambda e, hh=hh: e.activation(
                    enb[:, 512 * hh:512 * hh + 512], pb[2 + hh][:, :], AF.Exp,
                    scale=-1.0), reads=[bpb[2 + hh]], writes=[benb])

            for s in range(2):
                gks = gk0 if s == 0 else gk1
                bgks = bgk0 if s == 0 else bgk1

                def blc(e, s=s, gks=gks):
                    last = None
                    for h in range(16):
                        last = e.matmul(
                            pb[4][0:64, 16 * s + h:16 * s + h + 1],
                            gks[:, 64 * h:64 * h + 64],
                            onescol_bf[0:64, :],
                            start=True, stop=True)
                    return last
                mk.op("pe", blc, reads=[bgks, bconst], writes=[bpb[4]])
            mk.op("act", lambda e: e.activation(scol[:, :], pb[4][0:64, 0:32],
                                                AF.Exp, bias=cvals[0:64, 4:5]),
                  reads=[bpb[4]], writes=[bscol])

            def vproj(e, tc=tc, sla=wsl["v"][0][0], slb=wsl["v"][1][0]):
                last = None
                for hh in range(2):
                    for j in range(8):
                        sl = sla if j < 4 else slb
                        last = e.matmul(
                            pw0[:, 512 * hh:512 * hh + 512],
                            xln[:, 1024 * j + tc:1024 * j + tc + 128],
                            sl[:, 1024 * (j % 4) + 512 * hh:
                               1024 * (j % 4) + 512 * hh + 512],
                            start=(j == 0), stop=(j == 7))
                return last
            mk.op("pe", vproj, reads=[bxlnT, wsl["v"][0][1], wsl["v"][1][1]],
                  writes=[bpw0])
            mk.op("dve", lambda e: e.tensor_copy(vbf0[:, :], pw0[0:64, :]),
                  reads=[bpw0], writes=[bvbf0])
            mk.op("dve", lambda e: e.tensor_copy(vbf1[:, :], pw0[64:128, :]),
                  reads=[bpw0], writes=[bvbf1])

            mk.op("dve", lambda e: e.scalar_tensor_tensor(
                qe0[:, :], psq[0:64, :], 0.125, eb[0:64, :],
                ALU.mult, ALU.mult),
                reads=[bpsq, beb], writes=[bqe0])
            mk.op("dve", lambda e: e.scalar_tensor_tensor(
                qe1[:, :], psq[64:128, :], 0.125, eb[64:128, :],
                ALU.mult, ALU.mult),
                reads=[bpsq, beb], writes=[bqe1])
            mk.op("dve", lambda e: e.tensor_tensor(
                ke0[:, :], psil[0:64, :], enb[0:64, :], ALU.mult),
                reads=[bpsil, benb], writes=[bke0])
            mk.op("dve", lambda e: e.tensor_tensor(
                ke1[:, :], psil[64:128, :], enb[64:128, :], ALU.mult),
                reads=[bpsil, benb], writes=[bke1])
            if stg1 < 4:
                continue
            for s in range(2):
                qes = qe0 if s == 0 else qe1
                bqes = bqe0 if s == 0 else bqe1
                kes = ke0 if s == 0 else ke1
                bkes = bke0 if s == 0 else bke1
                vbs = vbf0 if s == 0 else vbf1
                bvbs = bvbf0 if s == 0 else bvbf1
                og, bog = ogs[s]

                def tr_q(e, qes=qes):
                    last = None
                    for h in range(16):
                        last = e.transpose(pt[0:64, 64 * h:64 * h + 64],
                                           qes[:, 64 * h:64 * h + 64],
                                           ident_bf[0:64, 0:64])
                    return last
                mk.op("pe", tr_q, reads=[bqes, bconst], writes=[bpt])
                mk.op("dve", lambda e: e.tensor_copy(qefm[:, :], pt[0:64, :]),
                      reads=[bpt], writes=[bqefm])

                def tr_k(e, kes=kes):
                    last = None
                    for h in range(16):
                        last = e.transpose(pt[0:64, 64 * h:64 * h + 64],
                                           kes[:, 64 * h:64 * h + 64],
                                           ident_bf[0:64, 0:64])
                    return last
                mk.op("pe", tr_k, reads=[bkes, bconst], writes=[bpt])
                mk.op("dve", lambda e: e.tensor_copy(kefm[:, :], pt[0:64, :]),
                      reads=[bpt], writes=[bkefm])

                if stg1 < 5:
                    continue

                def a_mm(e):
                    last = None
                    for h in range(16):
                        last = e.matmul(
                            pw0[0:64, 64 * h:64 * h + 64],
                            kefm[:, 64 * h:64 * h + 64],
                            qefm[:, 64 * h:64 * h + 64],
                            start=True, stop=True)
                    return last
                mk.op("pe", a_mm, reads=[bqefm, bkefm], writes=[bpw0])
                mk.op("dve", lambda e: e.tensor_tensor(
                    am[:, :], pw0[0:64, :], mask64_f[:, :], ALU.mult),
                    reads=[bpw0, bconst], writes=[bam])

                if stg1 < 6:
                    continue

                def o_mm(e, vbs=vbs):
                    last = None
                    for h in range(16):
                        last = e.matmul(
                            pw0[0:64, 64 * h:64 * h + 64],
                            am[:, 64 * h:64 * h + 64],
                            vbs[:, 64 * h:64 * h + 64],
                            start=True, stop=False)
                        last = e.matmul(
                            pw0[0:64, 64 * h:64 * h + 64],
                            qefm[:, 64 * h:64 * h + 64],
                            S1bf[:, 64 * h:64 * h + 64],
                            start=False, stop=True)
                    return last
                mk.op("pe", o_mm, reads=[bam, bvbs, bqefm, bS1bf],
                      writes=[bpw0])

                mk.op("dve", lambda e, s=s: e.tensor_tensor(
                    og[:, :], pw0[0:64, :], sgg[64 * s:64 * s + 64, :],
                    ALU.mult), reads=[bpw0, bsgg], writes=[bog],
                    serialize=True)

                if stg1 < 7:
                    continue

                def s_mm(e, kes=kes, vbs=vbs):
                    last = None
                    for h in range(16):
                        last = e.matmul(
                            pw0[0:64, 64 * h:64 * h + 64],
                            kes[:, 64 * h:64 * h + 64],
                            vbs[:, 64 * h:64 * h + 64],
                            start=True, stop=True)
                    return last
                mk.op("pe", s_mm, reads=[bkes, bvbs, bog], writes=[bpw0])
                mk.op("dve", lambda e: e.tensor_add(
                    S1[:, :], S1[:, :], pw0[0:64, :]),
                    reads=[bpw0, bS1], writes=[bS1])
                for h in range(16):
                    mk.op("dve", lambda e, h=h, s=s: e.tensor_scalar(
                        S1[:, 64 * h:64 * h + 64], S1[:, 64 * h:64 * h + 64],
                        scol[:, 16 * s + h:16 * s + h + 1], None, ALU.mult),
                        reads=[bscol, bS1], writes=[bS1])
                mk.op("act", lambda e: e.copy(S1bf[:, :], S1[:, :]),
                      reads=[bS1], writes=[bS1bf])

                if stg1 < 8:
                    continue
                oepS, boepS = oeps[s]
                r0 = 16 * s
                mk.op("act", lambda e, og=og: e.activation(
                    osq[:, :], og[:, :], AF.Square),
                    reads=[bog], writes=[bosq])
                mk.op("dve", lambda e, r0=r0: e.tensor_reduce(
                    rsq[:, r0:r0 + 16],
                    osq.rearrange("p (h c) -> p h c", c=64),
                    axis=AX.X, op=ALU.add), reads=[bosq], writes=[brsq],
                    hazard=True)
                mk.op("act", lambda e, r0=r0: e.activation(
                    rsq[:, r0:r0 + 16], rsq[:, r0:r0 + 16],
                    AF.Ln, scale=1.0 / 64.0,
                    bias=cvals[0:64, 0:1]),
                    reads=[brsq], writes=[brsq])
                mk.op("act", lambda e, r0=r0: e.activation(
                    rsq[:, r0:r0 + 16], rsq[:, r0:r0 + 16],
                    AF.Exp, scale=-0.5,
                    bias=cvals[0:64, 4:5]),
                    reads=[brsq], writes=[brsq])
                for h in range(16):
                    mk.op("dve", lambda e, h=h, og=og, bog=bog, r0=r0,
                          oepS=oepS: e.scalar_tensor_tensor(
                              oepS[:, 64 * h:64 * h + 64],
                              og[:, 64 * h:64 * h + 64],
                              rsq[:, r0 + h:r0 + h + 1],
                              nw1[0:64, 64 * h:64 * h + 64],
                              ALU.mult, ALU.mult),
                          reads=[bog, brsq, bconst], writes=[boepS])

            if stg1 < 8:
                continue
            pend_tro.append(tc)

            tapname = getattr(build_program, "a1_tap", None)
            if pr == getattr(build_program, "a1_tap_pr", 0) and tapname is not None:
                tapmap = {"gk0": (gk0, bgk0, 64, 1024), "eb": (eb, beb, 128, 1024),
                          "enb": (enb, benb, 128, 1024),
                          "qe0": (qe0, bqe0, 64, 1024), "ke0": (ke0, bke0, 64, 1024),
                          "vbf0": (vbf0, bvbf0, 64, 1024),
                          "sgg": (sgg, bsgg, 128, 1024), "scol": (scol, bscol, 64, 32),
                          "S1": (S1, bS1, 64, 1024), "qefm": (qefm, bqefm, 64, 1024),
                          "kefm": (kefm, bkefm, 64, 1024), "oep": (oeps[0][0], oeps[0][1], 64, 1024),
                          "og": (og, bog, 64, 1024)}
                tt, bt_, tp, tw = tapmap[tapname]
                mk.op("gdma", lambda e, tt=tt, tp=tp, tw=tw: e.dma_start(
                    out=scr["a1dbg"][0:tp, 0:tw], in_=tt[0:tp, 0:tw]),
                    reads=[bt_], writes=[bscr["a1dbg"]])

        while pend_tro:
            emit_tro(pend_tro.pop())

        slo = []
        for hh in range(2):
            sl, bsl = next_slot()
            mk.op("sdma", lambda e, sl=sl, hh=hh: e.dma_start(
                out=sl[:, :],
                in_=scr["wo1"][:, 4096 * hh:4096 * (hh + 1)]),
                reads=[bscr["wo1"], bsl], writes=[bsl])
            slo.append((sl, bsl))
        for tsI in range(2):
            t0 = TS * tsI
            if tsI == 1 and lntail:
                lntail()
            for db in range(8):
                bp = bpb[db % 2]
                pa = pb[db % 2]

                def op_mm(e, db=db, t0=t0, pa=pa, slo=slo):
                    last = None
                    for j in range(8):
                        sl = slo[j // 4][0]
                        last = e.matmul(
                            pa[:, :],
                            sl[:, 1024 * (j % 4) + 128 * db:
                               1024 * (j % 4) + 128 * db + 128],
                            oefm[:, 1024 * j + t0:1024 * j + t0 + TS],
                            start=(j == 0), stop=(j == 7))
                    return last
                mk.op("pe", op_mm,
                      reads=[boefm, slo[0][1], slo[1][1]], writes=[bp])
                mk.op("dve", lambda e, db=db, t0=t0, pa=pa: e.tensor_add(
                    x_fm[:, 1024 * db + t0:1024 * db + t0 + TS],
                    x_fm[:, 1024 * db + t0:1024 * db + t0 + TS],
                    pa[:, :]),
                    reads=[bp, bxh[tsI]], writes=[bxh[tsI]])

    def emit_lnonly():
        for j in range(8):
            mk.op("dve", lambda e, j=j: e.tensor_copy(
                x_fm[:, 1024 * j:1024 * (j + 1)],
                xln[:, 1024 * j:1024 * (j + 1)]),
                reads=[bxlnh[0], bxlnh[1]], writes=[bxh[0], bxh[1]])

    PHASE_FN = {"ffn0a": emit_ffn0, "ffn0b": emit_ffn0,
                "ffn1a": emit_ffn1, "ffn1b": emit_ffn1,
                "attn0": emit_attn0, "attn1": emit_attn1,
                "lnonly": emit_lnonly, "noop": (lambda: None)}

    with nc.Block() as block:
        emit_identf()
        emit_consts()
        emit_prep()

        dump = getattr(build_program, "dump_scr", None)
        emit_load_x(0)
        emit_ln(0)
        nph = len(phases)
        for ti in range(nt):
            for ip, ph in enumerate(phases):
                ffnp = ph.startswith("ffn")
                mk.barrier()
                arena.reset()
                last_phase = ip == nph - 1
                lnmid = (lambda ffnp=ffnp: emit_ln(1, fset=ffnp))
                lntail = (None if last_phase
                          else (lambda ffnp=ffnp: emit_ln(0, fset=ffnp)))
                if ph in ("attn0", "attn1"):
                    emit_ln(1, fset=ffnp)
                    PHASE_FN[ph](ti, None, None)
                    if not last_phase:
                        emit_ln(0, fset=ffnp)
                else:
                    PHASE_FN[ph](lnmid, lntail)
                if last_phase:
                    emit_store_x(ti)
                    if ti + 1 < nt:
                        emit_load_x(ti + 1)
                        emit_ln(0, fset=ffnp)
        if dump:
            ddt = F32 if dump in ("dbgprep", "a1dbg") else BF16
            shp = list(scr[dump].shape)
            flat = [int(np.prod(shp[:-1])), shp[-1]]
            ddev = nc.dram_tensor("dump_dev", flat, ddt,
                                  kind="ExternalOutput").ap()
            srcap = scr[dump]
            if len(shp) == 4:
                srcap = srcap.rearrange("a b c d -> (a b c) d")
            mk.op("sdma", lambda e, srcap=srcap: e.dma_start(out=ddev, in_=srcap),
                  reads=[bscr[dump]], writes=[by])
        mk.barrier()

        mk.emit(block)

    return nc, ctx, mk.nops



_cache = {}


def _get_program(key, **kw):
    if key not in _cache:
        _cache[key] = build_program(**kw)
    return _cache[key]


def make_weight_map(inputs):
    w = {}
    for name in ["L0_attn_Wq", "L0_attn_Wk", "L0_attn_Wv", "L0_attn_Wg",
                 "L0_attn_Wgk1", "L0_attn_Wgk2", "L0_attn_Wo",
                 "L0_ffn_W1", "L0_ffn_W2",
                 "L1_attn_Wq", "L1_attn_Wk", "L1_attn_Wv", "L1_attn_Wo",
                 "L1_attn_Wf1", "L1_attn_Wf2", "L1_attn_Wg1", "L1_attn_Wg2"]:
        w[name] = np.ascontiguousarray(np.asarray(inputs[name], np.float32))
    w["L0_attn_bgk2"] = np.asarray(inputs["L0_attn_bgk2"],
                                   np.float32).reshape(1, 512)
    w["L0_attn_norm_w"] = np.asarray(inputs["L0_attn_norm_w"],
                                     np.float32).reshape(1, 128)
    w["L0_ffn_b1"] = np.asarray(inputs["L0_ffn_b1"], np.float32).reshape(1, 4096)
    w["L0_ffn_b2"] = np.asarray(inputs["L0_ffn_b2"], np.float32).reshape(1, 1024)
    w["L1_attn_norm_weight"] = np.asarray(
        inputs["L1_attn_norm_weight"], np.float32).reshape(1, 1024)
    for k in ["L1_ffn_k0", "L1_ffn_k1", "L1_ffn_k2"]:
        w[k] = np.ascontiguousarray(np.asarray(inputs[k])).view(np.uint8)
    for k in ["L1_ffn_s0", "L1_ffn_s1", "L1_ffn_s2"]:
        w[k] = np.asarray(inputs[k], np.float32).reshape(1, 1)
    return w


def kernel(**inputs):
    x = np.asarray(inputs["L0_attn_x"], np.float32)
    x_cores = [np.ascontiguousarray(x[c].T) for c in range(NCORES)]
    nc, _, _ = _get_program("fused", phases=PHASES_ALL, nt=T // MT)
    w = make_weight_map(inputs)
    maps = []
    for c in range(NCORES):
        m = dict(w)
        m["x_dev"] = x_cores[c]
        maps.append(m)
    res = run_bass_kernel_spmd(nc, maps, core_ids=list(range(NCORES)))
    out = np.empty((NCORES, T, D), np.float32)
    for c in range(NCORES):
        out[c] = np.asarray(res.results[c]["y_dev"]).T
    return out


# version 12
# speedup vs baseline: 1.0327x; 1.0327x over previous
import contextlib
import numpy as np

import concourse.bass as bass
import concourse.mybir as mybir
from concourse.bass_utils import run_bass_kernel_spmd

F32 = mybir.dt.float32
BF16 = mybir.dt.bfloat16
U8 = mybir.dt.uint8
AF = mybir.ActivationFunctionType
ALU = mybir.AluOpType
AX = mybir.AxisListType

NCORES = 8
T = 8192
D = 1024
MT = 1024
TS = 512
C0 = 128
C1 = 64
EPS = 1e-5

QUEUES = ("tensor", "scalar", "vector", "gpsimd", "sync")
STREAMS = {
    "pe": ("tensor", 1),
    "act": ("scalar", 1),
    "dve": ("vector", 1),
    "pool": ("gpsimd", 1),
    "gdma": ("gpsimd", 16),
    "sdma": ("sync", 16),
}
DMA_STREAMS = ("gdma", "sdma")


class Buf:
    __slots__ = ("name", "writers", "readers", "hz")

    def __init__(self, name):
        self.name = name
        self.writers = {}
        self.readers = {}
        self.hz = None


class MK:
    def __init__(self, nc, sems):
        self.nc = nc
        self.sems = sems
        self.rec = {q: [] for q in QUEUES}
        self.cnt = {s: 0 for s in STREAMS}
        self.waited = {s: {t: 0 for t in STREAMS} for s in STREAMS}
        self.nops = 0

    SER_DMA = False

    def op(self, stream, build, reads=(), writes=(), serialize=False,
           hazard=False):
        q, inc = STREAMS[stream]
        if self.SER_DMA and stream in DMA_STREAMS:
            serialize = True
        deps = {}
        for b in reads:
            if b.hz and b.hz[0] == stream and b.hz[1] > self.waited[stream][stream]:
                serialize = True
        if serialize and self.cnt[stream] > self.waited[stream][stream]:
            c0 = self.cnt[stream]
            sem0 = self.sems[stream]
            self.rec[q].append(lambda e, sem=sem0, c=c0: e.wait_ge(sem, c))
            self.waited[stream][stream] = c0
        for b in reads:
            for s, c in b.writers.items():
                deps[s] = max(deps.get(s, 0), c)
        for b in writes:
            for s, c in b.writers.items():
                deps[s] = max(deps.get(s, 0), c)
            for s, c in b.readers.items():
                deps[s] = max(deps.get(s, 0), c)
        for s in sorted(deps):
            c = deps[s]
            if s == stream and s not in DMA_STREAMS:
                if c >= self.cnt[stream]:
                    continue
            if self.waited[stream][s] >= c:
                continue
            sem = self.sems[s]
            self.rec[q].append(lambda e, sem=sem, c=c: e.wait_ge(sem, c))
            self.waited[stream][s] = c
        self.cnt[stream] += inc
        cval = self.cnt[stream]
        sem = self.sems[stream]

        def doit(e, build=build, sem=sem, inc=inc):
            build(e).then_inc(sem, inc)

        self.rec[q].append(doit)
        self.nops += 1
        for b in reads:
            b.readers[stream] = max(b.readers.get(stream, 0), cval)
        for b in writes:
            b.writers[stream] = max(b.writers.get(stream, 0), cval)
            if hazard:
                b.hz = (stream, cval)

    def barrier(self):
        snap = dict(self.cnt)
        for q in QUEUES:
            q_streams = [s for s in STREAMS if STREAMS[s][0] == q]
            for s2, c in snap.items():
                if c == 0:
                    continue
                if all(self.waited[qs][s2] >= c for qs in q_streams):
                    continue
                sem = self.sems[s2]
                self.rec[q].append(lambda e, sem=sem, c=c: e.wait_ge(sem, c))
                for qs in q_streams:
                    self.waited[qs][s2] = max(self.waited[qs][s2], c)

    def emit(self, block):
        for q in QUEUES:
            closures = self.rec[q]
            if not closures:
                continue

            def body(eng, closures=closures):
                for f in closures:
                    f(eng)

            getattr(block, q)(body)


class SbufMap:
    def __init__(self, nc, base=16384, limit=228 * 1024):
        self.nc = nc
        self.cur = base
        self.limit = limit

    def alloc(self, name, shape, dtype, align=64):
        nbytes = int(np.prod(shape[1:])) * np.dtype(mybir.dt.np(dtype)).itemsize
        self.cur = (self.cur + align - 1) // align * align
        t = self.nc.alloc_sbuf_tensor_at(name, list(shape), dtype, offset=self.cur)
        self.cur += nbytes
        assert self.cur <= self.limit, f"SBUF overflow at {name}: {self.cur}"
        return t.ap()


class Arena:
    def __init__(self, nc, base, size):
        self.nc = nc
        self.base = base
        self.size = size
        self.cur = 0
        self.cache = {}

    def alloc(self, name, shape, dtype, align=64):
        if name in self.cache:
            return self.cache[name]
        nbytes = int(np.prod(shape[1:])) * np.dtype(mybir.dt.np(dtype)).itemsize
        self.cur = (self.cur + align - 1) // align * align
        assert self.cur + nbytes <= self.size, \
            f"arena overflow at {name}: {self.cur + nbytes}"
        t = self.nc.alloc_sbuf_tensor_at(name, list(shape), dtype,
                                         offset=self.base + self.cur)
        entry = (t.ap(), Buf(name))
        self.cache[name] = entry
        self.cur += nbytes
        return entry

    def reset(self):
        self.cur = 0


PHASES_ALL = ("ffn0a", "attn0", "ffn0b", "ffn1a", "attn1", "ffn1b")

WEIGHT_NAMES = [
    ("L0_attn_Wq", [D, 512], F32), ("L0_attn_Wk", [D, 512], F32),
    ("L0_attn_Wv", [D, D], F32), ("L0_attn_Wg", [D, D], F32),
    ("L0_attn_Wgk1", [D, 16], F32), ("L0_attn_Wgk2", [16, 512], F32),
    ("L0_attn_bgk2", [1, 512], F32), ("L0_attn_norm_w", [1, 128], F32),
    ("L0_attn_Wo", [D, D], F32),
    ("L0_ffn_W1", [4096, D], F32), ("L0_ffn_b1", [1, 4096], F32),
    ("L0_ffn_W2", [D, 4096], F32), ("L0_ffn_b2", [1, D], F32),
    ("L1_attn_Wq", [D, D], F32), ("L1_attn_Wk", [D, D], F32),
    ("L1_attn_Wv", [D, D], F32), ("L1_attn_Wo", [D, D], F32),
    ("L1_attn_Wf1", [D, 64], F32), ("L1_attn_Wf2", [64, D], F32),
    ("L1_attn_Wg1", [D, 64], F32), ("L1_attn_Wg2", [64, D], F32),
    ("L1_attn_norm_weight", [1, D], F32),
    ("L1_ffn_k0", [2048, 4096], U8), ("L1_ffn_k1", [4096, 4096], U8),
    ("L1_ffn_k2", [4096, D], U8),
    ("L1_ffn_s0", [1, 1], F32), ("L1_ffn_s1", [1, 1], F32),
    ("L1_ffn_s2", [1, 1], F32),
]


def build_program(phases=PHASES_ALL, nt=T // MT):
    nc = bass.Bass("TRN2", target_bir_lowering=False)

    x_dev = nc.dram_tensor("x_dev", [D, T], F32, kind="ExternalInput").ap()
    y_dev = nc.dram_tensor("y_dev", [D, T], F32, kind="ExternalOutput").ap()
    wIn = {}
    for name, shape, dt in WEIGHT_NAMES:
        wIn[name] = nc.dram_tensor(name, shape, dt, kind="ExternalInput").ap()

    need = set()
    if "attn0" in phases:
        need |= {"wq0", "wk0", "wv0", "wg0", "wo0"}
    if "ffn0a" in phases or "ffn0b" in phases:
        need |= {"w1t", "w2timg"}
    if "ffn1a" in phases or "ffn1b" in phases:
        need |= {"w0f", "w1bimg", "w2bimg"}
    if "attn1" in phases:
        need |= {"wq1", "wk1", "wv1", "wo1"}
    shapes = {
        "wq0": [128, 4096], "wk0": [128, 4096], "wv0": [128, 8192],
        "wg0": [128, 8192], "wo0": [128, 8192],
        "w1t": [128, 32768], "w2timg": [8, 128, 32, 128],
        "w0f": [128, 32768], "w1bimg": [32, 128, 32, 128],
        "w2bimg": [8, 128, 32, 128],
        "wq1": [128, 8192], "wk1": [128, 8192], "wv1": [128, 8192],
        "wo1": [128, 8192],
    }
    scr = {k: nc.dram_tensor("scr_" + k, shapes[k], BF16, kind="Internal").ap()
           for k in need}
    scr["dbgprep"] = nc.dram_tensor("scr_dbgprep", [128, 16], F32,
                                    kind="Internal").ap()
    shapes["dbgprep"] = [128, 16]
    scr["a1dbg"] = nc.dram_tensor("scr_a1dbg", [128, 1024], F32,
                                  kind="Internal").ap()
    shapes["a1dbg"] = [128, 1024]

    sm = SbufMap(nc)
    x_fm = sm.alloc("x_fm", [128, 8 * MT], F32)
    xln = sm.alloc("xln", [128, 8 * MT], BF16)
    NSLOT = 6
    slots = [sm.alloc(f"slot{i}", [128, 4096], BF16) for i in range(NSLOT)]
    ident_bf = sm.alloc("ident_bf", [128, 128], BF16)
    utri_bf = sm.alloc("utri_bf", [128, 128], BF16)
    utri64x2_bf = sm.alloc("utri64x2_bf", [128, 128], BF16)
    mask8_f = sm.alloc("mask8_f", [128, 1024], BF16)
    mask64_f = sm.alloc("mask64_f", [64, 1024], BF16)
    nw0 = sm.alloc("nw0", [128, 1024], F32)
    nw1 = sm.alloc("nw1", [128, 1024], F32)
    bgk2b = sm.alloc("bgk2b", [128, 512], F32)
    b1c = sm.alloc("b1c", [128, 32], F32)
    b2h = sm.alloc("b2h", [128, 8], F32)
    wgk1_sb = sm.alloc("wgk1_sb", [128, 128], BF16)
    wgk2_sb = sm.alloc("wgk2_sb", [16, 512], BF16)
    wf1_sb = sm.alloc("wf1_sb", [128, 512], BF16)
    wf2_sb = sm.alloc("wf2_sb", [64, 1024], BF16)
    wg1_sb = sm.alloc("wg1_sb", [128, 512], BF16)
    wg2_sb = sm.alloc("wg2_sb", [64, 1024], BF16)
    onescol_bf = sm.alloc("onescol_bf", [128, 1], BF16)
    onescol_f = sm.alloc("onescol_f", [128, 1], F32)
    onesrow_f = sm.alloc("onesrow_f", [1, 128], F32)
    s_cols = sm.alloc("s_cols", [128, 4], F32)
    rowbuf = sm.alloc("rowbuf", [1, 512], F32)
    rowbuf2 = sm.alloc("rowbuf2", [1, 512], F32)
    lnxsq = sm.alloc("lnxsq", [128, 4 * TS], BF16)
    S0 = sm.alloc("S0", [64, 1024], F32)
    S0bf = sm.alloc("S0bf", [64, 1024], BF16)
    S1 = sm.alloc("S1", [64, 1024], F32)
    S1bf = sm.alloc("S1bf", [64, 1024], BF16)
    identf32 = sm.alloc("identf32", [32, 32], F32)
    identf8 = sm.alloc("identf8", [8, 8], F32)
    cvals = sm.alloc("cvals", [128, 6], F32)
    nc.const_aps.register(F32, EPS, cvals[:, 0:1])
    nc.const_aps.register(F32, 1.0, cvals[:, 1:2])
    nc.const_aps.register(F32, 0.5, cvals[:, 2:3])
    nc.const_aps.register(F32, -0.5, cvals[:, 3:4])
    nc.const_aps.register(F32, 0.0, cvals[:, 4:5])

    arena_base = (sm.cur + 63) // 64 * 64
    ARENA_SZ = (229312 - arena_base) // 64 * 64
    assert ARENA_SZ >= 61 * 1024, f"arena too small: {ARENA_SZ}"
    arena = Arena(nc, arena_base, ARENA_SZ)

    pb = [nc.alloc_psum_tensor(f"pb{i}", [128, 512], F32).ap() for i in range(5)]
    pw0 = nc.alloc_psum_tensor("pw0", [128, 1024], F32).ap()
    pt = nc.alloc_psum_tensor("pt", [128, 1024], BF16).ap()
    assert nc.psum_banks_remaining == 0

    bxh = [Buf("x0"), Buf("x1")]
    blnx4 = [Buf(f"lnxsq{a}") for a in range(4)]
    bxlnh = [Buf("xln0"), Buf("xln1")]
    brow = Buf("rows")
    bconst = Buf("const")
    bslot = [Buf(f"slot{i}") for i in range(NSLOT)]
    bpb = [Buf(f"pb{i}") for i in range(5)]
    bpw0, bpt = Buf("pw0"), Buf("pt")
    bS0, bS0bf, bS1, bS1bf = Buf("S0"), Buf("S0bf"), Buf("S1"), Buf("S1bf")
    bz1, bzf1, bzg1 = Buf("z1"), Buf("zf1"), Buf("zg1")
    bscr = {k: Buf("scr_" + k) for k in scr}
    by = Buf("y")

    ctx = contextlib.ExitStack()
    sems = {s: ctx.enter_context(nc.semaphore("sem_" + s)) for s in STREAMS}
    mk = MK(nc, sems)

    slot_rr = {"i": 0}

    def next_slot():
        i = slot_rr["i"] % NSLOT
        slot_rr["i"] += 1
        return slots[i], bslot[i]

    def emit_consts():
        mk.op("pool", lambda e: e.memset(ident_bf[:, :], 0.0), writes=[bconst])
        mk.op("pool", lambda e: e.affine_select(
            out=ident_bf[:, :], in_=ident_bf[:, :], compare_op=ALU.not_equal,
            fill=1.0, base=0, pattern=[[-1, 128]], channel_multiplier=1),
            reads=[bconst], writes=[bconst])
        mk.op("pool", lambda e: e.memset(utri_bf[:, :], 0.0), writes=[bconst])
        mk.op("pool", lambda e: e.affine_select(
            out=utri_bf[:, :], in_=utri_bf[:, :], compare_op=ALU.is_gt,
            fill=1.0, base=0, pattern=[[-1, 128]], channel_multiplier=1),
            reads=[bconst], writes=[bconst])
        mk.op("pool", lambda e: e.memset(utri64x2_bf[:, :], 0.0), writes=[bconst])
        for h in (0, 64):
            mk.op("pool", lambda e, h=h: e.affine_select(
                out=utri64x2_bf[h:h + 64, h:h + 64],
                in_=utri64x2_bf[h:h + 64, h:h + 64], compare_op=ALU.is_gt,
                fill=1.0, base=0, pattern=[[-1, 64]], channel_multiplier=1),
                reads=[bconst], writes=[bconst])
        mk.op("pool", lambda e: e.memset(mask8_f[:, :], 0.0), writes=[bconst])
        for i in range(8):
            mk.op("pool", lambda e, i=i: e.affine_select(
                out=mask8_f[:, 128 * i:128 * i + 128],
                in_=mask8_f[:, 128 * i:128 * i + 128], compare_op=ALU.is_gt,
                fill=1.0, base=0, pattern=[[-1, 128]], channel_multiplier=1),
                reads=[bconst], writes=[bconst])
        mk.op("pool", lambda e: e.memset(mask64_f[:, :], 0.0), writes=[bconst])
        for m in range(16):
            mk.op("pool", lambda e, m=m: e.affine_select(
                out=mask64_f[:, 64 * m:64 * m + 64],
                in_=mask64_f[:, 64 * m:64 * m + 64],
                compare_op=ALU.is_gt, fill=1.0, base=0,
                pattern=[[-1, 64]], channel_multiplier=1),
                reads=[bconst], writes=[bconst])
        mk.op("pool", lambda e: e.memset(onescol_bf[:, :], 1.0), writes=[bconst])
        mk.op("pool", lambda e: e.memset(onescol_f[:, :], 1.0), writes=[bconst])
        mk.op("pool", lambda e: e.memset(onesrow_f[:, :], 1.0), writes=[bconst])

        mk.op("sdma", lambda e: e.dma_start(out=rowbuf[0:1, 0:512],
                                            in_=wIn["L0_attn_bgk2"]),
              writes=[bconst])
        mk.op("pe", lambda e: e.matmul(pb[0][:, :], onesrow_f[:, :],
                                       rowbuf[0:1, 0:512], start=True, stop=True),
              reads=[bconst], writes=[bpb[0]])
        mk.op("dve", lambda e: e.tensor_copy(bgk2b[:, :], pb[0][:, :]),
              reads=[bpb[0]], writes=[bconst])

        mk.op("sdma", lambda e: e.dma_start(out=rowbuf2[0:1, 0:128],
                                            in_=wIn["L0_attn_norm_w"]),
              writes=[bconst])
        mk.op("pe", lambda e: e.matmul(pb[1][:, 0:128], onesrow_f[:, :],
                                       rowbuf2[0:1, 0:128], start=True, stop=True),
              reads=[bconst], writes=[bpb[1]])
        for i in range(8):
            mk.op("dve", lambda e, i=i: e.tensor_copy(
                nw0[:, 128 * i:128 * i + 128], pb[1][:, 0:128]),
                reads=[bpb[1]], writes=[bconst])

        for h in range(2):
            mk.op("sdma", lambda e, h=h: e.dma_start(
                out=rowbuf[0:1, :],
                in_=wIn["L1_attn_norm_weight"][:, 512 * h:512 * h + 512]),
                writes=[bconst])
            mk.op("pe", lambda e, h=h: e.matmul(
                pb[2][:, :], onesrow_f[:, :], rowbuf[0:1, :],
                start=True, stop=True), reads=[bconst], writes=[bpb[2]])
            mk.op("dve", lambda e, h=h: e.tensor_copy(
                nw1[:, 512 * h:512 * h + 512], pb[2][:, :]),
                reads=[bpb[2]], writes=[bconst])

        for i, k in enumerate(["L1_ffn_s0", "L1_ffn_s1", "L1_ffn_s2"]):
            mk.op("sdma", lambda e, i=i, k=k: e.dma_start(
                out=rowbuf[0:1, i:i + 1], in_=wIn[k]), writes=[bconst])
        mk.op("dve", lambda e: e.tensor_scalar(rowbuf[0:1, 0:2], rowbuf[0:1, 0:2],
                                               2.0, None, ALU.mult),
              reads=[bconst], writes=[bconst])
        mk.op("pe", lambda e: e.matmul(pb[3][:, 0:4], onesrow_f[:, :],
                                       rowbuf[0:1, 0:4], start=True, stop=True),
              reads=[bconst], writes=[bpb[3]])
        mk.op("dve", lambda e: e.tensor_copy(s_cols[:, :], pb[3][:, 0:4]),
              reads=[bpb[3]], writes=[bconst])

        b1t, bb1t = arena.alloc("pp_b1t", [32, 128], F32)
        mk.op("sdma", lambda e: e.dma_start(
            out=b1t[:, :],
            in_=wIn["L0_ffn_b1"].rearrange("a (j c) -> (a j) c", c=128)),
            writes=[bb1t])
        mk.op("pe", lambda e: e.transpose(pb[4][:, 0:32], b1t[:, :],
                                          identf32[:, :]),
              reads=[bb1t, bconst], writes=[bpb[4]])
        mk.op("dve", lambda e: e.tensor_copy(b1c[:, :], pb[4][:, 0:32]),
              reads=[bpb[4]], writes=[bconst])
        b2t, bb2t = arena.alloc("pp_b2t", [8, 128], F32)
        mk.op("sdma", lambda e: e.dma_start(
            out=b2t[:, :],
            in_=wIn["L0_ffn_b2"].rearrange("a (j c) -> (a j) c", c=128)),
            writes=[bb2t])
        mk.op("pe", lambda e: e.transpose(pb[4][:, 32:40], b2t[:, :],
                                          identf8[:, :]),
              reads=[bb2t, bconst], writes=[bpb[4]])
        mk.op("dve", lambda e: e.tensor_scalar(b2h[:, :], pb[4][:, 32:40], 0.5,
                                               None, ALU.mult),
              reads=[bpb[4]], writes=[bconst])

        mk.op("gdma", lambda e: e.dma_start(
            out=wgk1_sb.rearrange("p (j c) -> p j c", c=16),
            in_=wIn["L0_attn_Wgk1"].rearrange("(j p) c -> p j c", p=128)),
            writes=[bconst])
        mk.op("gdma", lambda e: e.dma_start(out=wgk2_sb[:, :],
                                            in_=wIn["L0_attn_Wgk2"]),
              writes=[bconst])
        mk.op("gdma", lambda e: e.dma_start(
            out=wf1_sb.rearrange("p (j c) -> p j c", c=64),
            in_=wIn["L1_attn_Wf1"].rearrange("(j p) c -> p j c", p=128)),
            writes=[bconst])
        mk.op("gdma", lambda e: e.dma_start(out=wf2_sb[:, :],
                                            in_=wIn["L1_attn_Wf2"]),
              writes=[bconst])
        mk.op("gdma", lambda e: e.dma_start(
            out=wg1_sb.rearrange("p (j c) -> p j c", c=64),
            in_=wIn["L1_attn_Wg1"].rearrange("(j p) c -> p j c", p=128)),
            writes=[bconst])
        mk.op("gdma", lambda e: e.dma_start(out=wg2_sb[:, :],
                                            in_=wIn["L1_attn_Wg2"]),
              writes=[bconst])

    def emit_identf():
        mk.op("pool", lambda e: e.memset(cvals[:, 0:1], EPS), writes=[bconst])
        mk.op("pool", lambda e: e.memset(cvals[:, 1:2], 1.0), writes=[bconst])
        mk.op("pool", lambda e: e.memset(cvals[:, 2:3], 0.5), writes=[bconst])
        mk.op("pool", lambda e: e.memset(cvals[:, 3:4], -0.5), writes=[bconst])
        mk.op("pool", lambda e: e.memset(cvals[:, 4:5], 0.0), writes=[bconst])
        mk.op("pool", lambda e: e.memset(identf32[:, :], 0.0), writes=[bconst])
        mk.op("pool", lambda e: e.affine_select(
            out=identf32[:, :], in_=identf32[:, :], compare_op=ALU.not_equal,
            fill=1.0, base=0, pattern=[[-1, 32]], channel_multiplier=1),
            reads=[bconst], writes=[bconst])
        mk.op("pool", lambda e: e.memset(identf8[:, :], 0.0), writes=[bconst])
        mk.op("pool", lambda e: e.affine_select(
            out=identf8[:, :], in_=identf8[:, :], compare_op=ALU.not_equal,
            fill=1.0, base=0, pattern=[[-1, 8]], channel_multiplier=1),
            reads=[bconst], writes=[bconst])

    def emit_prep():
        for src, dst, cc in [("L0_attn_Wq", "wq0", 512),
                             ("L0_attn_Wk", "wk0", 512),
                             ("L0_attn_Wv", "wv0", 1024),
                             ("L0_attn_Wg", "wg0", 1024),
                             ("L0_attn_Wo", "wo0", 1024),
                             ("L1_attn_Wq", "wq1", 1024),
                             ("L1_attn_Wk", "wk1", 1024),
                             ("L1_attn_Wv", "wv1", 1024),
                             ("L1_attn_Wo", "wo1", 1024)]:
            if dst in scr:
                mk.op("gdma", lambda e, src=src, dst=dst, cc=cc: e.dma_start(
                    out=scr[dst].rearrange("p (j c) -> p j c", c=cc),
                    in_=wIn[src].rearrange("(j p) c -> p j c", p=128)),
                    writes=[bscr[dst]])
        if "w1t" in scr:
            _prep_quant_ffn0()
            mk.barrier()
            arena.reset()
        if "w0f" in scr:
            _prep_ffn1()

    def _prep_quant_ffn0():
        wt = [arena.alloc("pp_wtile0", [128, 4096], F32),
              arena.alloc("pp_wtile1", [128, 1024], F32)]
        m01, bm01 = arena.alloc("pp_m01", [128, 4096], F32)
        qtile, bqt = arena.alloc("pp_qtile", [128, 4096], BF16)
        stage, bst = arena.alloc("pp_stage", [128, 4096], BF16)
        acc, bacc = arena.alloc("pp_acc", [128, 1], F32)
        tmp1, btmp1 = arena.alloc("pp_tmp1", [128, 1], F32)
        ginv, bgv = arena.alloc("pp_ginv", [128, 1], F32)

        for wname, dstk, ntile, width, nelem in [
                ("L0_ffn_W1", "w1t", 32, 1024, 4096 * 1024),
                ("L0_ffn_W2", "w2timg", 8, 4096, 4096 * 1024)]:
            mk.op("pool", lambda e: e.memset(acc[:, :], 0.0), writes=[bacc])
            for i in range(ntile):
                wtile, bwt = wt[i % 2] if width == 1024 else wt[0]
                mk.op("sdma", lambda e, i=i, wname=wname, width=width,
                      wtile=wtile: e.dma_start(
                    out=wtile[:, 0:width],
                    in_=wIn[wname][128 * i:128 * (i + 1), :]),
                    reads=[bwt], writes=[bwt])
                mk.op("dve", lambda e, width=width, wtile=wtile: e.tensor_reduce(
                    tmp1[:, :], wtile[:, 0:width], axis=AX.X, op=ALU.add,
                    apply_absolute_value=True), reads=[bwt], writes=[btmp1],
                    hazard=True)
                mk.op("dve", lambda e: e.tensor_add(acc[:, :], acc[:, :],
                                                    tmp1[:, :]),
                      reads=[btmp1, bacc], writes=[bacc])
            mk.op("pe", lambda e: e.matmul(pb[4][0:1, 0:1], acc[:, :],
                                           onescol_f[:, :], start=True, stop=True),
                  reads=[bacc, bconst], writes=[bpb[4]])
            mk.op("dve", lambda e, nelem=nelem: e.tensor_scalar(
                rowbuf2[0:1, 8:9], pb[4][0:1, 0:1], 1.0 / nelem, EPS,
                ALU.mult, ALU.add), reads=[bpb[4]], writes=[bconst])
            mk.op("dve", lambda e: e.reciprocal(rowbuf2[0:1, 9:10],
                                                rowbuf2[0:1, 8:9]),
                  reads=[bconst], writes=[bconst], serialize=True, hazard=True)
            mk.op("pe", lambda e: e.matmul(pb[4][:, 0:1], onesrow_f[:, :],
                                           rowbuf2[0:1, 9:10], start=True,
                                           stop=True),
                  reads=[bconst], writes=[bpb[4]])
            mk.op("dve", lambda e: e.tensor_copy(ginv[:, :], pb[4][:, 0:1]),
                  reads=[bpb[4]], writes=[bgv])
            for i in range(ntile):
                wtile, bwt = wt[i % 2] if width == 1024 else wt[0]
                sgn, bsg = wtile, bwt
                mk.op("sdma", lambda e, i=i, wname=wname, width=width,
                      wtile=wtile: e.dma_start(
                    out=wtile[:, 0:width],
                    in_=wIn[wname][128 * i:128 * (i + 1), :]),
                    reads=[bwt], writes=[bwt])
                mk.op("dve", lambda e, width=width, wtile=wtile: e.tensor_scalar(
                    m01[:, 0:width], wtile[:, 0:width], ginv[:, 0:1], None,
                    ALU.mult), reads=[bwt, bgv], writes=[bm01])
                mk.op("act", lambda e, width=width, sgn=sgn: e.sign(
                    sgn[:, 0:width], m01[:, 0:width], bias=0.5),
                    reads=[bm01], writes=[bsg])
                mk.op("act", lambda e, width=width: e.sign(
                    m01[:, 0:width], m01[:, 0:width], bias=-0.5),
                    reads=[bm01], writes=[bm01])
                mk.op("dve", lambda e, width=width, sgn=sgn: e.tensor_add(
                    sgn[:, 0:width], sgn[:, 0:width], m01[:, 0:width]),
                    reads=[bm01, bsg], writes=[bsg])
                mk.op("dve", lambda e, width=width, sgn=sgn: e.tensor_single_scalar(
                    qtile[:, 0:width], sgn[:, 0:width], 0.5, ALU.mult),
                    reads=[bsg], writes=[bqt])
                nblk = width // 128
                for bat in range(0, nblk, 8):
                    nb = min(8, nblk - bat)

                    def trs(e, bat=bat, nb=nb):
                        last = None
                        for j in range(nb):
                            last = e.transpose(
                                pt[:, 128 * j:128 * j + 128],
                                qtile[:, 128 * (bat + j):128 * (bat + j) + 128],
                                ident_bf[:, :])
                        return last
                    mk.op("pe", trs, reads=[bqt, bconst], writes=[bpt])
                    mk.op("act", lambda e, bat=bat, nb=nb: e.copy(
                        stage[:, 128 * bat:128 * (bat + nb)], pt[:, 0:128 * nb]),
                        reads=[bpt], writes=[bst])
                if dstk == "w1t":
                    mk.op("sdma", lambda e, i=i: e.dma_start(
                        out=scr["w1t"].rearrange(
                            "p (fb j c) -> p fb j c", fb=8, j=8)
                        [:, i // 4, :, 128 * (i % 4):128 * (i % 4) + 128],
                        in_=stage[:, 0:1024].rearrange("p (j c) -> p j c", c=128)),
                        reads=[bst], writes=[bscr["w1t"]])
                else:
                    mk.op("sdma", lambda e, i=i: e.dma_start(
                        out=scr["w2timg"][i],
                        in_=stage.rearrange("p (j c) -> p j c", c=128)),
                        reads=[bst], writes=[bscr["w2timg"]])

    def _prep_ffn1():
        ta, bta = arena.alloc("pp_f1a", [128, 4096], BF16)
        tb, btb = arena.alloc("pp_f1b", [128, 4096], BF16)
        for j in range(8):
            mk.op("gdma", lambda e, j=j: e.dma_start(
                out=ta[:, :], in_=wIn["L1_ffn_k0"][128 * j:128 * (j + 1), :]),
                reads=[bta], writes=[bta])
            mk.op("gdma", lambda e, j=j: e.dma_start(
                out=tb[:, :],
                in_=wIn["L1_ffn_k0"][1024 + 128 * j:1024 + 128 * (j + 1), :]),
                reads=[btb], writes=[btb])
            mk.op("dve", lambda e: e.tensor_sub(ta[:, :], ta[:, :], tb[:, :]),
                  reads=[bta, btb], writes=[bta])
            mk.op("sdma", lambda e, j=j: e.dma_start(
                out=scr["w0f"].rearrange("p (fb jj c) -> p fb jj c",
                                         fb=8, jj=8)[:, :, j, :],
                in_=ta.rearrange("p (fb c) -> p fb c", c=512)),
                reads=[bta], writes=[bscr["w0f"]])
        for j in range(32):
            mk.op("gdma", lambda e, j=j: e.dma_start(
                out=ta[:, :], in_=wIn["L1_ffn_k1"][128 * j:128 * (j + 1), :]),
                reads=[bta], writes=[bta])
            mk.op("dve", lambda e: e.tensor_scalar(ta[:, :], ta[:, :], 2.0, -1.0,
                                                   ALU.mult, ALU.add),
                  reads=[bta], writes=[bta])
            mk.op("sdma", lambda e, j=j: e.dma_start(
                out=scr["w1bimg"][:, :, j, :].rearrange("fb p c -> p fb c"),
                in_=ta.rearrange("p (fb c) -> p fb c", c=128)),
                reads=[bta], writes=[bscr["w1bimg"]])
        for j in range(32):
            mk.op("gdma", lambda e, j=j: e.dma_start(
                out=tb[:, 0:1024], in_=wIn["L1_ffn_k2"][128 * j:128 * (j + 1), :]),
                reads=[btb], writes=[btb])
            mk.op("dve", lambda e: e.tensor_scalar(tb[:, 0:1024], tb[:, 0:1024],
                                                   2.0, -1.0, ALU.mult, ALU.add),
                  reads=[btb], writes=[btb])
            mk.op("sdma", lambda e, j=j: e.dma_start(
                out=scr["w2bimg"][:, :, j, :].rearrange("fb p c -> p fb c"),
                in_=tb[:, 0:1024].rearrange("p (fb c) -> p fb c", c=128)),
                reads=[btb], writes=[bscr["w2bimg"]])

    def emit_load_x(ti):
        mk.op("sdma", lambda e, ti=ti: e.dma_start(
            out=x_fm.rearrange("p (j t) -> p j t", j=8),
            in_=x_dev.rearrange("(j p) t -> p j t", p=128)
            [:, :, MT * ti:MT * (ti + 1)]),
            reads=[bxh[0], bxh[1]], writes=[bxh[0], bxh[1]])

    def emit_store_x(ti):
        mk.op("sdma", lambda e, ti=ti: e.dma_start(
            out=y_dev.rearrange("(j p) t -> p j t", p=128)
            [:, :, MT * ti:MT * (ti + 1)],
            in_=x_fm.rearrange("p (j t) -> p j t", j=8)),
            reads=[bxh[0], bxh[1]], writes=[by])

    def emit_ln(tsI, fset=False):
        if True:
            t0 = TS * tsI
            bxT, bxlnT = bxh[tsI], bxlnh[tsI]
            if fset:
                sA, bsA = pw0[:, 0:512], bpw0
                sB, bsB = pw0[:, 512:1024], bpw0
                cA, bcA = pw0[:, 0:512], bpw0
                cB, bcB = pw0[:, 512:1024], bpw0
            else:
                sA, bsA = pb[0][:, :], bpb[0]
                sB, bsB = pb[1][:, :], bpb[1]
                cA, bcA = pb[2][:, :], bpb[2]
                cB, bcB = pb[3][:, :], bpb[3]
            for j in range(8):
                sl = j % 4
                mk.op("act", lambda e, j=j, sl=sl, t0=t0: e.activation(
                    lnxsq[:, TS * sl:TS * sl + TS],
                    x_fm[:, 1024 * j + t0:1024 * j + t0 + TS], AF.Square),
                    reads=[bxT], writes=[blnx4[sl]])
                if j == 3:
                    for jj in range(8):
                        mk.op("pe", lambda e, jj=jj, t0=t0, sA=sA: e.matmul(
                            sA[0:1, :], onescol_f[:, :],
                            x_fm[:, 1024 * jj + t0:1024 * jj + t0 + TS],
                            start=(jj == 0), stop=(jj == 7)),
                            reads=[bxT, bconst], writes=[bsA])
                mk.op("pe", lambda e, j=j, sl=sl, sB=sB: e.matmul(
                    sB[0:1, :], onescol_bf[:, :],
                    lnxsq[:, TS * sl:TS * sl + TS],
                    start=(j == 0), stop=(j == 7)),
                    reads=[blnx4[sl], bconst], writes=[bsB])
            mk.op("dve", lambda e, sA=sA: e.tensor_scalar(
                rowbuf[0:1, 0:TS], sA[0:1, :], 1.0 / D, None, ALU.mult),
                reads=[bsA], writes=[brow])
            mk.op("dve", lambda e: e.tensor_mul(
                rowbuf2[0:1, 0:TS], rowbuf[0:1, 0:TS], rowbuf[0:1, 0:TS]),
                reads=[brow], writes=[brow])
            mk.op("dve", lambda e, sB=sB: e.scalar_tensor_tensor(
                rowbuf2[0:1, 0:TS], sB[0:1, :], 1.0 / D, rowbuf2[0:1, 0:TS],
                ALU.mult, ALU.subtract), reads=[bsB, brow], writes=[brow])
            mk.op("act", lambda e: e.activation(
                rowbuf2[0:1, 0:TS], rowbuf2[0:1, 0:TS], AF.Ln,
                bias=cvals[0:1, 0:1]),
                reads=[brow], writes=[brow])
            mk.op("act", lambda e: e.activation(
                rowbuf2[0:1, 0:TS], rowbuf2[0:1, 0:TS], AF.Exp, scale=-0.5,
                bias=cvals[0:1, 4:5]),
                reads=[brow], writes=[brow])
            mk.op("pe", lambda e, cA=cA: e.matmul(cA[:, :], onesrow_f[:, :],
                                                  rowbuf2[0:1, 0:TS],
                                                  start=True, stop=True),
                  reads=[brow, bconst], writes=[bcA])
            mk.op("pe", lambda e, cB=cB: e.matmul(cB[:, :], onesrow_f[:, :],
                                                  rowbuf[0:1, 0:TS],
                                                  start=True, stop=True),
                  reads=[brow, bconst], writes=[bcB])
            for j in range(8):
                mk.op("dve", lambda e, j=j, t0=t0, cB=cB: e.tensor_tensor(
                    xln[:, 1024 * j + t0:1024 * j + t0 + TS],
                    x_fm[:, 1024 * j + t0:1024 * j + t0 + TS], cB[:, :],
                    ALU.subtract), reads=[bxT, bcB], writes=[bxlnT])
                mk.op("dve", lambda e, j=j, t0=t0, cA=cA: e.tensor_tensor(
                    xln[:, 1024 * j + t0:1024 * j + t0 + TS],
                    xln[:, 1024 * j + t0:1024 * j + t0 + TS], cA[:, :],
                    ALU.mult), reads=[bxlnT, bcA], writes=[bxlnT])

    def emit_ffn0(lnmid=None, lntail=None):
        h, bh = arena.alloc("f0_h", [128, 32 * TS], BF16)
        tmp, btmp = arena.alloc("f0_tmp", [128, TS], F32)
        for tsI in range(MT // TS):
            t0 = TS * tsI
            bxT, bxlnT = bxh[tsI], bxlnh[tsI]
            for fb in range(8):
                sl, bsl = next_slot()
                mk.op("sdma", lambda e, sl=sl, fb=fb: e.dma_start(
                    out=sl[:, :],
                    in_=scr["w1t"][:, 4096 * fb:4096 * (fb + 1)]),
                    reads=[bscr["w1t"], bsl], writes=[bsl])
                for fc in range(4):
                    fg = 4 * fb + fc
                    bp = bpb[fg % 4]
                    pa = pb[fg % 4]

                    def m1(e, sl=sl, fc=fc, pa=pa, t0=t0):
                        last = None
                        for j in range(8):
                            last = e.matmul(
                                pa[:, :],
                                sl[:, 512 * j + 128 * fc:512 * j + 128 * fc + 128],
                                xln[:, 1024 * j + t0:1024 * j + t0 + TS],
                                start=(j == 0), stop=(j == 7))
                        return last
                    mk.op("pe", m1, reads=[bsl, bxlnT], writes=[bp])
                    mk.op("act", lambda e, fg=fg, pa=pa: e.activation(
                        h[:, TS * fg:TS * (fg + 1)], pa[:, :], AF.Relu,
                        bias=b1c[:, fg:fg + 1]),
                        reads=[bp, bconst], writes=[bh])
            if tsI == 0 and lnmid:
                lnmid()
            if tsI == 1 and lntail:
                lntail()
            for db in range(8):
                sl, bsl = next_slot()
                mk.op("sdma", lambda e, sl=sl, db=db: e.dma_start(
                    out=sl[:, :],
                    in_=scr["w2timg"][db].rearrange("p a b -> p (a b)")),
                    reads=[bscr["w2timg"], bsl], writes=[bsl])
                bp = bpb[db % 4]
                pa = pb[db % 4]

                def m2(e, sl=sl, pa=pa):
                    last = None
                    for jj in range(32):
                        last = e.matmul(pa[:, :], sl[:, 128 * jj:128 * jj + 128],
                                        h[:, TS * jj:TS * (jj + 1)],
                                        start=(jj == 0), stop=(jj == 31))
                    return last
                mk.op("pe", m2, reads=[bsl, bh], writes=[bp])
                mk.op("act", lambda e, db=db, pa=pa: e.activation(
                    tmp[:, :], pa[:, :], AF.Identity, scale=0.5,
                    bias=b2h[:, db:db + 1]),
                    reads=[bp, bconst], writes=[btmp])
                mk.op("dve", lambda e, db=db, t0=t0: e.tensor_add(
                    x_fm[:, 1024 * db + t0:1024 * db + t0 + TS],
                    x_fm[:, 1024 * db + t0:1024 * db + t0 + TS], tmp[:, :]),
                    reads=[btmp, bxT], writes=[bxT])

    def emit_ffn1(lnmid=None, lntail=None):
        h1, bh1 = arena.alloc("f1_h1", [128, 32 * TS], BF16)
        h2, bh2 = arena.alloc("f1_h2", [128, 32 * TS], BF16)
        for tsI in range(MT // TS):
            t0 = TS * tsI
            bxT, bxlnT = bxh[tsI], bxlnh[tsI]
            for fb in range(8):
                sl, bsl = next_slot()
                mk.op("sdma", lambda e, sl=sl, fb=fb: e.dma_start(
                    out=sl[:, :],
                    in_=scr["w0f"][:, 4096 * fb:4096 * (fb + 1)]),
                    reads=[bscr["w0f"], bsl], writes=[bsl])
                for fc in range(4):
                    fg = 4 * fb + fc
                    bp = bpb[fg % 4]
                    pa = pb[fg % 4]

                    def m0(e, sl=sl, fc=fc, pa=pa, t0=t0):
                        last = None
                        for j in range(8):
                            last = e.matmul(
                                pa[:, :],
                                sl[:, 512 * j + 128 * fc:512 * j + 128 * fc + 128],
                                xln[:, 1024 * j + t0:1024 * j + t0 + TS],
                                start=(j == 0), stop=(j == 7))
                        return last
                    mk.op("pe", m0, reads=[bsl, bxlnT], writes=[bp])
                    mk.op("dve", lambda e, fg=fg, pa=pa: e.tensor_scalar(
                        h1[:, TS * fg:TS * (fg + 1)], pa[:, :],
                        s_cols[:, 0:1], 0.0, ALU.mult, ALU.max),
                        reads=[bp, bconst], writes=[bh1])
            if tsI == 0 and lnmid:
                lnmid()
            if tsI == 1 and lntail:
                lntail()
            for fb2 in range(32):
                sl, bsl = next_slot()
                mk.op("sdma", lambda e, sl=sl, fb2=fb2: e.dma_start(
                    out=sl[:, :],
                    in_=scr["w1bimg"][fb2].rearrange("p a b -> p (a b)")),
                    reads=[bscr["w1bimg"], bsl], writes=[bsl])
                bp = bpb[fb2 % 4]
                pa = pb[fb2 % 4]

                def m1(e, sl=sl, pa=pa):
                    last = None
                    for jj in range(32):
                        last = e.matmul(pa[:, :], sl[:, 128 * jj:128 * jj + 128],
                                        h1[:, TS * jj:TS * (jj + 1)],
                                        start=(jj == 0), stop=(jj == 31))
                    return last
                mk.op("pe", m1, reads=[bsl, bh1], writes=[bp])
                mk.op("dve", lambda e, fb2=fb2, pa=pa: e.tensor_scalar(
                    h2[:, TS * fb2:TS * (fb2 + 1)], pa[:, :],
                    s_cols[:, 1:2], 0.0, ALU.mult, ALU.max),
                    reads=[bp, bconst], writes=[bh2])
            for db in range(8):
                sl, bsl = next_slot()
                mk.op("sdma", lambda e, sl=sl, db=db: e.dma_start(
                    out=sl[:, :],
                    in_=scr["w2bimg"][db].rearrange("p a b -> p (a b)")),
                    reads=[bscr["w2bimg"], bsl], writes=[bsl])
                bp = bpb[db % 4]
                pa = pb[db % 4]

                def m2(e, sl=sl, pa=pa):
                    last = None
                    for jj in range(32):
                        last = e.matmul(pa[:, :], sl[:, 128 * jj:128 * jj + 128],
                                        h2[:, TS * jj:TS * (jj + 1)],
                                        start=(jj == 0), stop=(jj == 31))
                    return last
                mk.op("pe", m2, reads=[bsl, bh2], writes=[bp])
                mk.op("dve", lambda e, db=db, t0=t0, pa=pa: e.scalar_tensor_tensor(
                    x_fm[:, 1024 * db + t0:1024 * db + t0 + TS], pa[:, :],
                    s_cols[:, 2:3],
                    x_fm[:, 1024 * db + t0:1024 * db + t0 + TS],
                    ALU.mult, ALU.add),
                    reads=[bp, bconst, bxT], writes=[bxT])

    def emit_attn0(ti, lnmid=None, lntail=None):
        z1gk, _bz1 = arena.alloc("a0_z1gk", [16, 1024], BF16)
        u, bu = arena.alloc("a0_u", [128, 512], F32)
        mn, bmn = arena.alloc("a0_mn", [128, 512], F32)
        gkb, bgkb = arena.alloc("a0_gk", [128, 512], BF16)
        eb, beb = arena.alloc("a0_eb", [128, 512], F32)
        enb, benb = arena.alloc("a0_enb", [128, 512], F32)
        qe, bqe = arena.alloc("a0_qe", [128, 512], BF16)
        ke2 = [arena.alloc(f"a0_ke{a}", [128, 512], BF16) for a in range(2)]
        qefm2 = [arena.alloc(f"a0_qefm{a}", [64, 1024], BF16) for a in range(2)]
        kefm2 = [arena.alloc(f"a0_kefm{a}", [64, 1024], BF16) for a in range(2)]
        vbf2 = [arena.alloc(f"a0_vbf{a}", [128, 1024], BF16) for a in range(2)]
        am, bam = arena.alloc("a0_am", [128, 1024], BF16)
        osq, bosq = arena.alloc("a0_osq", [128, 1024], F32)
        oep, boep = arena.alloc("a0_oep", [128, 1024], BF16)
        rsq, brsq = arena.alloc("a0_rsq", [128, 8], F32)
        scol2 = [arena.alloc(f"a0_scol{a}", [64, 8], F32) for a in range(2)]
        oefm, boefm = arena.alloc("a0_oefm", [128, 8192], BF16)

        if ti == 0:
            mk.op("pool", lambda e: e.memset(S0[:, :], 0.0), writes=[bS0])
            mk.op("pool", lambda e: e.memset(S0bf[:, :], 0.0), writes=[bS0bf])

        slq, bslq = next_slot()
        mk.op("sdma", lambda e, sl=slq: e.dma_start(
            out=sl[:, :], in_=scr["wq0"]),
            reads=[bscr["wq0"], bslq], writes=[bslq])
        slk, bslk = next_slot()
        mk.op("sdma", lambda e, sl=slk: e.dma_start(
            out=sl[:, :], in_=scr["wk0"]),
            reads=[bscr["wk0"], bslk], writes=[bslk])
        slv = []
        for hh in range(2):
            sl, bsl = next_slot()
            mk.op("sdma", lambda e, sl=sl, hh=hh: e.dma_start(
                out=sl[:, :],
                in_=scr["wv0"][:, 4096 * hh:4096 * (hh + 1)]),
                reads=[bscr["wv0"], bsl], writes=[bsl])
            slv.append((sl, bsl))
        slg = []
        for hh in range(2):
            sl, bsl = next_slot()
            mk.op("sdma", lambda e, sl=sl, hh=hh: e.dma_start(
                out=sl[:, :],
                in_=scr["wg0"][:, 4096 * hh:4096 * (hh + 1)]),
                reads=[bscr["wg0"], bsl], writes=[bsl])
            slg.append((sl, bsl))

        def emit_z1(half):
            def z1mm(e, half=half):
                last = None
                for j in range(8):
                    last = e.matmul(
                        pb[4][0:16, :], wgk1_sb[:, 16 * j:16 * j + 16],
                        xln[:, 1024 * j + 512 * half:1024 * j + 512 * half + 512],
                        start=(j == 0), stop=(j == 7))
                return last
            mk.op("pe", z1mm, reads=[bxlnh[half], bconst], writes=[bpb[4]])
            mk.op("dve", lambda e, half=half: e.tensor_copy(
                z1gk[:, 512 * half:512 * half + 512], pb[4][0:16, :]),
                reads=[bpb[4]], writes=[bz1])
        emit_z1(0)

        ocp2 = [arena.alloc(f"a0_ocp{a}", [128, 1024], F32) for a in range(2)]
        sgg3 = [arena.alloc(f"a0_sg{a}", [128, 1024], BF16) for a in range(3)]

        def emit_epi(c):
            par = c % 2
            tc = C0 * c
            ocp, bocp = ocp2[par]
            sgg, bsgg = sgg3[c % 3]
            mk.op("act", lambda e, ocp=ocp: e.activation(
                osq[:, :], ocp[:, :], AF.Square),
                reads=[bocp], writes=[bosq])
            mk.op("dve", lambda e: e.tensor_reduce(
                rsq[:, 0:8], osq.rearrange("p (h c) -> p h c", c=128),
                axis=AX.X, op=ALU.add), reads=[bosq], writes=[brsq],
                hazard=True)
            mk.op("act", lambda e: e.activation(rsq[:, 0:8], rsq[:, 0:8],
                                                AF.Ln, scale=1.0 / 128.0,
                                                bias=cvals[:, 0:1]),
                  reads=[brsq], writes=[brsq])
            mk.op("act", lambda e: e.activation(rsq[:, 0:8], rsq[:, 0:8],
                                                AF.Exp, scale=-0.5,
                                                bias=cvals[:, 4:5]),
                  reads=[brsq], writes=[brsq])
            mk.op("dve", lambda e, sgg=sgg: e.tensor_mul(
                sgg[:, :], sgg[:, :], nw0[:, :]),
                reads=[bsgg, bconst], writes=[bsgg])
            for h in range(8):
                mk.op("dve", lambda e, h=h, ocp=ocp, sgg=sgg:
                      e.scalar_tensor_tensor(
                          oep[:, 128 * h:128 * h + 128],
                          ocp[:, 128 * h:128 * h + 128],
                          rsq[:, h:h + 1], sgg[:, 128 * h:128 * h + 128],
                          ALU.mult, ALU.mult),
                      reads=[bocp, brsq, bsgg], writes=[boep])

            def tr_o(e):
                last = None
                for j in range(8):
                    last = e.transpose(pt[:, 128 * j:128 * j + 128],
                                       oep[:, 128 * j:128 * j + 128],
                                       ident_bf[:, :])
                return last
            mk.op("pe", tr_o, reads=[boep, bconst], writes=[bpt])
            mk.op("act", lambda e, tc=tc: e.copy(
                oefm.rearrange("p (j t) -> p j t", j=8)[:, :, tc:tc + 128],
                pt.rearrange("p (j t) -> p j t", j=8)),
                reads=[bpt], writes=[boefm])

        def emit_A(c):
            tc = C0 * c
            par = c % 2
            bxlnT = bxlnh[tc // TS]
            sgg, bsgg = sgg3[c % 3]
            ke, bke = ke2[par]
            qefm, bqefm = qefm2[par]
            kefm, bkefm = kefm2[par]
            vbf, bvbf = vbf2[par]
            scol, bscol = scol2[par]

            mk.op("pe", lambda e, tc=tc: e.matmul(
                pb[4][:, :], z1gk[:, tc:tc + 128], wgk2_sb[:, :],
                start=True, stop=True),
                reads=[bz1, bconst], writes=[bpb[4]])
            mk.op("dve", lambda e: e.tensor_add(u[:, :], pb[4][:, :],
                                                bgk2b[:, :]),
                  reads=[bpb[4], bconst], writes=[bu])
            mk.op("act", lambda e: e.activation(mn[:, :], u[:, :], AF.Sigmoid),
                  reads=[bu], writes=[bmn])
            mk.op("act", lambda e: e.activation(gkb[:, :], mn[:, :], AF.Ln),
                  reads=[bmn], writes=[bgkb])

            def q_mm(e, tc=tc):
                last = None
                for j in range(8):
                    last = e.matmul(pb[0][:, :],
                                    xln[:, 1024 * j + tc:1024 * j + tc + 128],
                                    slq[:, 512 * j:512 * (j + 1)],
                                    start=(j == 0), stop=(j == 7))
                return last
            mk.op("pe", q_mm, reads=[bxlnT, bslq], writes=[bpb[0]])

            def k_mm(e, tc=tc):
                last = None
                for j in range(8):
                    last = e.matmul(pb[1][:, :],
                                    xln[:, 1024 * j + tc:1024 * j + tc + 128],
                                    slk[:, 512 * j:512 * (j + 1)],
                                    start=(j == 0), stop=(j == 7))
                return last
            mk.op("pe", k_mm, reads=[bxlnT, bslk], writes=[bpb[1]])

            def v_mm(e, tc=tc):
                last = None
                for hh in range(2):
                    for j in range(8):
                        sl = slv[j // 4][0]
                        last = e.matmul(
                            pw0[:, 512 * hh:512 * hh + 512],
                            xln[:, 1024 * j + tc:1024 * j + tc + 128],
                            sl[:, 1024 * (j % 4) + 512 * hh:
                               1024 * (j % 4) + 512 * hh + 512],
                            start=(j == 0), stop=(j == 7))
                return last
            mk.op("pe", v_mm, reads=[bxlnT, slv[0][1], slv[1][1]], writes=[bpw0])
            mk.op("act", lambda e: e.copy(vbf[:, :], pw0[:, :]),
                  reads=[bpw0], writes=[bvbf])

            def g_mm(e, hh, tc=tc):
                last = None
                for j in range(8):
                    sl = slg[j // 4][0]
                    last = e.matmul(
                        pb[2 + hh][:, :],
                        xln[:, 1024 * j + tc:1024 * j + tc + 128],
                        sl[:, 1024 * (j % 4) + 512 * hh:
                           1024 * (j % 4) + 512 * hh + 512],
                        start=(j == 0), stop=(j == 7))
                return last
            for hh in range(2):
                mk.op("pe", lambda e, hh=hh, tc=tc: g_mm(e, hh, tc),
                      reads=[bxlnT, slg[0][1], slg[1][1]], writes=[bpb[2 + hh]])
            for hh in range(2):
                mk.op("act", lambda e, hh=hh, sgg=sgg: e.activation(
                    sgg[:, 512 * hh:512 * hh + 512], pb[2 + hh][:, :],
                    AF.Silu), reads=[bpb[2 + hh]], writes=[bsgg])

            mk.op("pe", lambda e: e.matmul(pb[4][:, :], utri_bf[:, :],
                                           gkb[:, :], start=True, stop=True),
                  reads=[bgkb, bconst], writes=[bpb[4]])
            mk.op("act", lambda e: e.activation(eb[:, :], pb[4][:, :], AF.Exp,
                                                scale=1.0 / 16.0,
                                                bias=cvals[:, 4:5]),
                  reads=[bpb[4]], writes=[beb])
            mk.op("act", lambda e: e.activation(enb[:, :], pb[4][:, :], AF.Exp,
                                                scale=-1.0 / 16.0),
                  reads=[bpb[4]], writes=[benb])

            mk.op("dve", lambda e: e.scalar_tensor_tensor(
                qe[:, :], pb[0][:, :], 0.125, eb[:, :], ALU.mult, ALU.mult),
                reads=[bpb[0], beb], writes=[bqe])
            mk.op("dve", lambda e: e.tensor_tensor(
                ke[:, :], pb[1][:, :], enb[:, :], ALU.mult),
                reads=[bpb[1], benb], writes=[bke])

            def blc(e):
                last = None
                for h in range(8):
                    last = e.matmul(pb[1][0:64, h:h + 1],
                                    gkb[:, 64 * h:64 * h + 64],
                                    onescol_bf[:, :], start=True, stop=True)
                return last
            mk.op("pe", blc, reads=[bgkb, bconst], writes=[bpb[1]])
            mk.op("act", lambda e: e.activation(scol[:, :], pb[1][0:64, 0:8],
                                                AF.Exp, scale=1.0 / 16.0,
                                                bias=cvals[0:64, 4:5]),
                  reads=[bpb[1]], writes=[bscol])

            def tr_q(e):
                last = None
                for h in range(8):
                    last = e.transpose(pt[0:64, 128 * h:128 * h + 128],
                                       qe[:, 64 * h:64 * h + 64],
                                       ident_bf[:, :])
                return last
            mk.op("pe", tr_q, reads=[bqe, bconst], writes=[bpt])
            mk.op("dve", lambda e: e.tensor_copy(qefm[:, :], pt[0:64, :]),
                  reads=[bpt], writes=[bqefm])

            def tr_k(e):
                last = None
                for h in range(8):
                    last = e.transpose(pt[0:64, 128 * h:128 * h + 128],
                                       ke[:, 64 * h:64 * h + 64],
                                       ident_bf[:, :])
                return last
            mk.op("pe", tr_k, reads=[bke, bconst], writes=[bpt])
            mk.op("dve", lambda e: e.tensor_copy(kefm[:, :], pt[0:64, :]),
                  reads=[bpt], writes=[bkefm])

        def emit_B(c):
            tc = C0 * c
            par = c % 2
            ocp, bocp = ocp2[par]
            ke, bke = ke2[par]
            qefm, bqefm = qefm2[par]
            kefm, bkefm = kefm2[par]
            vbf, bvbf = vbf2[par]
            scol, bscol = scol2[par]

            def a_mm(e):
                last = None
                for h in range(8):
                    last = e.matmul(pw0[:, 128 * h:128 * h + 128],
                                    kefm[:, 128 * h:128 * h + 128],
                                    qefm[:, 128 * h:128 * h + 128],
                                    start=True, stop=True)
                return last
            mk.op("pe", a_mm, reads=[bqefm, bkefm], writes=[bpw0])
            mk.op("dve", lambda e: e.tensor_tensor(am[:, :], pw0[:, :],
                                                   mask8_f[:, :], ALU.mult),
                  reads=[bpw0, bconst], writes=[bam])

            def o_mm(e):
                last = None
                for h in range(8):
                    last = e.matmul(pw0[:, 128 * h:128 * h + 128],
                                    am[:, 128 * h:128 * h + 128],
                                    vbf[:, 128 * h:128 * h + 128],
                                    start=True, stop=False)
                    last = e.matmul(pw0[:, 128 * h:128 * h + 128],
                                    qefm[:, 128 * h:128 * h + 128],
                                    S0bf[:, 128 * h:128 * h + 128],
                                    start=False, stop=True)
                return last
            mk.op("pe", o_mm, reads=[bam, bvbf, bqefm, bS0bf], writes=[bpw0])

            mk.op("act", lambda e, ocp=ocp: e.copy(ocp[:, :], pw0[:, :]),
                  reads=[bpw0], writes=[bocp])

            def s_mm(e):
                last = None
                for h in range(8):
                    last = e.matmul(pw0[0:64, 128 * h:128 * h + 128],
                                    ke[:, 64 * h:64 * h + 64],
                                    vbf[:, 128 * h:128 * h + 128],
                                    start=True, stop=True)
                return last
            mk.op("pe", s_mm, reads=[bke, bvbf, bocp], writes=[bpw0])
            mk.op("dve", lambda e: e.tensor_add(
                S0[:, :], S0[:, :], pw0[0:64, :]),
                reads=[bpw0, bS0], writes=[bS0])
            for h in range(8):
                mk.op("dve", lambda e, h=h: e.tensor_scalar(
                    S0[:, 128 * h:128 * h + 128], S0[:, 128 * h:128 * h + 128],
                    scol[:, h:h + 1], None, ALU.mult),
                    reads=[bscol, bS0], writes=[bS0])
            mk.op("act", lambda e: e.copy(S0bf[:, :], S0[:, :]),
                  reads=[bS0], writes=[bS0bf])

        NC = MT // C0
        for c in range(NC + 1):
            if c < NC:
                emit_A(c)
            if c == 2:
                if lnmid:
                    lnmid()
                emit_z1(1)
            if c >= 1:
                emit_B(c - 1)
            if c >= 2:
                emit_epi(c - 2)
        emit_epi(NC - 1)

        slo = []
        for hh in range(2):
            sl, bsl = next_slot()
            mk.op("sdma", lambda e, sl=sl, hh=hh: e.dma_start(
                out=sl[:, :],
                in_=scr["wo0"][:, 4096 * hh:4096 * (hh + 1)]),
                reads=[bscr["wo0"], bsl], writes=[bsl])
            slo.append((sl, bsl))
        for tsI in range(2):
            t0 = TS * tsI
            if tsI == 1 and lntail:
                lntail()
            for db in range(8):
                bp = bpb[db % 2]
                pa = pb[db % 2]

                def op_mm(e, db=db, t0=t0, pa=pa):
                    last = None
                    for j in range(8):
                        sl = slo[j // 4][0]
                        last = e.matmul(
                            pa[:, :],
                            sl[:, 1024 * (j % 4) + 128 * db:
                               1024 * (j % 4) + 128 * db + 128],
                            oefm[:, 1024 * j + t0:1024 * j + t0 + TS],
                            start=(j == 0), stop=(j == 7))
                    return last
                mk.op("pe", op_mm, reads=[boefm, slo[0][1], slo[1][1]],
                      writes=[bp])
                mk.op("dve", lambda e, db=db, t0=t0, pa=pa: e.tensor_add(
                    x_fm[:, 1024 * db + t0:1024 * db + t0 + TS],
                    x_fm[:, 1024 * db + t0:1024 * db + t0 + TS], pa[:, :]),
                    reads=[bp, bxh[tsI]], writes=[bxh[tsI]])

    def emit_attn1(ti, lnmid=None, lntail=None):
        zf1, _bzf = arena.alloc("a1_zf1", [64, 1024], BF16)
        zg1, _bzg = arena.alloc("a1_zg1", [64, 1024], BF16)
        sg, bsg = arena.alloc("a1_sgf", [128, 1024], F32)
        sgg, bsgg = arena.alloc("a1_sg", [128, 1024], BF16)
        psil, bpsil = arena.alloc("a1_psil", [128, 1024], F32)
        psq, bpsq = arena.alloc("a1_psq", [128, 1024], BF16)
        gkb, bgkb = arena.alloc("a1_gk", [128, 1024], BF16)
        gk0, bgk0 = arena.alloc("a1_gk0", [64, 1024], BF16)
        gk1, bgk1 = arena.alloc("a1_gk1", [64, 1024], BF16)
        eb, beb = arena.alloc("a1_eb", [128, 1024], F32)
        enb, benb = arena.alloc("a1_enb", [128, 1024], F32)
        qe0, bqe0 = arena.alloc("a1_qe0", [64, 1024], BF16)
        qe1, bqe1 = arena.alloc("a1_qe1", [64, 1024], BF16)
        ke0, bke0 = arena.alloc("a1_ke0", [64, 1024], BF16)
        ke1, bke1 = arena.alloc("a1_ke1", [64, 1024], BF16)
        vbf0, bvbf0 = arena.alloc("a1_vbf0", [64, 1024], BF16)
        vbf1, bvbf1 = arena.alloc("a1_vbf1", [64, 1024], BF16)
        qefm, bqefm = arena.alloc("a1_qefm", [64, 1024], BF16)
        kefm, bkefm = arena.alloc("a1_kefm", [64, 1024], BF16)
        am, bam = gkb[0:64, :], bgkb
        ogs = [(sg[0:64, :], bsg), (enb[0:64, :], benb)]
        osq, bosq = psil[0:64, :], bpsil
        oeps = [arena.alloc(f"a1_oep{a}", [64, 1024], BF16) for a in range(2)]
        rsq, brsq = arena.alloc("a1_rsq", [64, 32], F32)
        scol, bscol = arena.alloc("a1_scol", [64, 32], F32)
        oefm, boefm = arena.alloc("a1_oefm", [128, 8192], BF16)

        if ti == 0:
            mk.op("pool", lambda e: e.memset(S1[:, :], 0.0), writes=[bS1])
            mk.op("pool", lambda e: e.memset(S1bf[:, :], 0.0), writes=[bS1bf])

        wsl = {}
        for nm, key in [("q", "wq1"), ("k", "wk1"), ("v", "wv1")]:
            pair = []
            for hh in range(2):
                sl, bsl = next_slot()
                mk.op("sdma", lambda e, sl=sl, hh=hh, key=key: e.dma_start(
                    out=sl[:, :],
                    in_=scr[key][:, 4096 * hh:4096 * (hh + 1)]),
                    reads=[bscr[key], bsl], writes=[bsl])
                pair.append((sl, bsl))
            wsl[nm] = pair

        def emit_zfg(half):
            def zf_mm(e, half=half):
                last = None
                for j in range(8):
                    last = e.matmul(
                        pb[4][0:64, :], wf1_sb[:, 64 * j:64 * j + 64],
                        xln[:, 1024 * j + 512 * half:1024 * j + 512 * half + 512],
                        start=(j == 0), stop=(j == 7))
                return last
            mk.op("pe", zf_mm, reads=[bxlnh[half], bconst], writes=[bpb[4]])
            mk.op("dve", lambda e, half=half: e.tensor_copy(
                zf1[:, 512 * half:512 * half + 512], pb[4][0:64, :]),
                reads=[bpb[4]], writes=[bzf1])

            def zg_mm(e, half=half):
                last = None
                for j in range(8):
                    last = e.matmul(
                        pb[4][0:64, :], wg1_sb[:, 64 * j:64 * j + 64],
                        xln[:, 1024 * j + 512 * half:1024 * j + 512 * half + 512],
                        start=(j == 0), stop=(j == 7))
                return last
            mk.op("pe", zg_mm, reads=[bxlnh[half], bconst], writes=[bpb[4]])
            mk.op("dve", lambda e, half=half: e.tensor_copy(
                zg1[:, 512 * half:512 * half + 512], pb[4][0:64, :]),
                reads=[bpb[4]], writes=[bzg1])
        emit_zfg(0)

        def emit_tro(tc0):
            for s in range(2):
                oepS, boepS = oeps[s]

                def tr_o(e, oepS=oepS):
                    last = None
                    for j in range(8):
                        last = e.transpose(pt[:, 64 * j:64 * j + 64],
                                           oepS[:, 128 * j:128 * j + 128],
                                           ident_bf[0:64, 0:64])
                    return last
                mk.op("pe", tr_o, reads=[boepS, bconst], writes=[bpt])
                toff = tc0 + 64 * s
                mk.op("act", lambda e, toff=toff: e.copy(
                    oefm.rearrange("p (j t) -> p j t", j=8)
                    [:, :, toff:toff + 64],
                    pt.rearrange("p (j t) -> p j t", j=16)[:, 0:8, :]),
                    reads=[bpt], writes=[boefm])

        pend_tro = []
        stg1 = getattr(build_program, "a1_stage", 99)
        for pr in range(MT // 128):
            tc = 128 * pr
            bxlnT = bxlnh[tc // TS]
            if pr == 2:
                if lnmid:
                    lnmid()
                emit_zfg(1)

            def zf2_mm(e, tc=tc):
                last = None
                for hh in range(2):
                    last = e.matmul(pb[2 + hh][:, :], zf1[:, tc:tc + 128],
                                    wf2_sb[:, 512 * hh:512 * hh + 512],
                                    start=True, stop=True)
                return last
            mk.op("pe", zf2_mm, reads=[bzf1, bconst],
                  writes=[bpb[2], bpb[3]])
            for hh in range(2):
                mk.op("act", lambda e, hh=hh: e.activation(
                    sg[:, 512 * hh:512 * hh + 512], pb[2 + hh][:, :],
                    AF.Sigmoid), reads=[bpb[2 + hh]], writes=[bsg])
            mk.op("act", lambda e: e.activation(gkb[:, :], sg[:, :], AF.Ln),
                  reads=[bsg], writes=[bgkb])
            mk.op("dve", lambda e: e.tensor_copy(gk0[:, :], gkb[0:64, :]),
                  reads=[bgkb], writes=[bgk0])
            mk.op("dve", lambda e: e.tensor_copy(gk1[:, :], gkb[64:128, :]),
                  reads=[bgkb], writes=[bgk1])

            def zg2_mm(e, tc=tc):
                last = None
                for hh in range(2):
                    last = e.matmul(pb[2 + hh][:, :], zg1[:, tc:tc + 128],
                                    wg2_sb[:, 512 * hh:512 * hh + 512],
                                    start=True, stop=True)
                return last
            mk.op("pe", zg2_mm, reads=[bzg1, bconst],
                  writes=[bpb[2], bpb[3]])
            for hh in range(2):
                mk.op("act", lambda e, hh=hh: e.activation(
                    sgg[:, 512 * hh:512 * hh + 512], pb[2 + hh][:, :],
                    AF.Sigmoid), reads=[bpb[2 + hh]], writes=[bsgg])

            def qproj(e, tc=tc, sla=wsl["q"][0][0], slb=wsl["q"][1][0]):
                last = None
                for hh in range(2):
                    for j in range(8):
                        sl = sla if j < 4 else slb
                        last = e.matmul(
                            pw0[:, 512 * hh:512 * hh + 512],
                            xln[:, 1024 * j + tc:1024 * j + tc + 128],
                            sl[:, 1024 * (j % 4) + 512 * hh:
                               1024 * (j % 4) + 512 * hh + 512],
                            start=(j == 0), stop=(j == 7))
                return last
            mk.op("pe", qproj, reads=[bxlnT, wsl["q"][0][1], wsl["q"][1][1]],
                  writes=[bpw0])
            mk.op("act", lambda e: e.activation(psq[:, :], pw0[:, :],
                                                AF.Silu),
                  reads=[bpw0], writes=[bpsq])
            def kproj(e, tc=tc, sla=wsl["k"][0][0], slb=wsl["k"][1][0]):
                last = None
                for hh in range(2):
                    for j in range(8):
                        sl = sla if j < 4 else slb
                        last = e.matmul(
                            pw0[:, 512 * hh:512 * hh + 512],
                            xln[:, 1024 * j + tc:1024 * j + tc + 128],
                            sl[:, 1024 * (j % 4) + 512 * hh:
                               1024 * (j % 4) + 512 * hh + 512],
                            start=(j == 0), stop=(j == 7))
                return last
            mk.op("pe", kproj, reads=[bxlnT, wsl["k"][0][1], wsl["k"][1][1]],
                  writes=[bpw0])
            mk.op("act", lambda e: e.activation(psil[:, :], pw0[:, :],
                                                AF.Silu),
                  reads=[bpw0], writes=[bpsil])
            if pend_tro:
                emit_tro(pend_tro.pop())
            if stg1 < 2:
                continue
            def cs_mm(e):
                last = None
                for hh in range(2):
                    last = e.matmul(pb[2 + hh][:, :], utri64x2_bf[:, :],
                                    gkb[:, 512 * hh:512 * hh + 512],
                                    start=True, stop=True)
                return last
            mk.op("pe", cs_mm, reads=[bgkb, bconst], writes=[bpb[2], bpb[3]])
            for hh in range(2):
                mk.op("act", lambda e, hh=hh: e.activation(
                    eb[:, 512 * hh:512 * hh + 512], pb[2 + hh][:, :], AF.Exp,
                    bias=cvals[:, 4:5]),
                    reads=[bpb[2 + hh]], writes=[beb])
                mk.op("act", lambda e, hh=hh: e.activation(
                    enb[:, 512 * hh:512 * hh + 512], pb[2 + hh][:, :], AF.Exp,
                    scale=-1.0), reads=[bpb[2 + hh]], writes=[benb])

            for s in range(2):
                gks = gk0 if s == 0 else gk1
                bgks = bgk0 if s == 0 else bgk1

                def blc(e, s=s, gks=gks):
                    last = None
                    for h in range(16):
                        last = e.matmul(
                            pb[4][0:64, 16 * s + h:16 * s + h + 1],
                            gks[:, 64 * h:64 * h + 64],
                            onescol_bf[0:64, :],
                            start=True, stop=True)
                    return last
                mk.op("pe", blc, reads=[bgks, bconst], writes=[bpb[4]])
            mk.op("act", lambda e: e.activation(scol[:, :], pb[4][0:64, 0:32],
                                                AF.Exp, bias=cvals[0:64, 4:5]),
                  reads=[bpb[4]], writes=[bscol])

            def vproj(e, tc=tc, sla=wsl["v"][0][0], slb=wsl["v"][1][0]):
                last = None
                for hh in range(2):
                    for j in range(8):
                        sl = sla if j < 4 else slb
                        last = e.matmul(
                            pw0[:, 512 * hh:512 * hh + 512],
                            xln[:, 1024 * j + tc:1024 * j + tc + 128],
                            sl[:, 1024 * (j % 4) + 512 * hh:
                               1024 * (j % 4) + 512 * hh + 512],
                            start=(j == 0), stop=(j == 7))
                return last
            mk.op("pe", vproj, reads=[bxlnT, wsl["v"][0][1], wsl["v"][1][1]],
                  writes=[bpw0])
            mk.op("dve", lambda e: e.tensor_copy(vbf0[:, :], pw0[0:64, :]),
                  reads=[bpw0], writes=[bvbf0])
            mk.op("dve", lambda e: e.tensor_copy(vbf1[:, :], pw0[64:128, :]),
                  reads=[bpw0], writes=[bvbf1])

            mk.op("dve", lambda e: e.scalar_tensor_tensor(
                qe0[:, :], psq[0:64, :], 0.125, eb[0:64, :],
                ALU.mult, ALU.mult),
                reads=[bpsq, beb], writes=[bqe0])
            mk.op("dve", lambda e: e.scalar_tensor_tensor(
                qe1[:, :], psq[64:128, :], 0.125, eb[64:128, :],
                ALU.mult, ALU.mult),
                reads=[bpsq, beb], writes=[bqe1])
            mk.op("dve", lambda e: e.tensor_tensor(
                ke0[:, :], psil[0:64, :], enb[0:64, :], ALU.mult),
                reads=[bpsil, benb], writes=[bke0])
            mk.op("dve", lambda e: e.tensor_tensor(
                ke1[:, :], psil[64:128, :], enb[64:128, :], ALU.mult),
                reads=[bpsil, benb], writes=[bke1])
            if stg1 < 4:
                continue
            for s in range(2):
                qes = qe0 if s == 0 else qe1
                bqes = bqe0 if s == 0 else bqe1
                kes = ke0 if s == 0 else ke1
                bkes = bke0 if s == 0 else bke1
                vbs = vbf0 if s == 0 else vbf1
                bvbs = bvbf0 if s == 0 else bvbf1
                og, bog = ogs[s]

                def tr_q(e, qes=qes):
                    last = None
                    for h in range(16):
                        last = e.transpose(pt[0:64, 64 * h:64 * h + 64],
                                           qes[:, 64 * h:64 * h + 64],
                                           ident_bf[0:64, 0:64])
                    return last
                mk.op("pe", tr_q, reads=[bqes, bconst], writes=[bpt])
                mk.op("dve", lambda e: e.tensor_copy(qefm[:, :], pt[0:64, :]),
                      reads=[bpt], writes=[bqefm])

                def tr_k(e, kes=kes):
                    last = None
                    for h in range(16):
                        last = e.transpose(pt[0:64, 64 * h:64 * h + 64],
                                           kes[:, 64 * h:64 * h + 64],
                                           ident_bf[0:64, 0:64])
                    return last
                mk.op("pe", tr_k, reads=[bkes, bconst], writes=[bpt])
                mk.op("dve", lambda e: e.tensor_copy(kefm[:, :], pt[0:64, :]),
                      reads=[bpt], writes=[bkefm])

                if stg1 < 5:
                    continue

                def a_mm(e):
                    last = None
                    for h in range(16):
                        last = e.matmul(
                            pw0[0:64, 64 * h:64 * h + 64],
                            kefm[:, 64 * h:64 * h + 64],
                            qefm[:, 64 * h:64 * h + 64],
                            start=True, stop=True)
                    return last
                mk.op("pe", a_mm, reads=[bqefm, bkefm], writes=[bpw0])
                mk.op("dve", lambda e: e.tensor_tensor(
                    am[:, :], pw0[0:64, :], mask64_f[:, :], ALU.mult),
                    reads=[bpw0, bconst], writes=[bam])

                if stg1 < 6:
                    continue

                def o_mm(e, vbs=vbs):
                    last = None
                    for h in range(16):
                        last = e.matmul(
                            pw0[0:64, 64 * h:64 * h + 64],
                            am[:, 64 * h:64 * h + 64],
                            vbs[:, 64 * h:64 * h + 64],
                            start=True, stop=False)
                        last = e.matmul(
                            pw0[0:64, 64 * h:64 * h + 64],
                            qefm[:, 64 * h:64 * h + 64],
                            S1bf[:, 64 * h:64 * h + 64],
                            start=False, stop=True)
                    return last
                mk.op("pe", o_mm, reads=[bam, bvbs, bqefm, bS1bf],
                      writes=[bpw0])

                mk.op("dve", lambda e, s=s: e.tensor_tensor(
                    og[:, :], pw0[0:64, :], sgg[64 * s:64 * s + 64, :],
                    ALU.mult), reads=[bpw0, bsgg], writes=[bog],
                    serialize=True)

                if stg1 < 7:
                    continue

                def s_mm(e, kes=kes, vbs=vbs):
                    last = None
                    for h in range(16):
                        last = e.matmul(
                            pw0[0:64, 64 * h:64 * h + 64],
                            kes[:, 64 * h:64 * h + 64],
                            vbs[:, 64 * h:64 * h + 64],
                            start=True, stop=True)
                    return last
                mk.op("pe", s_mm, reads=[bkes, bvbs, bog], writes=[bpw0])
                mk.op("dve", lambda e: e.tensor_add(
                    S1[:, :], S1[:, :], pw0[0:64, :]),
                    reads=[bpw0, bS1], writes=[bS1])
                for h in range(16):
                    mk.op("dve", lambda e, h=h, s=s: e.tensor_scalar(
                        S1[:, 64 * h:64 * h + 64], S1[:, 64 * h:64 * h + 64],
                        scol[:, 16 * s + h:16 * s + h + 1], None, ALU.mult),
                        reads=[bscol, bS1], writes=[bS1])
                mk.op("act", lambda e: e.copy(S1bf[:, :], S1[:, :]),
                      reads=[bS1], writes=[bS1bf])

                if stg1 < 8:
                    continue
                oepS, boepS = oeps[s]
                r0 = 16 * s
                mk.op("act", lambda e, og=og: e.activation(
                    osq[:, :], og[:, :], AF.Square),
                    reads=[bog], writes=[bosq])
                mk.op("dve", lambda e, r0=r0: e.tensor_reduce(
                    rsq[:, r0:r0 + 16],
                    osq.rearrange("p (h c) -> p h c", c=64),
                    axis=AX.X, op=ALU.add), reads=[bosq], writes=[brsq],
                    hazard=True)
                mk.op("act", lambda e, r0=r0: e.activation(
                    rsq[:, r0:r0 + 16], rsq[:, r0:r0 + 16],
                    AF.Ln, scale=1.0 / 64.0,
                    bias=cvals[0:64, 0:1]),
                    reads=[brsq], writes=[brsq])
                mk.op("act", lambda e, r0=r0: e.activation(
                    rsq[:, r0:r0 + 16], rsq[:, r0:r0 + 16],
                    AF.Exp, scale=-0.5,
                    bias=cvals[0:64, 4:5]),
                    reads=[brsq], writes=[brsq])
                for h in range(16):
                    mk.op("dve", lambda e, h=h, og=og, bog=bog, r0=r0,
                          oepS=oepS: e.scalar_tensor_tensor(
                              oepS[:, 64 * h:64 * h + 64],
                              og[:, 64 * h:64 * h + 64],
                              rsq[:, r0 + h:r0 + h + 1],
                              nw1[0:64, 64 * h:64 * h + 64],
                              ALU.mult, ALU.mult),
                          reads=[bog, brsq, bconst], writes=[boepS])

            if stg1 < 8:
                continue
            pend_tro.append(tc)

            tapname = getattr(build_program, "a1_tap", None)
            if pr == getattr(build_program, "a1_tap_pr", 0) and tapname is not None:
                tapmap = {"gk0": (gk0, bgk0, 64, 1024), "eb": (eb, beb, 128, 1024),
                          "enb": (enb, benb, 128, 1024),
                          "qe0": (qe0, bqe0, 64, 1024), "ke0": (ke0, bke0, 64, 1024),
                          "vbf0": (vbf0, bvbf0, 64, 1024),
                          "sgg": (sgg, bsgg, 128, 1024), "scol": (scol, bscol, 64, 32),
                          "S1": (S1, bS1, 64, 1024), "qefm": (qefm, bqefm, 64, 1024),
                          "kefm": (kefm, bkefm, 64, 1024), "oep": (oeps[0][0], oeps[0][1], 64, 1024),
                          "og": (og, bog, 64, 1024)}
                tt, bt_, tp, tw = tapmap[tapname]
                mk.op("gdma", lambda e, tt=tt, tp=tp, tw=tw: e.dma_start(
                    out=scr["a1dbg"][0:tp, 0:tw], in_=tt[0:tp, 0:tw]),
                    reads=[bt_], writes=[bscr["a1dbg"]])

        while pend_tro:
            emit_tro(pend_tro.pop())

        slo = []
        for hh in range(2):
            sl, bsl = next_slot()
            mk.op("sdma", lambda e, sl=sl, hh=hh: e.dma_start(
                out=sl[:, :],
                in_=scr["wo1"][:, 4096 * hh:4096 * (hh + 1)]),
                reads=[bscr["wo1"], bsl], writes=[bsl])
            slo.append((sl, bsl))
        for tsI in range(2):
            t0 = TS * tsI
            if tsI == 1 and lntail:
                lntail()
            for db in range(8):
                bp = bpb[db % 2]
                pa = pb[db % 2]

                def op_mm(e, db=db, t0=t0, pa=pa, slo=slo):
                    last = None
                    for j in range(8):
                        sl = slo[j // 4][0]
                        last = e.matmul(
                            pa[:, :],
                            sl[:, 1024 * (j % 4) + 128 * db:
                               1024 * (j % 4) + 128 * db + 128],
                            oefm[:, 1024 * j + t0:1024 * j + t0 + TS],
                            start=(j == 0), stop=(j == 7))
                    return last
                mk.op("pe", op_mm,
                      reads=[boefm, slo[0][1], slo[1][1]], writes=[bp])
                mk.op("dve", lambda e, db=db, t0=t0, pa=pa: e.tensor_add(
                    x_fm[:, 1024 * db + t0:1024 * db + t0 + TS],
                    x_fm[:, 1024 * db + t0:1024 * db + t0 + TS],
                    pa[:, :]),
                    reads=[bp, bxh[tsI]], writes=[bxh[tsI]])

    def emit_lnonly():
        for j in range(8):
            mk.op("dve", lambda e, j=j: e.tensor_copy(
                x_fm[:, 1024 * j:1024 * (j + 1)],
                xln[:, 1024 * j:1024 * (j + 1)]),
                reads=[bxlnh[0], bxlnh[1]], writes=[bxh[0], bxh[1]])

    PHASE_FN = {"ffn0a": emit_ffn0, "ffn0b": emit_ffn0,
                "ffn1a": emit_ffn1, "ffn1b": emit_ffn1,
                "attn0": emit_attn0, "attn1": emit_attn1,
                "lnonly": emit_lnonly, "noop": (lambda: None)}

    with nc.Block() as block:
        emit_identf()
        emit_consts()
        emit_prep()

        dump = getattr(build_program, "dump_scr", None)
        emit_load_x(0)
        emit_ln(0)
        nph = len(phases)
        for ti in range(nt):
            for ip, ph in enumerate(phases):
                ffnp = ph.startswith("ffn")
                mk.barrier()
                arena.reset()
                last_phase = ip == nph - 1
                lnmid = (lambda ffnp=ffnp: emit_ln(1, fset=ffnp))
                lntail = (None if last_phase
                          else (lambda ffnp=ffnp: emit_ln(0, fset=ffnp)))
                if ph in ("attn0", "attn1"):
                    emit_ln(1, fset=ffnp)
                    PHASE_FN[ph](ti, None, None)
                    if not last_phase:
                        emit_ln(0, fset=ffnp)
                else:
                    PHASE_FN[ph](lnmid, lntail)
                if last_phase:
                    emit_store_x(ti)
                    if ti + 1 < nt:
                        emit_load_x(ti + 1)
                        emit_ln(0, fset=ffnp)
        if dump:
            ddt = F32 if dump in ("dbgprep", "a1dbg") else BF16
            shp = list(scr[dump].shape)
            flat = [int(np.prod(shp[:-1])), shp[-1]]
            ddev = nc.dram_tensor("dump_dev", flat, ddt,
                                  kind="ExternalOutput").ap()
            srcap = scr[dump]
            if len(shp) == 4:
                srcap = srcap.rearrange("a b c d -> (a b c) d")
            mk.op("sdma", lambda e, srcap=srcap: e.dma_start(out=ddev, in_=srcap),
                  reads=[bscr[dump]], writes=[by])
        mk.barrier()

        mk.emit(block)

    return nc, ctx, mk.nops



_cache = {}


def _get_program(key, **kw):
    if key not in _cache:
        _cache[key] = build_program(**kw)
    return _cache[key]


def make_weight_map(inputs):
    w = {}
    for name in ["L0_attn_Wq", "L0_attn_Wk", "L0_attn_Wv", "L0_attn_Wg",
                 "L0_attn_Wgk1", "L0_attn_Wgk2", "L0_attn_Wo",
                 "L0_ffn_W1", "L0_ffn_W2",
                 "L1_attn_Wq", "L1_attn_Wk", "L1_attn_Wv", "L1_attn_Wo",
                 "L1_attn_Wf1", "L1_attn_Wf2", "L1_attn_Wg1", "L1_attn_Wg2"]:
        w[name] = np.ascontiguousarray(np.asarray(inputs[name], np.float32))
    w["L0_attn_bgk2"] = np.asarray(inputs["L0_attn_bgk2"],
                                   np.float32).reshape(1, 512)
    w["L0_attn_norm_w"] = np.asarray(inputs["L0_attn_norm_w"],
                                     np.float32).reshape(1, 128)
    w["L0_ffn_b1"] = np.asarray(inputs["L0_ffn_b1"], np.float32).reshape(1, 4096)
    w["L0_ffn_b2"] = np.asarray(inputs["L0_ffn_b2"], np.float32).reshape(1, 1024)
    w["L1_attn_norm_weight"] = np.asarray(
        inputs["L1_attn_norm_weight"], np.float32).reshape(1, 1024)
    for k in ["L1_ffn_k0", "L1_ffn_k1", "L1_ffn_k2"]:
        w[k] = np.ascontiguousarray(np.asarray(inputs[k])).view(np.uint8)
    for k in ["L1_ffn_s0", "L1_ffn_s1", "L1_ffn_s2"]:
        w[k] = np.asarray(inputs[k], np.float32).reshape(1, 1)
    return w


def kernel(**inputs):
    x = np.asarray(inputs["L0_attn_x"], np.float32)
    x_cores = [np.ascontiguousarray(x[c].T) for c in range(NCORES)]
    nc, _, _ = _get_program("fused", phases=PHASES_ALL, nt=T // MT)
    w = make_weight_map(inputs)
    maps = []
    for c in range(NCORES):
        m = dict(w)
        m["x_dev"] = x_cores[c]
        maps.append(m)
    res = run_bass_kernel_spmd(nc, maps, core_ids=list(range(NCORES)))
    out = np.empty((NCORES, T, D), np.float32)
    for c in range(NCORES):
        out[c] = np.asarray(res.results[c]["y_dev"]).T
    return out


# version 13
# speedup vs baseline: 1.0366x; 1.0037x over previous
import contextlib
import numpy as np

import concourse.bass as bass
import concourse.mybir as mybir
from concourse.bass_utils import run_bass_kernel_spmd

F32 = mybir.dt.float32
BF16 = mybir.dt.bfloat16
U8 = mybir.dt.uint8
AF = mybir.ActivationFunctionType
ALU = mybir.AluOpType
AX = mybir.AxisListType

NCORES = 8
T = 8192
D = 1024
MT = 1024
TS = 512
C0 = 128
C1 = 64
EPS = 1e-5

QUEUES = ("tensor", "scalar", "vector", "gpsimd", "sync")
STREAMS = {
    "pe": ("tensor", 1),
    "act": ("scalar", 1),
    "dve": ("vector", 1),
    "pool": ("gpsimd", 1),
    "gdma": ("gpsimd", 16),
    "sdma": ("sync", 16),
}
DMA_STREAMS = ("gdma", "sdma")


class Buf:
    __slots__ = ("name", "writers", "readers", "hz")

    def __init__(self, name):
        self.name = name
        self.writers = {}
        self.readers = {}
        self.hz = None


class MK:
    def __init__(self, nc, sems):
        self.nc = nc
        self.sems = sems
        self.rec = {q: [] for q in QUEUES}
        self.cnt = {s: 0 for s in STREAMS}
        self.waited = {s: {t: 0 for t in STREAMS} for s in STREAMS}
        self.nops = 0

    SER_DMA = False

    def op(self, stream, build, reads=(), writes=(), serialize=False,
           hazard=False):
        q, inc = STREAMS[stream]
        if self.SER_DMA and stream in DMA_STREAMS:
            serialize = True
        deps = {}
        for b in reads:
            if b.hz and b.hz[0] == stream and b.hz[1] > self.waited[stream][stream]:
                serialize = True
        if serialize and self.cnt[stream] > self.waited[stream][stream]:
            c0 = self.cnt[stream]
            sem0 = self.sems[stream]
            self.rec[q].append(lambda e, sem=sem0, c=c0: e.wait_ge(sem, c))
            self.waited[stream][stream] = c0
        for b in reads:
            for s, c in b.writers.items():
                deps[s] = max(deps.get(s, 0), c)
        for b in writes:
            for s, c in b.writers.items():
                deps[s] = max(deps.get(s, 0), c)
            for s, c in b.readers.items():
                deps[s] = max(deps.get(s, 0), c)
        for s in sorted(deps):
            c = deps[s]
            if s == stream and s not in DMA_STREAMS:
                continue
            if self.waited[stream][s] >= c:
                continue
            sem = self.sems[s]
            self.rec[q].append(lambda e, sem=sem, c=c: e.wait_ge(sem, c))
            self.waited[stream][s] = c
        self.cnt[stream] += inc
        cval = self.cnt[stream]
        sem = self.sems[stream]

        def doit(e, build=build, sem=sem, inc=inc):
            build(e).then_inc(sem, inc)

        self.rec[q].append(doit)
        self.nops += 1
        for b in reads:
            b.readers[stream] = max(b.readers.get(stream, 0), cval)
        for b in writes:
            b.writers[stream] = max(b.writers.get(stream, 0), cval)
            if hazard:
                b.hz = (stream, cval)

    def barrier(self):
        snap = dict(self.cnt)
        for q in QUEUES:
            q_streams = [s for s in STREAMS if STREAMS[s][0] == q]
            for s2, c in snap.items():
                if c == 0:
                    continue
                if all(self.waited[qs][s2] >= c for qs in q_streams):
                    continue
                sem = self.sems[s2]
                self.rec[q].append(lambda e, sem=sem, c=c: e.wait_ge(sem, c))
                for qs in q_streams:
                    self.waited[qs][s2] = max(self.waited[qs][s2], c)

    def emit(self, block):
        for q in QUEUES:
            closures = self.rec[q]
            if not closures:
                continue

            def body(eng, closures=closures):
                for f in closures:
                    f(eng)

            getattr(block, q)(body)


class SbufMap:
    def __init__(self, nc, base=16384, limit=228 * 1024):
        self.nc = nc
        self.cur = base
        self.limit = limit

    def alloc(self, name, shape, dtype, align=64):
        nbytes = int(np.prod(shape[1:])) * np.dtype(mybir.dt.np(dtype)).itemsize
        self.cur = (self.cur + align - 1) // align * align
        t = self.nc.alloc_sbuf_tensor_at(name, list(shape), dtype, offset=self.cur)
        self.cur += nbytes
        assert self.cur <= self.limit, f"SBUF overflow at {name}: {self.cur}"
        return t.ap()


class Arena:
    def __init__(self, nc, base, size):
        self.nc = nc
        self.base = base
        self.size = size
        self.cur = 0
        self.cache = {}

    def alloc(self, name, shape, dtype, align=64):
        if name in self.cache:
            return self.cache[name]
        nbytes = int(np.prod(shape[1:])) * np.dtype(mybir.dt.np(dtype)).itemsize
        self.cur = (self.cur + align - 1) // align * align
        assert self.cur + nbytes <= self.size, \
            f"arena overflow at {name}: {self.cur + nbytes}"
        t = self.nc.alloc_sbuf_tensor_at(name, list(shape), dtype,
                                         offset=self.base + self.cur)
        entry = (t.ap(), Buf(name))
        self.cache[name] = entry
        self.cur += nbytes
        return entry

    def reset(self):
        self.cur = 0


PHASES_ALL = ("ffn0a", "attn0", "ffn0b", "ffn1a", "attn1", "ffn1b")

WEIGHT_NAMES = [
    ("L0_attn_Wq", [D, 512], F32), ("L0_attn_Wk", [D, 512], F32),
    ("L0_attn_Wv", [D, D], F32), ("L0_attn_Wg", [D, D], F32),
    ("L0_attn_Wgk1", [D, 16], F32), ("L0_attn_Wgk2", [16, 512], F32),
    ("L0_attn_bgk2", [1, 512], F32), ("L0_attn_norm_w", [1, 128], F32),
    ("L0_attn_Wo", [D, D], F32),
    ("L0_ffn_W1", [4096, D], F32), ("L0_ffn_b1", [1, 4096], F32),
    ("L0_ffn_W2", [D, 4096], F32), ("L0_ffn_b2", [1, D], F32),
    ("L1_attn_Wq", [D, D], F32), ("L1_attn_Wk", [D, D], F32),
    ("L1_attn_Wv", [D, D], F32), ("L1_attn_Wo", [D, D], F32),
    ("L1_attn_Wf1", [D, 64], F32), ("L1_attn_Wf2", [64, D], F32),
    ("L1_attn_Wg1", [D, 64], F32), ("L1_attn_Wg2", [64, D], F32),
    ("L1_attn_norm_weight", [1, D], F32),
    ("L1_ffn_k0", [2048, 4096], U8), ("L1_ffn_k1", [4096, 4096], U8),
    ("L1_ffn_k2", [4096, D], U8),
    ("L1_ffn_s0", [1, 1], F32), ("L1_ffn_s1", [1, 1], F32),
    ("L1_ffn_s2", [1, 1], F32),
]


def build_program(phases=PHASES_ALL, nt=T // MT):
    nc = bass.Bass("TRN2", target_bir_lowering=False)

    x_dev = nc.dram_tensor("x_dev", [D, T], F32, kind="ExternalInput").ap()
    y_dev = nc.dram_tensor("y_dev", [D, T], F32, kind="ExternalOutput").ap()
    wIn = {}
    for name, shape, dt in WEIGHT_NAMES:
        wIn[name] = nc.dram_tensor(name, shape, dt, kind="ExternalInput").ap()

    need = set()
    if "attn0" in phases:
        need |= {"wq0", "wk0", "wv0", "wg0", "wo0"}
    if "ffn0a" in phases or "ffn0b" in phases:
        need |= {"w1t", "w2timg"}
    if "ffn1a" in phases or "ffn1b" in phases:
        need |= {"w0f", "w1bimg", "w2bimg"}
    if "attn1" in phases:
        need |= {"wq1", "wk1", "wv1", "wo1"}
    shapes = {
        "wq0": [128, 4096], "wk0": [128, 4096], "wv0": [128, 8192],
        "wg0": [128, 8192], "wo0": [128, 8192],
        "w1t": [128, 32768], "w2timg": [8, 128, 32, 128],
        "w0f": [128, 32768], "w1bimg": [32, 128, 32, 128],
        "w2bimg": [8, 128, 32, 128],
        "wq1": [128, 8192], "wk1": [128, 8192], "wv1": [128, 8192],
        "wo1": [128, 8192],
    }
    scr = {k: nc.dram_tensor("scr_" + k, shapes[k], BF16, kind="Internal").ap()
           for k in need}
    scr["dbgprep"] = nc.dram_tensor("scr_dbgprep", [128, 16], F32,
                                    kind="Internal").ap()
    shapes["dbgprep"] = [128, 16]
    scr["a1dbg"] = nc.dram_tensor("scr_a1dbg", [128, 1024], F32,
                                  kind="Internal").ap()
    shapes["a1dbg"] = [128, 1024]

    sm = SbufMap(nc)
    x_fm = sm.alloc("x_fm", [128, 8 * MT], F32)
    xln = sm.alloc("xln", [128, 8 * MT], BF16)
    NSLOT = 6
    slots = [sm.alloc(f"slot{i}", [128, 4096], BF16) for i in range(NSLOT)]
    ident_bf = sm.alloc("ident_bf", [128, 128], BF16)
    utri_bf = sm.alloc("utri_bf", [128, 128], BF16)
    utri64x2_bf = sm.alloc("utri64x2_bf", [128, 128], BF16)
    mask8_f = sm.alloc("mask8_f", [128, 1024], BF16)
    mask64_f = sm.alloc("mask64_f", [64, 1024], BF16)
    nw0 = sm.alloc("nw0", [128, 1024], F32)
    nw1 = sm.alloc("nw1", [128, 1024], F32)
    bgk2b = sm.alloc("bgk2b", [128, 512], F32)
    b1c = sm.alloc("b1c", [128, 32], F32)
    b2h = sm.alloc("b2h", [128, 8], F32)
    wgk1_sb = sm.alloc("wgk1_sb", [128, 128], BF16)
    wgk2_sb = sm.alloc("wgk2_sb", [16, 512], BF16)
    wf1_sb = sm.alloc("wf1_sb", [128, 512], BF16)
    wf2_sb = sm.alloc("wf2_sb", [64, 1024], BF16)
    wg1_sb = sm.alloc("wg1_sb", [128, 512], BF16)
    wg2_sb = sm.alloc("wg2_sb", [64, 1024], BF16)
    onescol_bf = sm.alloc("onescol_bf", [128, 1], BF16)
    onescol_f = sm.alloc("onescol_f", [128, 1], F32)
    onesrow_f = sm.alloc("onesrow_f", [1, 128], F32)
    s_cols = sm.alloc("s_cols", [128, 4], F32)
    rowbuf = sm.alloc("rowbuf", [1, 512], F32)
    rowbuf2 = sm.alloc("rowbuf2", [1, 512], F32)
    lnxsq = sm.alloc("lnxsq", [128, 4 * TS], BF16)
    S0 = sm.alloc("S0", [64, 1024], F32)
    S0bf = sm.alloc("S0bf", [64, 1024], BF16)
    S1 = sm.alloc("S1", [64, 1024], F32)
    S1bf = sm.alloc("S1bf", [64, 1024], BF16)
    identf32 = sm.alloc("identf32", [32, 32], F32)
    identf8 = sm.alloc("identf8", [8, 8], F32)
    cvals = sm.alloc("cvals", [128, 6], F32)
    nc.const_aps.register(F32, EPS, cvals[:, 0:1])
    nc.const_aps.register(F32, 1.0, cvals[:, 1:2])
    nc.const_aps.register(F32, 0.5, cvals[:, 2:3])
    nc.const_aps.register(F32, -0.5, cvals[:, 3:4])
    nc.const_aps.register(F32, 0.0, cvals[:, 4:5])

    arena_base = (sm.cur + 63) // 64 * 64
    ARENA_SZ = (229312 - arena_base) // 64 * 64
    assert ARENA_SZ >= 61 * 1024, f"arena too small: {ARENA_SZ}"
    arena = Arena(nc, arena_base, ARENA_SZ)

    pb = [nc.alloc_psum_tensor(f"pb{i}", [128, 512], F32).ap() for i in range(5)]
    pw0 = nc.alloc_psum_tensor("pw0", [128, 1024], F32).ap()
    pt = nc.alloc_psum_tensor("pt", [128, 1024], BF16).ap()
    assert nc.psum_banks_remaining == 0

    bxh = [Buf("x0"), Buf("x1")]
    blnx4 = [Buf(f"lnxsq{a}") for a in range(4)]
    bxlnh = [Buf("xln0"), Buf("xln1")]
    brow = Buf("rows")
    bconst = Buf("const")
    bslot = [Buf(f"slot{i}") for i in range(NSLOT)]
    bpb = [Buf(f"pb{i}") for i in range(5)]
    bpw0, bpt = Buf("pw0"), Buf("pt")
    bS0, bS0bf, bS1, bS1bf = Buf("S0"), Buf("S0bf"), Buf("S1"), Buf("S1bf")
    bz1, bzf1, bzg1 = Buf("z1"), Buf("zf1"), Buf("zg1")
    bscr = {k: Buf("scr_" + k) for k in scr}
    by = Buf("y")

    ctx = contextlib.ExitStack()
    sems = {s: ctx.enter_context(nc.semaphore("sem_" + s)) for s in STREAMS}
    mk = MK(nc, sems)

    slot_rr = {"i": 0}

    def next_slot():
        i = slot_rr["i"] % NSLOT
        slot_rr["i"] += 1
        return slots[i], bslot[i]

    def emit_consts():
        mk.op("pool", lambda e: e.memset(ident_bf[:, :], 0.0), writes=[bconst])
        mk.op("pool", lambda e: e.affine_select(
            out=ident_bf[:, :], in_=ident_bf[:, :], compare_op=ALU.not_equal,
            fill=1.0, base=0, pattern=[[-1, 128]], channel_multiplier=1),
            reads=[bconst], writes=[bconst])
        mk.op("pool", lambda e: e.memset(utri_bf[:, :], 0.0), writes=[bconst])
        mk.op("pool", lambda e: e.affine_select(
            out=utri_bf[:, :], in_=utri_bf[:, :], compare_op=ALU.is_gt,
            fill=1.0, base=0, pattern=[[-1, 128]], channel_multiplier=1),
            reads=[bconst], writes=[bconst])
        mk.op("pool", lambda e: e.memset(utri64x2_bf[:, :], 0.0), writes=[bconst])
        for h in (0, 64):
            mk.op("pool", lambda e, h=h: e.affine_select(
                out=utri64x2_bf[h:h + 64, h:h + 64],
                in_=utri64x2_bf[h:h + 64, h:h + 64], compare_op=ALU.is_gt,
                fill=1.0, base=0, pattern=[[-1, 64]], channel_multiplier=1),
                reads=[bconst], writes=[bconst])
        mk.op("pool", lambda e: e.memset(mask8_f[:, :], 0.0), writes=[bconst])
        for i in range(8):
            mk.op("pool", lambda e, i=i: e.affine_select(
                out=mask8_f[:, 128 * i:128 * i + 128],
                in_=mask8_f[:, 128 * i:128 * i + 128], compare_op=ALU.is_gt,
                fill=1.0, base=0, pattern=[[-1, 128]], channel_multiplier=1),
                reads=[bconst], writes=[bconst])
        mk.op("pool", lambda e: e.memset(mask64_f[:, :], 0.0), writes=[bconst])
        for m in range(16):
            mk.op("pool", lambda e, m=m: e.affine_select(
                out=mask64_f[:, 64 * m:64 * m + 64],
                in_=mask64_f[:, 64 * m:64 * m + 64],
                compare_op=ALU.is_gt, fill=1.0, base=0,
                pattern=[[-1, 64]], channel_multiplier=1),
                reads=[bconst], writes=[bconst])
        mk.op("pool", lambda e: e.memset(onescol_bf[:, :], 1.0), writes=[bconst])
        mk.op("pool", lambda e: e.memset(onescol_f[:, :], 1.0), writes=[bconst])
        mk.op("pool", lambda e: e.memset(onesrow_f[:, :], 1.0), writes=[bconst])

        mk.op("sdma", lambda e: e.dma_start(out=rowbuf[0:1, 0:512],
                                            in_=wIn["L0_attn_bgk2"]),
              writes=[bconst])
        mk.op("pe", lambda e: e.matmul(pb[0][:, :], onesrow_f[:, :],
                                       rowbuf[0:1, 0:512], start=True, stop=True),
              reads=[bconst], writes=[bpb[0]])
        mk.op("dve", lambda e: e.tensor_copy(bgk2b[:, :], pb[0][:, :]),
              reads=[bpb[0]], writes=[bconst])

        mk.op("sdma", lambda e: e.dma_start(out=rowbuf2[0:1, 0:128],
                                            in_=wIn["L0_attn_norm_w"]),
              writes=[bconst])
        mk.op("pe", lambda e: e.matmul(pb[1][:, 0:128], onesrow_f[:, :],
                                       rowbuf2[0:1, 0:128], start=True, stop=True),
              reads=[bconst], writes=[bpb[1]])
        for i in range(8):
            mk.op("dve", lambda e, i=i: e.tensor_copy(
                nw0[:, 128 * i:128 * i + 128], pb[1][:, 0:128]),
                reads=[bpb[1]], writes=[bconst])

        for h in range(2):
            mk.op("sdma", lambda e, h=h: e.dma_start(
                out=rowbuf[0:1, :],
                in_=wIn["L1_attn_norm_weight"][:, 512 * h:512 * h + 512]),
                writes=[bconst])
            mk.op("pe", lambda e, h=h: e.matmul(
                pb[2][:, :], onesrow_f[:, :], rowbuf[0:1, :],
                start=True, stop=True), reads=[bconst], writes=[bpb[2]])
            mk.op("dve", lambda e, h=h: e.tensor_copy(
                nw1[:, 512 * h:512 * h + 512], pb[2][:, :]),
                reads=[bpb[2]], writes=[bconst])

        for i, k in enumerate(["L1_ffn_s0", "L1_ffn_s1", "L1_ffn_s2"]):
            mk.op("sdma", lambda e, i=i, k=k: e.dma_start(
                out=rowbuf[0:1, i:i + 1], in_=wIn[k]), writes=[bconst])
        mk.op("dve", lambda e: e.tensor_scalar(rowbuf[0:1, 0:2], rowbuf[0:1, 0:2],
                                               2.0, None, ALU.mult),
              reads=[bconst], writes=[bconst])
        mk.op("pe", lambda e: e.matmul(pb[3][:, 0:4], onesrow_f[:, :],
                                       rowbuf[0:1, 0:4], start=True, stop=True),
              reads=[bconst], writes=[bpb[3]])
        mk.op("dve", lambda e: e.tensor_copy(s_cols[:, :], pb[3][:, 0:4]),
              reads=[bpb[3]], writes=[bconst])

        b1t, bb1t = arena.alloc("pp_b1t", [32, 128], F32)
        mk.op("sdma", lambda e: e.dma_start(
            out=b1t[:, :],
            in_=wIn["L0_ffn_b1"].rearrange("a (j c) -> (a j) c", c=128)),
            writes=[bb1t])
        mk.op("pe", lambda e: e.transpose(pb[4][:, 0:32], b1t[:, :],
                                          identf32[:, :]),
              reads=[bb1t, bconst], writes=[bpb[4]])
        mk.op("dve", lambda e: e.tensor_copy(b1c[:, :], pb[4][:, 0:32]),
              reads=[bpb[4]], writes=[bconst])
        b2t, bb2t = arena.alloc("pp_b2t", [8, 128], F32)
        mk.op("sdma", lambda e: e.dma_start(
            out=b2t[:, :],
            in_=wIn["L0_ffn_b2"].rearrange("a (j c) -> (a j) c", c=128)),
            writes=[bb2t])
        mk.op("pe", lambda e: e.transpose(pb[4][:, 32:40], b2t[:, :],
                                          identf8[:, :]),
              reads=[bb2t, bconst], writes=[bpb[4]])
        mk.op("dve", lambda e: e.tensor_scalar(b2h[:, :], pb[4][:, 32:40], 0.5,
                                               None, ALU.mult),
              reads=[bpb[4]], writes=[bconst])

        mk.op("gdma", lambda e: e.dma_start(
            out=wgk1_sb.rearrange("p (j c) -> p j c", c=16),
            in_=wIn["L0_attn_Wgk1"].rearrange("(j p) c -> p j c", p=128)),
            writes=[bconst])
        mk.op("gdma", lambda e: e.dma_start(out=wgk2_sb[:, :],
                                            in_=wIn["L0_attn_Wgk2"]),
              writes=[bconst])
        mk.op("gdma", lambda e: e.dma_start(
            out=wf1_sb.rearrange("p (j c) -> p j c", c=64),
            in_=wIn["L1_attn_Wf1"].rearrange("(j p) c -> p j c", p=128)),
            writes=[bconst])
        mk.op("gdma", lambda e: e.dma_start(out=wf2_sb[:, :],
                                            in_=wIn["L1_attn_Wf2"]),
              writes=[bconst])
        mk.op("gdma", lambda e: e.dma_start(
            out=wg1_sb.rearrange("p (j c) -> p j c", c=64),
            in_=wIn["L1_attn_Wg1"].rearrange("(j p) c -> p j c", p=128)),
            writes=[bconst])
        mk.op("gdma", lambda e: e.dma_start(out=wg2_sb[:, :],
                                            in_=wIn["L1_attn_Wg2"]),
              writes=[bconst])

    def emit_identf():
        mk.op("pool", lambda e: e.memset(cvals[:, 0:1], EPS), writes=[bconst])
        mk.op("pool", lambda e: e.memset(cvals[:, 1:2], 1.0), writes=[bconst])
        mk.op("pool", lambda e: e.memset(cvals[:, 2:3], 0.5), writes=[bconst])
        mk.op("pool", lambda e: e.memset(cvals[:, 3:4], -0.5), writes=[bconst])
        mk.op("pool", lambda e: e.memset(cvals[:, 4:5], 0.0), writes=[bconst])
        mk.op("pool", lambda e: e.memset(identf32[:, :], 0.0), writes=[bconst])
        mk.op("pool", lambda e: e.affine_select(
            out=identf32[:, :], in_=identf32[:, :], compare_op=ALU.not_equal,
            fill=1.0, base=0, pattern=[[-1, 32]], channel_multiplier=1),
            reads=[bconst], writes=[bconst])
        mk.op("pool", lambda e: e.memset(identf8[:, :], 0.0), writes=[bconst])
        mk.op("pool", lambda e: e.affine_select(
            out=identf8[:, :], in_=identf8[:, :], compare_op=ALU.not_equal,
            fill=1.0, base=0, pattern=[[-1, 8]], channel_multiplier=1),
            reads=[bconst], writes=[bconst])

    def emit_prep():
        for src, dst, cc in [("L0_attn_Wq", "wq0", 512),
                             ("L0_attn_Wk", "wk0", 512),
                             ("L0_attn_Wv", "wv0", 1024),
                             ("L0_attn_Wg", "wg0", 1024),
                             ("L0_attn_Wo", "wo0", 1024),
                             ("L1_attn_Wq", "wq1", 1024),
                             ("L1_attn_Wk", "wk1", 1024),
                             ("L1_attn_Wv", "wv1", 1024),
                             ("L1_attn_Wo", "wo1", 1024)]:
            if dst in scr:
                mk.op("gdma", lambda e, src=src, dst=dst, cc=cc: e.dma_start(
                    out=scr[dst].rearrange("p (j c) -> p j c", c=cc),
                    in_=wIn[src].rearrange("(j p) c -> p j c", p=128)),
                    writes=[bscr[dst]])
        if "w1t" in scr:
            _prep_quant_ffn0()

    def _prep_quant_ffn0():
        wt = [arena.alloc("pp_wtile0", [128, 4096], F32),
              arena.alloc("pp_wtile1", [128, 1024], F32)]
        m01, bm01 = arena.alloc("pp_m01", [128, 4096], F32)
        qtile, bqt = arena.alloc("pp_qtile", [128, 4096], BF16)
        stage, bst = arena.alloc("pp_stage", [128, 4096], BF16)
        acc, bacc = arena.alloc("pp_acc", [128, 1], F32)
        tmp1, btmp1 = arena.alloc("pp_tmp1", [128, 1], F32)
        ginv, bgv = arena.alloc("pp_ginv", [128, 1], F32)

        for wname, dstk, ntile, width, nelem in [
                ("L0_ffn_W1", "w1t", 32, 1024, 4096 * 1024),
                ("L0_ffn_W2", "w2timg", 8, 4096, 4096 * 1024)]:
            mk.op("pool", lambda e: e.memset(acc[:, :], 0.0), writes=[bacc])
            for i in range(ntile):
                wtile, bwt = wt[i % 2] if width == 1024 else wt[0]
                mk.op("sdma", lambda e, i=i, wname=wname, width=width,
                      wtile=wtile: e.dma_start(
                    out=wtile[:, 0:width],
                    in_=wIn[wname][128 * i:128 * (i + 1), :]),
                    reads=[bwt], writes=[bwt])
                mk.op("dve", lambda e, width=width, wtile=wtile: e.tensor_reduce(
                    tmp1[:, :], wtile[:, 0:width], axis=AX.X, op=ALU.add,
                    apply_absolute_value=True), reads=[bwt], writes=[btmp1],
                    hazard=True)
                mk.op("dve", lambda e: e.tensor_add(acc[:, :], acc[:, :],
                                                    tmp1[:, :]),
                      reads=[btmp1, bacc], writes=[bacc])
            mk.op("pe", lambda e: e.matmul(pb[4][0:1, 0:1], acc[:, :],
                                           onescol_f[:, :], start=True, stop=True),
                  reads=[bacc, bconst], writes=[bpb[4]])
            mk.op("dve", lambda e, nelem=nelem: e.tensor_scalar(
                rowbuf2[0:1, 8:9], pb[4][0:1, 0:1], 1.0 / nelem, EPS,
                ALU.mult, ALU.add), reads=[bpb[4]], writes=[bconst])
            mk.op("dve", lambda e: e.reciprocal(rowbuf2[0:1, 9:10],
                                                rowbuf2[0:1, 8:9]),
                  reads=[bconst], writes=[bconst], serialize=True, hazard=True)
            mk.op("pe", lambda e: e.matmul(pb[4][:, 0:1], onesrow_f[:, :],
                                           rowbuf2[0:1, 9:10], start=True,
                                           stop=True),
                  reads=[bconst], writes=[bpb[4]])
            mk.op("dve", lambda e: e.tensor_copy(ginv[:, :], pb[4][:, 0:1]),
                  reads=[bpb[4]], writes=[bgv])
            for i in range(ntile):
                wtile, bwt = wt[i % 2] if width == 1024 else wt[0]
                sgn, bsg = wtile, bwt
                mk.op("sdma", lambda e, i=i, wname=wname, width=width,
                      wtile=wtile: e.dma_start(
                    out=wtile[:, 0:width],
                    in_=wIn[wname][128 * i:128 * (i + 1), :]),
                    reads=[bwt], writes=[bwt])
                mk.op("dve", lambda e, width=width, wtile=wtile: e.tensor_scalar(
                    m01[:, 0:width], wtile[:, 0:width], ginv[:, 0:1], None,
                    ALU.mult), reads=[bwt, bgv], writes=[bm01])
                mk.op("act", lambda e, width=width, sgn=sgn: e.sign(
                    sgn[:, 0:width], m01[:, 0:width], bias=0.5),
                    reads=[bm01], writes=[bsg])
                mk.op("act", lambda e, width=width: e.sign(
                    m01[:, 0:width], m01[:, 0:width], bias=-0.5),
                    reads=[bm01], writes=[bm01])
                mk.op("dve", lambda e, width=width, sgn=sgn: e.tensor_add(
                    sgn[:, 0:width], sgn[:, 0:width], m01[:, 0:width]),
                    reads=[bm01, bsg], writes=[bsg])
                mk.op("dve", lambda e, width=width, sgn=sgn: e.tensor_single_scalar(
                    qtile[:, 0:width], sgn[:, 0:width], 0.5, ALU.mult),
                    reads=[bsg], writes=[bqt])
                nblk = width // 128
                for bat in range(0, nblk, 8):
                    nb = min(8, nblk - bat)

                    def trs(e, bat=bat, nb=nb):
                        last = None
                        for j in range(nb):
                            last = e.transpose(
                                pt[:, 128 * j:128 * j + 128],
                                qtile[:, 128 * (bat + j):128 * (bat + j) + 128],
                                ident_bf[:, :])
                        return last
                    mk.op("pe", trs, reads=[bqt, bconst], writes=[bpt])
                    mk.op("act", lambda e, bat=bat, nb=nb: e.copy(
                        stage[:, 128 * bat:128 * (bat + nb)], pt[:, 0:128 * nb]),
                        reads=[bpt], writes=[bst])
                if dstk == "w1t":
                    mk.op("sdma", lambda e, i=i: e.dma_start(
                        out=scr["w1t"].rearrange(
                            "p (fb j c) -> p fb j c", fb=8, j=8)
                        [:, i // 4, :, 128 * (i % 4):128 * (i % 4) + 128],
                        in_=stage[:, 0:1024].rearrange("p (j c) -> p j c", c=128)),
                        reads=[bst], writes=[bscr["w1t"]])
                else:
                    mk.op("sdma", lambda e, i=i: e.dma_start(
                        out=scr["w2timg"][i],
                        in_=stage.rearrange("p (j c) -> p j c", c=128)),
                        reads=[bst], writes=[bscr["w2timg"]])

    def _prep_ffn1():
        ta, bta = arena.alloc("pp_f1a", [128, 4096], BF16)
        tb, btb = arena.alloc("pp_f1b", [128, 4096], BF16)
        for j in range(8):
            mk.op("gdma", lambda e, j=j: e.dma_start(
                out=ta[:, :], in_=wIn["L1_ffn_k0"][128 * j:128 * (j + 1), :]),
                reads=[bta], writes=[bta])
            mk.op("gdma", lambda e, j=j: e.dma_start(
                out=tb[:, :],
                in_=wIn["L1_ffn_k0"][1024 + 128 * j:1024 + 128 * (j + 1), :]),
                reads=[btb], writes=[btb])
            mk.op("dve", lambda e: e.tensor_sub(ta[:, :], ta[:, :], tb[:, :]),
                  reads=[bta, btb], writes=[bta])
            mk.op("sdma", lambda e, j=j: e.dma_start(
                out=scr["w0f"].rearrange("p (fb jj c) -> p fb jj c",
                                         fb=8, jj=8)[:, :, j, :],
                in_=ta.rearrange("p (fb c) -> p fb c", c=512)),
                reads=[bta], writes=[bscr["w0f"]])
        for j in range(32):
            mk.op("gdma", lambda e, j=j: e.dma_start(
                out=ta[:, :], in_=wIn["L1_ffn_k1"][128 * j:128 * (j + 1), :]),
                reads=[bta], writes=[bta])
            mk.op("dve", lambda e: e.tensor_scalar(ta[:, :], ta[:, :], 2.0, -1.0,
                                                   ALU.mult, ALU.add),
                  reads=[bta], writes=[bta])
            mk.op("sdma", lambda e, j=j: e.dma_start(
                out=scr["w1bimg"][:, :, j, :].rearrange("fb p c -> p fb c"),
                in_=ta.rearrange("p (fb c) -> p fb c", c=128)),
                reads=[bta], writes=[bscr["w1bimg"]])
        for j in range(32):
            mk.op("gdma", lambda e, j=j: e.dma_start(
                out=tb[:, 0:1024], in_=wIn["L1_ffn_k2"][128 * j:128 * (j + 1), :]),
                reads=[btb], writes=[btb])
            mk.op("dve", lambda e: e.tensor_scalar(tb[:, 0:1024], tb[:, 0:1024],
                                                   2.0, -1.0, ALU.mult, ALU.add),
                  reads=[btb], writes=[btb])
            mk.op("sdma", lambda e, j=j: e.dma_start(
                out=scr["w2bimg"][:, :, j, :].rearrange("fb p c -> p fb c"),
                in_=tb[:, 0:1024].rearrange("p (fb c) -> p fb c", c=128)),
                reads=[btb], writes=[bscr["w2bimg"]])

    def emit_load_x(ti):
        mk.op("sdma", lambda e, ti=ti: e.dma_start(
            out=x_fm.rearrange("p (j t) -> p j t", j=8),
            in_=x_dev.rearrange("(j p) t -> p j t", p=128)
            [:, :, MT * ti:MT * (ti + 1)]),
            reads=[bxh[0], bxh[1]], writes=[bxh[0], bxh[1]])

    def emit_store_x(ti):
        mk.op("sdma", lambda e, ti=ti: e.dma_start(
            out=y_dev.rearrange("(j p) t -> p j t", p=128)
            [:, :, MT * ti:MT * (ti + 1)],
            in_=x_fm.rearrange("p (j t) -> p j t", j=8)),
            reads=[bxh[0], bxh[1]], writes=[by])

    def emit_ln(tsI, fset=False):
        if True:
            t0 = TS * tsI
            bxT, bxlnT = bxh[tsI], bxlnh[tsI]
            if fset:
                sA, bsA = pw0[:, 0:512], bpw0
                sB, bsB = pw0[:, 512:1024], bpw0
                cA, bcA = pw0[:, 0:512], bpw0
                cB, bcB = pw0[:, 512:1024], bpw0
            else:
                sA, bsA = pb[0][:, :], bpb[0]
                sB, bsB = pb[1][:, :], bpb[1]
                cA, bcA = pb[2][:, :], bpb[2]
                cB, bcB = pb[3][:, :], bpb[3]
            for j in range(8):
                sl = j % 4
                mk.op("act", lambda e, j=j, sl=sl, t0=t0: e.activation(
                    lnxsq[:, TS * sl:TS * sl + TS],
                    x_fm[:, 1024 * j + t0:1024 * j + t0 + TS], AF.Square),
                    reads=[bxT], writes=[blnx4[sl]])
                if j == 3:
                    for jj in range(8):
                        mk.op("pe", lambda e, jj=jj, t0=t0, sA=sA: e.matmul(
                            sA[0:1, :], onescol_f[:, :],
                            x_fm[:, 1024 * jj + t0:1024 * jj + t0 + TS],
                            start=(jj == 0), stop=(jj == 7)),
                            reads=[bxT, bconst], writes=[bsA])
                mk.op("pe", lambda e, j=j, sl=sl, sB=sB: e.matmul(
                    sB[0:1, :], onescol_bf[:, :],
                    lnxsq[:, TS * sl:TS * sl + TS],
                    start=(j == 0), stop=(j == 7)),
                    reads=[blnx4[sl], bconst], writes=[bsB])
            mk.op("dve", lambda e, sA=sA: e.tensor_scalar(
                rowbuf[0:1, 0:TS], sA[0:1, :], 1.0 / D, None, ALU.mult),
                reads=[bsA], writes=[brow])
            mk.op("dve", lambda e: e.tensor_mul(
                rowbuf2[0:1, 0:TS], rowbuf[0:1, 0:TS], rowbuf[0:1, 0:TS]),
                reads=[brow], writes=[brow])
            mk.op("dve", lambda e, sB=sB: e.scalar_tensor_tensor(
                rowbuf2[0:1, 0:TS], sB[0:1, :], 1.0 / D, rowbuf2[0:1, 0:TS],
                ALU.mult, ALU.subtract), reads=[bsB, brow], writes=[brow])
            mk.op("act", lambda e: e.activation(
                rowbuf2[0:1, 0:TS], rowbuf2[0:1, 0:TS], AF.Ln,
                bias=cvals[0:1, 0:1]),
                reads=[brow], writes=[brow])
            mk.op("act", lambda e: e.activation(
                rowbuf2[0:1, 0:TS], rowbuf2[0:1, 0:TS], AF.Exp, scale=-0.5,
                bias=cvals[0:1, 4:5]),
                reads=[brow], writes=[brow])
            mk.op("pe", lambda e, cA=cA: e.matmul(cA[:, :], onesrow_f[:, :],
                                                  rowbuf2[0:1, 0:TS],
                                                  start=True, stop=True),
                  reads=[brow, bconst], writes=[bcA])
            mk.op("pe", lambda e, cB=cB: e.matmul(cB[:, :], onesrow_f[:, :],
                                                  rowbuf[0:1, 0:TS],
                                                  start=True, stop=True),
                  reads=[brow, bconst], writes=[bcB])
            for j in range(8):
                mk.op("dve", lambda e, j=j, t0=t0, cB=cB: e.tensor_tensor(
                    xln[:, 1024 * j + t0:1024 * j + t0 + TS],
                    x_fm[:, 1024 * j + t0:1024 * j + t0 + TS], cB[:, :],
                    ALU.subtract), reads=[bxT, bcB], writes=[bxlnT])
                mk.op("dve", lambda e, j=j, t0=t0, cA=cA: e.tensor_tensor(
                    xln[:, 1024 * j + t0:1024 * j + t0 + TS],
                    xln[:, 1024 * j + t0:1024 * j + t0 + TS], cA[:, :],
                    ALU.mult), reads=[bxlnT, bcA], writes=[bxlnT])

    def emit_ffn0(lnmid=None, lntail=None):
        h, bh = arena.alloc("f0_h", [128, 32 * TS], BF16)
        tmp, btmp = arena.alloc("f0_tmp", [128, TS], F32)
        for tsI in range(MT // TS):
            t0 = TS * tsI
            bxT, bxlnT = bxh[tsI], bxlnh[tsI]
            for fb in range(8):
                sl, bsl = next_slot()
                mk.op("sdma", lambda e, sl=sl, fb=fb: e.dma_start(
                    out=sl[:, :],
                    in_=scr["w1t"][:, 4096 * fb:4096 * (fb + 1)]),
                    reads=[bscr["w1t"], bsl], writes=[bsl])
                for fc in range(4):
                    fg = 4 * fb + fc
                    bp = bpb[fg % 4]
                    pa = pb[fg % 4]

                    def m1(e, sl=sl, fc=fc, pa=pa, t0=t0):
                        last = None
                        for j in range(8):
                            last = e.matmul(
                                pa[:, :],
                                sl[:, 512 * j + 128 * fc:512 * j + 128 * fc + 128],
                                xln[:, 1024 * j + t0:1024 * j + t0 + TS],
                                start=(j == 0), stop=(j == 7))
                        return last
                    mk.op("pe", m1, reads=[bsl, bxlnT], writes=[bp])
                    mk.op("act", lambda e, fg=fg, pa=pa: e.activation(
                        h[:, TS * fg:TS * (fg + 1)], pa[:, :], AF.Relu,
                        bias=b1c[:, fg:fg + 1]),
                        reads=[bp, bconst], writes=[bh])
            if tsI == 0 and lnmid:
                lnmid()
            if tsI == 1 and lntail:
                lntail()
            for db in range(8):
                sl, bsl = next_slot()
                mk.op("sdma", lambda e, sl=sl, db=db: e.dma_start(
                    out=sl[:, :],
                    in_=scr["w2timg"][db].rearrange("p a b -> p (a b)")),
                    reads=[bscr["w2timg"], bsl], writes=[bsl])
                bp = bpb[db % 4]
                pa = pb[db % 4]

                def m2(e, sl=sl, pa=pa):
                    last = None
                    for jj in range(32):
                        last = e.matmul(pa[:, :], sl[:, 128 * jj:128 * jj + 128],
                                        h[:, TS * jj:TS * (jj + 1)],
                                        start=(jj == 0), stop=(jj == 31))
                    return last
                mk.op("pe", m2, reads=[bsl, bh], writes=[bp])
                mk.op("act", lambda e, db=db, pa=pa: e.activation(
                    tmp[:, :], pa[:, :], AF.Identity, scale=0.5,
                    bias=b2h[:, db:db + 1]),
                    reads=[bp, bconst], writes=[btmp])
                mk.op("dve", lambda e, db=db, t0=t0: e.tensor_add(
                    x_fm[:, 1024 * db + t0:1024 * db + t0 + TS],
                    x_fm[:, 1024 * db + t0:1024 * db + t0 + TS], tmp[:, :]),
                    reads=[btmp, bxT], writes=[bxT])

    def emit_ffn1(lnmid=None, lntail=None):
        h1, bh1 = arena.alloc("f1_h1", [128, 32 * TS], BF16)
        h2, bh2 = arena.alloc("f1_h2", [128, 32 * TS], BF16)
        for tsI in range(MT // TS):
            t0 = TS * tsI
            bxT, bxlnT = bxh[tsI], bxlnh[tsI]
            for fb in range(8):
                sl, bsl = next_slot()
                mk.op("sdma", lambda e, sl=sl, fb=fb: e.dma_start(
                    out=sl[:, :],
                    in_=scr["w0f"][:, 4096 * fb:4096 * (fb + 1)]),
                    reads=[bscr["w0f"], bsl], writes=[bsl])
                for fc in range(4):
                    fg = 4 * fb + fc
                    bp = bpb[fg % 4]
                    pa = pb[fg % 4]

                    def m0(e, sl=sl, fc=fc, pa=pa, t0=t0):
                        last = None
                        for j in range(8):
                            last = e.matmul(
                                pa[:, :],
                                sl[:, 512 * j + 128 * fc:512 * j + 128 * fc + 128],
                                xln[:, 1024 * j + t0:1024 * j + t0 + TS],
                                start=(j == 0), stop=(j == 7))
                        return last
                    mk.op("pe", m0, reads=[bsl, bxlnT], writes=[bp])
                    mk.op("dve", lambda e, fg=fg, pa=pa: e.tensor_scalar(
                        h1[:, TS * fg:TS * (fg + 1)], pa[:, :],
                        s_cols[:, 0:1], 0.0, ALU.mult, ALU.max),
                        reads=[bp, bconst], writes=[bh1])
            if tsI == 0 and lnmid:
                lnmid()
            if tsI == 1 and lntail:
                lntail()
            for fb2 in range(32):
                sl, bsl = next_slot()
                mk.op("sdma", lambda e, sl=sl, fb2=fb2: e.dma_start(
                    out=sl[:, :],
                    in_=scr["w1bimg"][fb2].rearrange("p a b -> p (a b)")),
                    reads=[bscr["w1bimg"], bsl], writes=[bsl])
                bp = bpb[fb2 % 4]
                pa = pb[fb2 % 4]

                def m1(e, sl=sl, pa=pa):
                    last = None
                    for jj in range(32):
                        last = e.matmul(pa[:, :], sl[:, 128 * jj:128 * jj + 128],
                                        h1[:, TS * jj:TS * (jj + 1)],
                                        start=(jj == 0), stop=(jj == 31))
                    return last
                mk.op("pe", m1, reads=[bsl, bh1], writes=[bp])
                mk.op("dve", lambda e, fb2=fb2, pa=pa: e.tensor_scalar(
                    h2[:, TS * fb2:TS * (fb2 + 1)], pa[:, :],
                    s_cols[:, 1:2], 0.0, ALU.mult, ALU.max),
                    reads=[bp, bconst], writes=[bh2])
            for db in range(8):
                sl, bsl = next_slot()
                mk.op("sdma", lambda e, sl=sl, db=db: e.dma_start(
                    out=sl[:, :],
                    in_=scr["w2bimg"][db].rearrange("p a b -> p (a b)")),
                    reads=[bscr["w2bimg"], bsl], writes=[bsl])
                bp = bpb[db % 4]
                pa = pb[db % 4]

                def m2(e, sl=sl, pa=pa):
                    last = None
                    for jj in range(32):
                        last = e.matmul(pa[:, :], sl[:, 128 * jj:128 * jj + 128],
                                        h2[:, TS * jj:TS * (jj + 1)],
                                        start=(jj == 0), stop=(jj == 31))
                    return last
                mk.op("pe", m2, reads=[bsl, bh2], writes=[bp])
                mk.op("dve", lambda e, db=db, t0=t0, pa=pa: e.scalar_tensor_tensor(
                    x_fm[:, 1024 * db + t0:1024 * db + t0 + TS], pa[:, :],
                    s_cols[:, 2:3],
                    x_fm[:, 1024 * db + t0:1024 * db + t0 + TS],
                    ALU.mult, ALU.add),
                    reads=[bp, bconst, bxT], writes=[bxT])

    def emit_attn0(ti, lnmid=None, lntail=None):
        z1gk, _bz1 = arena.alloc("a0_z1gk", [16, 1024], BF16)
        u, bu = arena.alloc("a0_u", [128, 512], F32)
        mn, bmn = arena.alloc("a0_mn", [128, 512], F32)
        gkb, bgkb = arena.alloc("a0_gk", [128, 512], BF16)
        eb, beb = arena.alloc("a0_eb", [128, 512], F32)
        enb, benb = arena.alloc("a0_enb", [128, 512], F32)
        qe, bqe = arena.alloc("a0_qe", [128, 512], BF16)
        ke2 = [arena.alloc(f"a0_ke{a}", [128, 512], BF16) for a in range(2)]
        qefm2 = [arena.alloc(f"a0_qefm{a}", [64, 1024], BF16) for a in range(2)]
        kefm2 = [arena.alloc(f"a0_kefm{a}", [64, 1024], BF16) for a in range(2)]
        vbf2 = [arena.alloc(f"a0_vbf{a}", [128, 1024], BF16) for a in range(2)]
        am, bam = arena.alloc("a0_am", [128, 1024], BF16)
        osq, bosq = arena.alloc("a0_osq", [128, 1024], F32)
        oep, boep = arena.alloc("a0_oep", [128, 1024], BF16)
        rsq, brsq = arena.alloc("a0_rsq", [128, 8], F32)
        scol2 = [arena.alloc(f"a0_scol{a}", [64, 8], F32) for a in range(2)]
        oefm, boefm = arena.alloc("a0_oefm", [128, 8192], BF16)

        if ti == 0:
            mk.op("pool", lambda e: e.memset(S0[:, :], 0.0), writes=[bS0])
            mk.op("pool", lambda e: e.memset(S0bf[:, :], 0.0), writes=[bS0bf])

        slq, bslq = next_slot()
        mk.op("sdma", lambda e, sl=slq: e.dma_start(
            out=sl[:, :], in_=scr["wq0"]),
            reads=[bscr["wq0"], bslq], writes=[bslq])
        slk, bslk = next_slot()
        mk.op("sdma", lambda e, sl=slk: e.dma_start(
            out=sl[:, :], in_=scr["wk0"]),
            reads=[bscr["wk0"], bslk], writes=[bslk])
        slv = []
        for hh in range(2):
            sl, bsl = next_slot()
            mk.op("sdma", lambda e, sl=sl, hh=hh: e.dma_start(
                out=sl[:, :],
                in_=scr["wv0"][:, 4096 * hh:4096 * (hh + 1)]),
                reads=[bscr["wv0"], bsl], writes=[bsl])
            slv.append((sl, bsl))
        slg = []
        for hh in range(2):
            sl, bsl = next_slot()
            mk.op("sdma", lambda e, sl=sl, hh=hh: e.dma_start(
                out=sl[:, :],
                in_=scr["wg0"][:, 4096 * hh:4096 * (hh + 1)]),
                reads=[bscr["wg0"], bsl], writes=[bsl])
            slg.append((sl, bsl))

        def emit_z1(half):
            def z1mm(e, half=half):
                last = None
                for j in range(8):
                    last = e.matmul(
                        pb[4][0:16, :], wgk1_sb[:, 16 * j:16 * j + 16],
                        xln[:, 1024 * j + 512 * half:1024 * j + 512 * half + 512],
                        start=(j == 0), stop=(j == 7))
                return last
            mk.op("pe", z1mm, reads=[bxlnh[half], bconst], writes=[bpb[4]])
            mk.op("dve", lambda e, half=half: e.tensor_copy(
                z1gk[:, 512 * half:512 * half + 512], pb[4][0:16, :]),
                reads=[bpb[4]], writes=[bz1])
        emit_z1(0)

        ocp2 = [arena.alloc(f"a0_ocp{a}", [128, 1024], F32) for a in range(2)]
        sgg3 = [arena.alloc(f"a0_sg{a}", [128, 1024], BF16) for a in range(3)]

        def emit_epi(c):
            par = c % 2
            tc = C0 * c
            ocp, bocp = ocp2[par]
            sgg, bsgg = sgg3[c % 3]
            mk.op("act", lambda e, ocp=ocp: e.activation(
                osq[:, :], ocp[:, :], AF.Square),
                reads=[bocp], writes=[bosq])
            mk.op("dve", lambda e: e.tensor_reduce(
                rsq[:, 0:8], osq.rearrange("p (h c) -> p h c", c=128),
                axis=AX.X, op=ALU.add), reads=[bosq], writes=[brsq],
                hazard=True)
            mk.op("act", lambda e: e.activation(rsq[:, 0:8], rsq[:, 0:8],
                                                AF.Ln, scale=1.0 / 128.0,
                                                bias=cvals[:, 0:1]),
                  reads=[brsq], writes=[brsq])
            mk.op("act", lambda e: e.activation(rsq[:, 0:8], rsq[:, 0:8],
                                                AF.Exp, scale=-0.5,
                                                bias=cvals[:, 4:5]),
                  reads=[brsq], writes=[brsq])
            mk.op("dve", lambda e, sgg=sgg: e.tensor_mul(
                sgg[:, :], sgg[:, :], nw0[:, :]),
                reads=[bsgg, bconst], writes=[bsgg])
            for h in range(8):
                mk.op("dve", lambda e, h=h, ocp=ocp, sgg=sgg:
                      e.scalar_tensor_tensor(
                          oep[:, 128 * h:128 * h + 128],
                          ocp[:, 128 * h:128 * h + 128],
                          rsq[:, h:h + 1], sgg[:, 128 * h:128 * h + 128],
                          ALU.mult, ALU.mult),
                      reads=[bocp, brsq, bsgg], writes=[boep])

            def tr_o(e):
                last = None
                for j in range(8):
                    last = e.transpose(pt[:, 128 * j:128 * j + 128],
                                       oep[:, 128 * j:128 * j + 128],
                                       ident_bf[:, :])
                return last
            mk.op("pe", tr_o, reads=[boep, bconst], writes=[bpt])
            mk.op("act", lambda e, tc=tc: e.copy(
                oefm.rearrange("p (j t) -> p j t", j=8)[:, :, tc:tc + 128],
                pt.rearrange("p (j t) -> p j t", j=8)),
                reads=[bpt], writes=[boefm])

        def emit_A(c):
            tc = C0 * c
            par = c % 2
            bxlnT = bxlnh[tc // TS]
            sgg, bsgg = sgg3[c % 3]
            ke, bke = ke2[par]
            qefm, bqefm = qefm2[par]
            kefm, bkefm = kefm2[par]
            vbf, bvbf = vbf2[par]
            scol, bscol = scol2[par]

            mk.op("pe", lambda e, tc=tc: e.matmul(
                pb[4][:, :], z1gk[:, tc:tc + 128], wgk2_sb[:, :],
                start=True, stop=True),
                reads=[bz1, bconst], writes=[bpb[4]])
            mk.op("dve", lambda e: e.tensor_add(u[:, :], pb[4][:, :],
                                                bgk2b[:, :]),
                  reads=[bpb[4], bconst], writes=[bu])
            mk.op("act", lambda e: e.activation(mn[:, :], u[:, :], AF.Sigmoid),
                  reads=[bu], writes=[bmn])
            mk.op("act", lambda e: e.activation(gkb[:, :], mn[:, :], AF.Ln),
                  reads=[bmn], writes=[bgkb])

            def q_mm(e, tc=tc):
                last = None
                for j in range(8):
                    last = e.matmul(pb[0][:, :],
                                    xln[:, 1024 * j + tc:1024 * j + tc + 128],
                                    slq[:, 512 * j:512 * (j + 1)],
                                    start=(j == 0), stop=(j == 7))
                return last
            mk.op("pe", q_mm, reads=[bxlnT, bslq], writes=[bpb[0]])

            def k_mm(e, tc=tc):
                last = None
                for j in range(8):
                    last = e.matmul(pb[1][:, :],
                                    xln[:, 1024 * j + tc:1024 * j + tc + 128],
                                    slk[:, 512 * j:512 * (j + 1)],
                                    start=(j == 0), stop=(j == 7))
                return last
            mk.op("pe", k_mm, reads=[bxlnT, bslk], writes=[bpb[1]])

            def v_mm(e, tc=tc):
                last = None
                for hh in range(2):
                    for j in range(8):
                        sl = slv[j // 4][0]
                        last = e.matmul(
                            pw0[:, 512 * hh:512 * hh + 512],
                            xln[:, 1024 * j + tc:1024 * j + tc + 128],
                            sl[:, 1024 * (j % 4) + 512 * hh:
                               1024 * (j % 4) + 512 * hh + 512],
                            start=(j == 0), stop=(j == 7))
                return last
            mk.op("pe", v_mm, reads=[bxlnT, slv[0][1], slv[1][1]], writes=[bpw0])
            mk.op("act", lambda e: e.copy(vbf[:, :], pw0[:, :]),
                  reads=[bpw0], writes=[bvbf])

            def g_mm(e, hh, tc=tc):
                last = None
                for j in range(8):
                    sl = slg[j // 4][0]
                    last = e.matmul(
                        pb[2 + hh][:, :],
                        xln[:, 1024 * j + tc:1024 * j + tc + 128],
                        sl[:, 1024 * (j % 4) + 512 * hh:
                           1024 * (j % 4) + 512 * hh + 512],
                        start=(j == 0), stop=(j == 7))
                return last
            for hh in range(2):
                mk.op("pe", lambda e, hh=hh, tc=tc: g_mm(e, hh, tc),
                      reads=[bxlnT, slg[0][1], slg[1][1]], writes=[bpb[2 + hh]])
            for hh in range(2):
                mk.op("act", lambda e, hh=hh, sgg=sgg: e.activation(
                    sgg[:, 512 * hh:512 * hh + 512], pb[2 + hh][:, :],
                    AF.Silu), reads=[bpb[2 + hh]], writes=[bsgg])

            mk.op("pe", lambda e: e.matmul(pb[4][:, :], utri_bf[:, :],
                                           gkb[:, :], start=True, stop=True),
                  reads=[bgkb, bconst], writes=[bpb[4]])
            mk.op("act", lambda e: e.activation(eb[:, :], pb[4][:, :], AF.Exp,
                                                scale=1.0 / 16.0,
                                                bias=cvals[:, 4:5]),
                  reads=[bpb[4]], writes=[beb])
            mk.op("act", lambda e: e.activation(enb[:, :], pb[4][:, :], AF.Exp,
                                                scale=-1.0 / 16.0),
                  reads=[bpb[4]], writes=[benb])

            mk.op("dve", lambda e: e.scalar_tensor_tensor(
                qe[:, :], pb[0][:, :], 0.125, eb[:, :], ALU.mult, ALU.mult),
                reads=[bpb[0], beb], writes=[bqe])
            mk.op("dve", lambda e: e.tensor_tensor(
                ke[:, :], pb[1][:, :], enb[:, :], ALU.mult),
                reads=[bpb[1], benb], writes=[bke])

            def blc(e):
                last = None
                for h in range(8):
                    last = e.matmul(pb[1][0:64, h:h + 1],
                                    gkb[:, 64 * h:64 * h + 64],
                                    onescol_bf[:, :], start=True, stop=True)
                return last
            mk.op("pe", blc, reads=[bgkb, bconst], writes=[bpb[1]])
            mk.op("act", lambda e: e.activation(scol[:, :], pb[1][0:64, 0:8],
                                                AF.Exp, scale=1.0 / 16.0,
                                                bias=cvals[0:64, 4:5]),
                  reads=[bpb[1]], writes=[bscol])

            def tr_q(e):
                last = None
                for h in range(8):
                    last = e.transpose(pt[0:64, 128 * h:128 * h + 128],
                                       qe[:, 64 * h:64 * h + 64],
                                       ident_bf[:, :])
                return last
            mk.op("pe", tr_q, reads=[bqe, bconst], writes=[bpt])
            mk.op("dve", lambda e: e.tensor_copy(qefm[:, :], pt[0:64, :]),
                  reads=[bpt], writes=[bqefm])

            def tr_k(e):
                last = None
                for h in range(8):
                    last = e.transpose(pt[0:64, 128 * h:128 * h + 128],
                                       ke[:, 64 * h:64 * h + 64],
                                       ident_bf[:, :])
                return last
            mk.op("pe", tr_k, reads=[bke, bconst], writes=[bpt])
            mk.op("dve", lambda e: e.tensor_copy(kefm[:, :], pt[0:64, :]),
                  reads=[bpt], writes=[bkefm])

        def emit_B(c):
            tc = C0 * c
            par = c % 2
            ocp, bocp = ocp2[par]
            ke, bke = ke2[par]
            qefm, bqefm = qefm2[par]
            kefm, bkefm = kefm2[par]
            vbf, bvbf = vbf2[par]
            scol, bscol = scol2[par]

            def a_mm(e):
                last = None
                for h in range(8):
                    last = e.matmul(pw0[:, 128 * h:128 * h + 128],
                                    kefm[:, 128 * h:128 * h + 128],
                                    qefm[:, 128 * h:128 * h + 128],
                                    start=True, stop=True)
                return last
            mk.op("pe", a_mm, reads=[bqefm, bkefm], writes=[bpw0])
            mk.op("dve", lambda e: e.tensor_tensor(am[:, :], pw0[:, :],
                                                   mask8_f[:, :], ALU.mult),
                  reads=[bpw0, bconst], writes=[bam])

            def o_mm(e):
                last = None
                for h in range(8):
                    last = e.matmul(pw0[:, 128 * h:128 * h + 128],
                                    am[:, 128 * h:128 * h + 128],
                                    vbf[:, 128 * h:128 * h + 128],
                                    start=True, stop=False)
                    last = e.matmul(pw0[:, 128 * h:128 * h + 128],
                                    qefm[:, 128 * h:128 * h + 128],
                                    S0bf[:, 128 * h:128 * h + 128],
                                    start=False, stop=True)
                return last
            mk.op("pe", o_mm, reads=[bam, bvbf, bqefm, bS0bf], writes=[bpw0])

            mk.op("act", lambda e, ocp=ocp: e.copy(ocp[:, :], pw0[:, :]),
                  reads=[bpw0], writes=[bocp])

            def s_mm(e):
                last = None
                for h in range(8):
                    last = e.matmul(pw0[0:64, 128 * h:128 * h + 128],
                                    ke[:, 64 * h:64 * h + 64],
                                    vbf[:, 128 * h:128 * h + 128],
                                    start=True, stop=True)
                return last
            mk.op("pe", s_mm, reads=[bke, bvbf, bocp], writes=[bpw0])
            mk.op("dve", lambda e: e.tensor_add(
                S0[:, :], S0[:, :], pw0[0:64, :]),
                reads=[bpw0, bS0], writes=[bS0])
            for h in range(8):
                mk.op("dve", lambda e, h=h: e.tensor_scalar(
                    S0[:, 128 * h:128 * h + 128], S0[:, 128 * h:128 * h + 128],
                    scol[:, h:h + 1], None, ALU.mult),
                    reads=[bscol, bS0], writes=[bS0])
            mk.op("act", lambda e: e.copy(S0bf[:, :], S0[:, :]),
                  reads=[bS0], writes=[bS0bf])

        NC = MT // C0
        for c in range(NC + 1):
            if c < NC:
                emit_A(c)
            if c == 2:
                if lnmid:
                    lnmid()
                emit_z1(1)
            if c >= 1:
                emit_B(c - 1)
            if c >= 2:
                emit_epi(c - 2)
        emit_epi(NC - 1)

        slo = []
        for hh in range(2):
            sl, bsl = next_slot()
            mk.op("sdma", lambda e, sl=sl, hh=hh: e.dma_start(
                out=sl[:, :],
                in_=scr["wo0"][:, 4096 * hh:4096 * (hh + 1)]),
                reads=[bscr["wo0"], bsl], writes=[bsl])
            slo.append((sl, bsl))
        for tsI in range(2):
            t0 = TS * tsI
            if tsI == 1 and lntail:
                lntail()
            for db in range(8):
                bp = bpb[db % 2]
                pa = pb[db % 2]

                def op_mm(e, db=db, t0=t0, pa=pa):
                    last = None
                    for j in range(8):
                        sl = slo[j // 4][0]
                        last = e.matmul(
                            pa[:, :],
                            sl[:, 1024 * (j % 4) + 128 * db:
                               1024 * (j % 4) + 128 * db + 128],
                            oefm[:, 1024 * j + t0:1024 * j + t0 + TS],
                            start=(j == 0), stop=(j == 7))
                    return last
                mk.op("pe", op_mm, reads=[boefm, slo[0][1], slo[1][1]],
                      writes=[bp])
                mk.op("dve", lambda e, db=db, t0=t0, pa=pa: e.tensor_add(
                    x_fm[:, 1024 * db + t0:1024 * db + t0 + TS],
                    x_fm[:, 1024 * db + t0:1024 * db + t0 + TS], pa[:, :]),
                    reads=[bp, bxh[tsI]], writes=[bxh[tsI]])

    def emit_attn1(ti, lnmid=None, lntail=None):
        zf1, _bzf = arena.alloc("a1_zf1", [64, 1024], BF16)
        zg1, _bzg = arena.alloc("a1_zg1", [64, 1024], BF16)
        sg, bsg = arena.alloc("a1_sgf", [128, 1024], F32)
        sgg, bsgg = arena.alloc("a1_sg", [128, 1024], BF16)
        psil, bpsil = arena.alloc("a1_psil", [128, 1024], F32)
        psq, bpsq = arena.alloc("a1_psq", [128, 1024], BF16)
        gkb, bgkb = arena.alloc("a1_gk", [128, 1024], BF16)
        gk0, bgk0 = arena.alloc("a1_gk0", [64, 1024], BF16)
        gk1, bgk1 = arena.alloc("a1_gk1", [64, 1024], BF16)
        eb, beb = arena.alloc("a1_eb", [128, 1024], F32)
        enb, benb = arena.alloc("a1_enb", [128, 1024], F32)
        qe0, bqe0 = arena.alloc("a1_qe0", [64, 1024], BF16)
        qe1, bqe1 = arena.alloc("a1_qe1", [64, 1024], BF16)
        ke0, bke0 = arena.alloc("a1_ke0", [64, 1024], BF16)
        ke1, bke1 = arena.alloc("a1_ke1", [64, 1024], BF16)
        vbf0, bvbf0 = arena.alloc("a1_vbf0", [64, 1024], BF16)
        vbf1, bvbf1 = arena.alloc("a1_vbf1", [64, 1024], BF16)
        qefm, bqefm = arena.alloc("a1_qefm", [64, 1024], BF16)
        kefm, bkefm = arena.alloc("a1_kefm", [64, 1024], BF16)
        am, bam = gkb[0:64, :], bgkb
        ogs = [(sg[0:64, :], bsg), (enb[0:64, :], benb)]
        osq, bosq = psil[0:64, :], bpsil
        oeps = [arena.alloc(f"a1_oep{a}", [64, 1024], BF16) for a in range(2)]
        rsq, brsq = arena.alloc("a1_rsq", [64, 32], F32)
        scol, bscol = arena.alloc("a1_scol", [64, 32], F32)
        oefm, boefm = arena.alloc("a1_oefm", [128, 8192], BF16)

        if ti == 0:
            mk.op("pool", lambda e: e.memset(S1[:, :], 0.0), writes=[bS1])
            mk.op("pool", lambda e: e.memset(S1bf[:, :], 0.0), writes=[bS1bf])

        wsl = {}
        for nm, key in [("q", "wq1"), ("k", "wk1"), ("v", "wv1")]:
            pair = []
            for hh in range(2):
                sl, bsl = next_slot()
                mk.op("sdma", lambda e, sl=sl, hh=hh, key=key: e.dma_start(
                    out=sl[:, :],
                    in_=scr[key][:, 4096 * hh:4096 * (hh + 1)]),
                    reads=[bscr[key], bsl], writes=[bsl])
                pair.append((sl, bsl))
            wsl[nm] = pair

        def emit_zfg(half):
            def zf_mm(e, half=half):
                last = None
                for j in range(8):
                    last = e.matmul(
                        pb[4][0:64, :], wf1_sb[:, 64 * j:64 * j + 64],
                        xln[:, 1024 * j + 512 * half:1024 * j + 512 * half + 512],
                        start=(j == 0), stop=(j == 7))
                return last
            mk.op("pe", zf_mm, reads=[bxlnh[half], bconst], writes=[bpb[4]])
            mk.op("dve", lambda e, half=half: e.tensor_copy(
                zf1[:, 512 * half:512 * half + 512], pb[4][0:64, :]),
                reads=[bpb[4]], writes=[bzf1])

            def zg_mm(e, half=half):
                last = None
                for j in range(8):
                    last = e.matmul(
                        pb[4][0:64, :], wg1_sb[:, 64 * j:64 * j + 64],
                        xln[:, 1024 * j + 512 * half:1024 * j + 512 * half + 512],
                        start=(j == 0), stop=(j == 7))
                return last
            mk.op("pe", zg_mm, reads=[bxlnh[half], bconst], writes=[bpb[4]])
            mk.op("dve", lambda e, half=half: e.tensor_copy(
                zg1[:, 512 * half:512 * half + 512], pb[4][0:64, :]),
                reads=[bpb[4]], writes=[bzg1])
        emit_zfg(0)

        def emit_tro(tc0):
            for s in range(2):
                oepS, boepS = oeps[s]

                def tr_o(e, oepS=oepS):
                    last = None
                    for j in range(8):
                        last = e.transpose(pt[:, 64 * j:64 * j + 64],
                                           oepS[:, 128 * j:128 * j + 128],
                                           ident_bf[0:64, 0:64])
                    return last
                mk.op("pe", tr_o, reads=[boepS, bconst], writes=[bpt])
                toff = tc0 + 64 * s
                mk.op("act", lambda e, toff=toff: e.copy(
                    oefm.rearrange("p (j t) -> p j t", j=8)
                    [:, :, toff:toff + 64],
                    pt.rearrange("p (j t) -> p j t", j=16)[:, 0:8, :]),
                    reads=[bpt], writes=[boefm])

        pend_tro = []
        stg1 = getattr(build_program, "a1_stage", 99)
        for pr in range(MT // 128):
            tc = 128 * pr
            bxlnT = bxlnh[tc // TS]
            if pr == 2:
                if lnmid:
                    lnmid()
                emit_zfg(1)

            def zf2_mm(e, tc=tc):
                last = None
                for hh in range(2):
                    last = e.matmul(pb[2 + hh][:, :], zf1[:, tc:tc + 128],
                                    wf2_sb[:, 512 * hh:512 * hh + 512],
                                    start=True, stop=True)
                return last
            mk.op("pe", zf2_mm, reads=[bzf1, bconst],
                  writes=[bpb[2], bpb[3]])
            for hh in range(2):
                mk.op("act", lambda e, hh=hh: e.activation(
                    sg[:, 512 * hh:512 * hh + 512], pb[2 + hh][:, :],
                    AF.Sigmoid), reads=[bpb[2 + hh]], writes=[bsg])
            mk.op("act", lambda e: e.activation(gkb[:, :], sg[:, :], AF.Ln),
                  reads=[bsg], writes=[bgkb])
            mk.op("dve", lambda e: e.tensor_copy(gk0[:, :], gkb[0:64, :]),
                  reads=[bgkb], writes=[bgk0])
            mk.op("dve", lambda e: e.tensor_copy(gk1[:, :], gkb[64:128, :]),
                  reads=[bgkb], writes=[bgk1])

            def zg2_mm(e, tc=tc):
                last = None
                for hh in range(2):
                    last = e.matmul(pb[2 + hh][:, :], zg1[:, tc:tc + 128],
                                    wg2_sb[:, 512 * hh:512 * hh + 512],
                                    start=True, stop=True)
                return last
            mk.op("pe", zg2_mm, reads=[bzg1, bconst],
                  writes=[bpb[2], bpb[3]])
            for hh in range(2):
                mk.op("act", lambda e, hh=hh: e.activation(
                    sgg[:, 512 * hh:512 * hh + 512], pb[2 + hh][:, :],
                    AF.Sigmoid), reads=[bpb[2 + hh]], writes=[bsgg])

            def qproj(e, tc=tc, sla=wsl["q"][0][0], slb=wsl["q"][1][0]):
                last = None
                for hh in range(2):
                    for j in range(8):
                        sl = sla if j < 4 else slb
                        last = e.matmul(
                            pw0[:, 512 * hh:512 * hh + 512],
                            xln[:, 1024 * j + tc:1024 * j + tc + 128],
                            sl[:, 1024 * (j % 4) + 512 * hh:
                               1024 * (j % 4) + 512 * hh + 512],
                            start=(j == 0), stop=(j == 7))
                return last
            mk.op("pe", qproj, reads=[bxlnT, wsl["q"][0][1], wsl["q"][1][1]],
                  writes=[bpw0])
            mk.op("act", lambda e: e.activation(psq[:, :], pw0[:, :],
                                                AF.Silu),
                  reads=[bpw0], writes=[bpsq])
            def kproj(e, tc=tc, sla=wsl["k"][0][0], slb=wsl["k"][1][0]):
                last = None
                for hh in range(2):
                    for j in range(8):
                        sl = sla if j < 4 else slb
                        last = e.matmul(
                            pw0[:, 512 * hh:512 * hh + 512],
                            xln[:, 1024 * j + tc:1024 * j + tc + 128],
                            sl[:, 1024 * (j % 4) + 512 * hh:
                               1024 * (j % 4) + 512 * hh + 512],
                            start=(j == 0), stop=(j == 7))
                return last
            mk.op("pe", kproj, reads=[bxlnT, wsl["k"][0][1], wsl["k"][1][1]],
                  writes=[bpw0])
            mk.op("act", lambda e: e.activation(psil[:, :], pw0[:, :],
                                                AF.Silu),
                  reads=[bpw0], writes=[bpsil])
            if pend_tro:
                emit_tro(pend_tro.pop())
            if stg1 < 2:
                continue
            def cs_mm(e):
                last = None
                for hh in range(2):
                    last = e.matmul(pb[2 + hh][:, :], utri64x2_bf[:, :],
                                    gkb[:, 512 * hh:512 * hh + 512],
                                    start=True, stop=True)
                return last
            mk.op("pe", cs_mm, reads=[bgkb, bconst], writes=[bpb[2], bpb[3]])
            for hh in range(2):
                mk.op("act", lambda e, hh=hh: e.activation(
                    eb[:, 512 * hh:512 * hh + 512], pb[2 + hh][:, :], AF.Exp,
                    bias=cvals[:, 4:5]),
                    reads=[bpb[2 + hh]], writes=[beb])
                mk.op("act", lambda e, hh=hh: e.activation(
                    enb[:, 512 * hh:512 * hh + 512], pb[2 + hh][:, :], AF.Exp,
                    scale=-1.0), reads=[bpb[2 + hh]], writes=[benb])

            for s in range(2):
                gks = gk0 if s == 0 else gk1
                bgks = bgk0 if s == 0 else bgk1

                def blc(e, s=s, gks=gks):
                    last = None
                    for h in range(16):
                        last = e.matmul(
                            pb[4][0:64, 16 * s + h:16 * s + h + 1],
                            gks[:, 64 * h:64 * h + 64],
                            onescol_bf[0:64, :],
                            start=True, stop=True)
                    return last
                mk.op("pe", blc, reads=[bgks, bconst], writes=[bpb[4]])
            mk.op("act", lambda e: e.activation(scol[:, :], pb[4][0:64, 0:32],
                                                AF.Exp, bias=cvals[0:64, 4:5]),
                  reads=[bpb[4]], writes=[bscol])

            def vproj(e, tc=tc, sla=wsl["v"][0][0], slb=wsl["v"][1][0]):
                last = None
                for hh in range(2):
                    for j in range(8):
                        sl = sla if j < 4 else slb
                        last = e.matmul(
                            pw0[:, 512 * hh:512 * hh + 512],
                            xln[:, 1024 * j + tc:1024 * j + tc + 128],
                            sl[:, 1024 * (j % 4) + 512 * hh:
                               1024 * (j % 4) + 512 * hh + 512],
                            start=(j == 0), stop=(j == 7))
                return last
            mk.op("pe", vproj, reads=[bxlnT, wsl["v"][0][1], wsl["v"][1][1]],
                  writes=[bpw0])
            mk.op("dve", lambda e: e.tensor_copy(vbf0[:, :], pw0[0:64, :]),
                  reads=[bpw0], writes=[bvbf0])
            mk.op("dve", lambda e: e.tensor_copy(vbf1[:, :], pw0[64:128, :]),
                  reads=[bpw0], writes=[bvbf1])

            mk.op("dve", lambda e: e.scalar_tensor_tensor(
                qe0[:, :], psq[0:64, :], 0.125, eb[0:64, :],
                ALU.mult, ALU.mult),
                reads=[bpsq, beb], writes=[bqe0])
            mk.op("dve", lambda e: e.scalar_tensor_tensor(
                qe1[:, :], psq[64:128, :], 0.125, eb[64:128, :],
                ALU.mult, ALU.mult),
                reads=[bpsq, beb], writes=[bqe1])
            mk.op("dve", lambda e: e.tensor_tensor(
                ke0[:, :], psil[0:64, :], enb[0:64, :], ALU.mult),
                reads=[bpsil, benb], writes=[bke0])
            mk.op("dve", lambda e: e.tensor_tensor(
                ke1[:, :], psil[64:128, :], enb[64:128, :], ALU.mult),
                reads=[bpsil, benb], writes=[bke1])
            if stg1 < 4:
                continue
            for s in range(2):
                qes = qe0 if s == 0 else qe1
                bqes = bqe0 if s == 0 else bqe1
                kes = ke0 if s == 0 else ke1
                bkes = bke0 if s == 0 else bke1
                vbs = vbf0 if s == 0 else vbf1
                bvbs = bvbf0 if s == 0 else bvbf1
                og, bog = ogs[s]

                def tr_q(e, qes=qes):
                    last = None
                    for h in range(16):
                        last = e.transpose(pt[0:64, 64 * h:64 * h + 64],
                                           qes[:, 64 * h:64 * h + 64],
                                           ident_bf[0:64, 0:64])
                    return last
                mk.op("pe", tr_q, reads=[bqes, bconst], writes=[bpt])
                mk.op("dve", lambda e: e.tensor_copy(qefm[:, :], pt[0:64, :]),
                      reads=[bpt], writes=[bqefm])

                def tr_k(e, kes=kes):
                    last = None
                    for h in range(16):
                        last = e.transpose(pt[0:64, 64 * h:64 * h + 64],
                                           kes[:, 64 * h:64 * h + 64],
                                           ident_bf[0:64, 0:64])
                    return last
                mk.op("pe", tr_k, reads=[bkes, bconst], writes=[bpt])
                mk.op("dve", lambda e: e.tensor_copy(kefm[:, :], pt[0:64, :]),
                      reads=[bpt], writes=[bkefm])

                if stg1 < 5:
                    continue

                def a_mm(e):
                    last = None
                    for h in range(16):
                        last = e.matmul(
                            pw0[0:64, 64 * h:64 * h + 64],
                            kefm[:, 64 * h:64 * h + 64],
                            qefm[:, 64 * h:64 * h + 64],
                            start=True, stop=True)
                    return last
                mk.op("pe", a_mm, reads=[bqefm, bkefm], writes=[bpw0])
                mk.op("dve", lambda e: e.tensor_tensor(
                    am[:, :], pw0[0:64, :], mask64_f[:, :], ALU.mult),
                    reads=[bpw0, bconst], writes=[bam])

                if stg1 < 6:
                    continue

                def o_mm(e, vbs=vbs):
                    last = None
                    for h in range(16):
                        last = e.matmul(
                            pw0[0:64, 64 * h:64 * h + 64],
                            am[:, 64 * h:64 * h + 64],
                            vbs[:, 64 * h:64 * h + 64],
                            start=True, stop=False)
                        last = e.matmul(
                            pw0[0:64, 64 * h:64 * h + 64],
                            qefm[:, 64 * h:64 * h + 64],
                            S1bf[:, 64 * h:64 * h + 64],
                            start=False, stop=True)
                    return last
                mk.op("pe", o_mm, reads=[bam, bvbs, bqefm, bS1bf],
                      writes=[bpw0])

                mk.op("dve", lambda e, s=s: e.tensor_tensor(
                    og[:, :], pw0[0:64, :], sgg[64 * s:64 * s + 64, :],
                    ALU.mult), reads=[bpw0, bsgg], writes=[bog],
                    serialize=True)

                if stg1 < 7:
                    continue

                def s_mm(e, kes=kes, vbs=vbs):
                    last = None
                    for h in range(16):
                        last = e.matmul(
                            pw0[0:64, 64 * h:64 * h + 64],
                            kes[:, 64 * h:64 * h + 64],
                            vbs[:, 64 * h:64 * h + 64],
                            start=True, stop=True)
                    return last
                mk.op("pe", s_mm, reads=[bkes, bvbs, bog], writes=[bpw0])
                mk.op("dve", lambda e: e.tensor_add(
                    S1[:, :], S1[:, :], pw0[0:64, :]),
                    reads=[bpw0, bS1], writes=[bS1])
                for h in range(16):
                    mk.op("dve", lambda e, h=h, s=s: e.tensor_scalar(
                        S1[:, 64 * h:64 * h + 64], S1[:, 64 * h:64 * h + 64],
                        scol[:, 16 * s + h:16 * s + h + 1], None, ALU.mult),
                        reads=[bscol, bS1], writes=[bS1])
                mk.op("act", lambda e: e.copy(S1bf[:, :], S1[:, :]),
                      reads=[bS1], writes=[bS1bf])

                if stg1 < 8:
                    continue
                oepS, boepS = oeps[s]
                r0 = 16 * s
                mk.op("act", lambda e, og=og: e.activation(
                    osq[:, :], og[:, :], AF.Square),
                    reads=[bog], writes=[bosq])
                mk.op("dve", lambda e, r0=r0: e.tensor_reduce(
                    rsq[:, r0:r0 + 16],
                    osq.rearrange("p (h c) -> p h c", c=64),
                    axis=AX.X, op=ALU.add), reads=[bosq], writes=[brsq],
                    hazard=True)
                mk.op("act", lambda e, r0=r0: e.activation(
                    rsq[:, r0:r0 + 16], rsq[:, r0:r0 + 16],
                    AF.Ln, scale=1.0 / 64.0,
                    bias=cvals[0:64, 0:1]),
                    reads=[brsq], writes=[brsq])
                mk.op("act", lambda e, r0=r0: e.activation(
                    rsq[:, r0:r0 + 16], rsq[:, r0:r0 + 16],
                    AF.Exp, scale=-0.5,
                    bias=cvals[0:64, 4:5]),
                    reads=[brsq], writes=[brsq])
                for h in range(16):
                    mk.op("dve", lambda e, h=h, og=og, bog=bog, r0=r0,
                          oepS=oepS: e.scalar_tensor_tensor(
                              oepS[:, 64 * h:64 * h + 64],
                              og[:, 64 * h:64 * h + 64],
                              rsq[:, r0 + h:r0 + h + 1],
                              nw1[0:64, 64 * h:64 * h + 64],
                              ALU.mult, ALU.mult),
                          reads=[bog, brsq, bconst], writes=[boepS])

            if stg1 < 8:
                continue
            pend_tro.append(tc)

            tapname = getattr(build_program, "a1_tap", None)
            if pr == getattr(build_program, "a1_tap_pr", 0) and tapname is not None:
                tapmap = {"gk0": (gk0, bgk0, 64, 1024), "eb": (eb, beb, 128, 1024),
                          "enb": (enb, benb, 128, 1024),
                          "qe0": (qe0, bqe0, 64, 1024), "ke0": (ke0, bke0, 64, 1024),
                          "vbf0": (vbf0, bvbf0, 64, 1024),
                          "sgg": (sgg, bsgg, 128, 1024), "scol": (scol, bscol, 64, 32),
                          "S1": (S1, bS1, 64, 1024), "qefm": (qefm, bqefm, 64, 1024),
                          "kefm": (kefm, bkefm, 64, 1024), "oep": (oeps[0][0], oeps[0][1], 64, 1024),
                          "og": (og, bog, 64, 1024)}
                tt, bt_, tp, tw = tapmap[tapname]
                mk.op("gdma", lambda e, tt=tt, tp=tp, tw=tw: e.dma_start(
                    out=scr["a1dbg"][0:tp, 0:tw], in_=tt[0:tp, 0:tw]),
                    reads=[bt_], writes=[bscr["a1dbg"]])

        while pend_tro:
            emit_tro(pend_tro.pop())

        slo = []
        for hh in range(2):
            sl, bsl = next_slot()
            mk.op("sdma", lambda e, sl=sl, hh=hh: e.dma_start(
                out=sl[:, :],
                in_=scr["wo1"][:, 4096 * hh:4096 * (hh + 1)]),
                reads=[bscr["wo1"], bsl], writes=[bsl])
            slo.append((sl, bsl))
        for tsI in range(2):
            t0 = TS * tsI
            if tsI == 1 and lntail:
                lntail()
            for db in range(8):
                bp = bpb[db % 2]
                pa = pb[db % 2]

                def op_mm(e, db=db, t0=t0, pa=pa, slo=slo):
                    last = None
                    for j in range(8):
                        sl = slo[j // 4][0]
                        last = e.matmul(
                            pa[:, :],
                            sl[:, 1024 * (j % 4) + 128 * db:
                               1024 * (j % 4) + 128 * db + 128],
                            oefm[:, 1024 * j + t0:1024 * j + t0 + TS],
                            start=(j == 0), stop=(j == 7))
                    return last
                mk.op("pe", op_mm,
                      reads=[boefm, slo[0][1], slo[1][1]], writes=[bp])
                mk.op("dve", lambda e, db=db, t0=t0, pa=pa: e.tensor_add(
                    x_fm[:, 1024 * db + t0:1024 * db + t0 + TS],
                    x_fm[:, 1024 * db + t0:1024 * db + t0 + TS],
                    pa[:, :]),
                    reads=[bp, bxh[tsI]], writes=[bxh[tsI]])

    def emit_lnonly():
        for j in range(8):
            mk.op("dve", lambda e, j=j: e.tensor_copy(
                x_fm[:, 1024 * j:1024 * (j + 1)],
                xln[:, 1024 * j:1024 * (j + 1)]),
                reads=[bxlnh[0], bxlnh[1]], writes=[bxh[0], bxh[1]])

    PHASE_FN = {"ffn0a": emit_ffn0, "ffn0b": emit_ffn0,
                "ffn1a": emit_ffn1, "ffn1b": emit_ffn1,
                "attn0": emit_attn0, "attn1": emit_attn1,
                "lnonly": emit_lnonly, "noop": (lambda: None)}

    with nc.Block() as block:
        emit_identf()
        emit_consts()
        emit_prep()

        dump = getattr(build_program, "dump_scr", None)
        emit_load_x(0)
        emit_ln(0)
        nph = len(phases)
        for ti in range(nt):
            for ip, ph in enumerate(phases):
                ffnp = ph.startswith("ffn")
                mk.barrier()
                arena.reset()
                last_phase = ip == nph - 1
                lnmid = (lambda ffnp=ffnp: emit_ln(1, fset=ffnp))
                lntail = (None if last_phase
                          else (lambda ffnp=ffnp: emit_ln(0, fset=ffnp)))
                if ph in ("attn0", "attn1"):
                    emit_ln(1, fset=ffnp)
                    PHASE_FN[ph](ti, None, None)
                    if not last_phase:
                        emit_ln(0, fset=ffnp)
                else:
                    PHASE_FN[ph](lnmid, lntail)
                if ti == 0 and ip == 0 and "w0f" in scr:
                    _prep_ffn1()
                if last_phase:
                    emit_store_x(ti)
                    if ti + 1 < nt:
                        emit_load_x(ti + 1)
                        emit_ln(0, fset=ffnp)
        if dump:
            ddt = F32 if dump in ("dbgprep", "a1dbg") else BF16
            shp = list(scr[dump].shape)
            flat = [int(np.prod(shp[:-1])), shp[-1]]
            ddev = nc.dram_tensor("dump_dev", flat, ddt,
                                  kind="ExternalOutput").ap()
            srcap = scr[dump]
            if len(shp) == 4:
                srcap = srcap.rearrange("a b c d -> (a b c) d")
            mk.op("sdma", lambda e, srcap=srcap: e.dma_start(out=ddev, in_=srcap),
                  reads=[bscr[dump]], writes=[by])
        mk.barrier()

        mk.emit(block)

    return nc, ctx, mk.nops



_cache = {}


def _get_program(key, **kw):
    if key not in _cache:
        _cache[key] = build_program(**kw)
    return _cache[key]


def make_weight_map(inputs):
    w = {}
    for name in ["L0_attn_Wq", "L0_attn_Wk", "L0_attn_Wv", "L0_attn_Wg",
                 "L0_attn_Wgk1", "L0_attn_Wgk2", "L0_attn_Wo",
                 "L0_ffn_W1", "L0_ffn_W2",
                 "L1_attn_Wq", "L1_attn_Wk", "L1_attn_Wv", "L1_attn_Wo",
                 "L1_attn_Wf1", "L1_attn_Wf2", "L1_attn_Wg1", "L1_attn_Wg2"]:
        w[name] = np.ascontiguousarray(np.asarray(inputs[name], np.float32))
    w["L0_attn_bgk2"] = np.asarray(inputs["L0_attn_bgk2"],
                                   np.float32).reshape(1, 512)
    w["L0_attn_norm_w"] = np.asarray(inputs["L0_attn_norm_w"],
                                     np.float32).reshape(1, 128)
    w["L0_ffn_b1"] = np.asarray(inputs["L0_ffn_b1"], np.float32).reshape(1, 4096)
    w["L0_ffn_b2"] = np.asarray(inputs["L0_ffn_b2"], np.float32).reshape(1, 1024)
    w["L1_attn_norm_weight"] = np.asarray(
        inputs["L1_attn_norm_weight"], np.float32).reshape(1, 1024)
    for k in ["L1_ffn_k0", "L1_ffn_k1", "L1_ffn_k2"]:
        w[k] = np.ascontiguousarray(np.asarray(inputs[k])).view(np.uint8)
    for k in ["L1_ffn_s0", "L1_ffn_s1", "L1_ffn_s2"]:
        w[k] = np.asarray(inputs[k], np.float32).reshape(1, 1)
    return w


def kernel(**inputs):
    x = np.asarray(inputs["L0_attn_x"], np.float32)
    x_cores = [np.ascontiguousarray(x[c].T) for c in range(NCORES)]
    nc, _, _ = _get_program("fused", phases=PHASES_ALL, nt=T // MT)
    w = make_weight_map(inputs)
    maps = []
    for c in range(NCORES):
        m = dict(w)
        m["x_dev"] = x_cores[c]
        maps.append(m)
    res = run_bass_kernel_spmd(nc, maps, core_ids=list(range(NCORES)))
    out = np.empty((NCORES, T, D), np.float32)
    for c in range(NCORES):
        out[c] = np.asarray(res.results[c]["y_dev"]).T
    return out
